# version 24
# baseline (speedup 1.0000x reference)
import numpy as np
import ml_dtypes
import concourse.bass as bass
import concourse.mybir as mybir
from concourse.bass_utils import run_bass_kernel_spmd

F32 = mybir.dt.float32
BF16 = mybir.dt.bfloat16
AF = mybir.ActivationFunctionType
ALU = mybir.AluOpType
AX = mybir.AxisListType

D = 1024
S = 4096
NT = S // 128
DC = D // 128
DH = 4096
HC = DH // 128
DG = 2048
GC = DG // 128
NEG = -1.0e30
T_WIN = 2.4414063e-4
EPS_LN = 1.0e-5
INV_SQRT_D = 1.0 / 32.0

ALL_STAGES = ("attn_a", "ffn1", "attn_b", "ffn2")
ENGS = ("sync", "tensor", "vector", "scalar", "gpsimd")
ENG_SEM = {"vector": "sV", "scalar": "sS"}


class Sem:
    def __init__(self, nc, name):
        self.h = nc.alloc_semaphore(name=name)
        self.n = 0

    @property
    def tok(self):
        return (self, self.n)


class Stream:

    def __init__(self, nc, name, nslots):
        self.sems = [Sem(nc, f"{name}{i}") for i in range(nslots)]

    def dma(self, ph, eng, slot, out, in_):
        s = self.sems[slot % len(self.sems)]
        ph.wait(eng, s.tok)
        ph.dma(eng, s, out, in_)
        return s.tok

    def wait_all(self, ph, eng):
        for s in self.sems:
            ph.wait(eng, s.tok)


class Ph:

    def __init__(self, g):
        self.nc = g["nc"]
        self.g = g
        self.q = {k: [] for k in ENGS}

    def wait(self, eng, tok):
        sem, v = tok
        assert v >= 0
        self.q[eng].append(lambda e, s=sem.h, v=v: e.wait_ge(s, v))

    def op(self, eng, fn, sem=None, k=1):
        if eng in ENG_SEM:
            au = self.g[ENG_SEM[eng]]
            self.wait(eng, au.tok)
            if sem is None:
                sem = au
        if sem is None:
            self.q[eng].append(lambda e, fn=fn: fn(e))
            return None
        def run(e, fn=fn, sem=sem, k=k):
            return fn(e).then_inc(sem.h, k)
        self.q[eng].append(run)
        sem.n += k
        return sem.tok

    def dma(self, eng, sem, out, in_):
        def run(e, o=out, i=in_, s=sem):
            return e.dma_start(o, i).then_inc(s.h, 16)
        self.q[eng].append(run)
        sem.n += 16
        return sem.tok

    def run(self):
        with self.nc.Block() as blk:
            for name, dec in (("sync", blk.sync), ("tensor", blk.tensor),
                              ("vector", blk.vector), ("scalar", blk.scalar),
                              ("gpsimd", blk.gpsimd)):
                if self.q[name]:
                    fns = self.q[name]
                    dec(lambda e, fns=fns: [fn(e) for fn in fns])



def build_program(stages=ALL_STAGES, taps=(), dumps=()):
    nc = bass.Bass("TRN2", target_bir_lowering=False)

    def din(name, shape, dtype=F32):
        return nc.dram_tensor(name, list(shape), dtype, kind="ExternalInput").ap()

    def dscr(name, shape, dtype):
        return nc.dram_tensor(name, list(shape), dtype, kind="Internal").ap()

    g = dict(nc=nc)
    g["x0"] = din("x0", [S, D])
    for n, sh in [("WqT", [D, D]), ("WkT", [D, D]), ("WvT", [D, D]), ("WoT", [D, D]),
                  ("bqT", [128, DC]), ("bkT", [128, DC]), ("bv", [1, D]),
                  ("bo", [1, D]), ("bfq", [1, D]), ("WtqX", [1, D]), ("WtqY", [1, D]),
                  ("btq", [1, 2]), ("WfqX", [1, D]), ("WfqY", [1, D]),
                  ("W1", [D, 3 * DG]), ("b1T", [128, 3 * GC]), ("W2", [DG, D]),
                  ("b2", [1, D]), ("inithT", [128, GC]),
                  ("w1T", [D, DH]), ("w2T", [D, DH]), ("h1T", [DH, DH]),
                  ("h2T", [DH, DH]), ("w3T", [DH, D]),
                  ("s1T", [128, HC]), ("s2T", [128, HC]), ("s3T", [128, HC]),
                  ("s4T", [128, HC]), ("sw3r", [1, D]),
                  ("ident", [128, 128]), ("tri", [128, 128])]:
        g[n] = din(n, sh)
    g["identb"] = din("identb", [128, 128], BF16)
    g["y"] = nc.dram_tensor("y", [S, D], F32, kind="ExternalOutput").ap()

    g["x1"] = dscr("x1", [S, D], F32)
    g["x2"] = dscr("x2", [S, D], F32)
    g["x3"] = dscr("x3", [S, D], F32)
    g["qTd"] = dscr("qTd", [DC, 128, S], BF16)
    g["kTd"] = dscr("kTd", [DC, 128, S], BF16)
    g["vd"] = dscr("vd", [NT, 128, D], BF16)
    g["hd"] = dscr("hd", [GC, 128, S], BF16)
    w12 = dscr("w12tb", [HC, 128, 2, DC * 128], BF16)
    g["w12tb"] = w12
    g["w1tb"] = w12[:, :, 0, :]
    g["w2tb"] = w12[:, :, 1, :]
    g["h1tb"] = dscr("h1tb", [HC, 128, HC * 128], BF16)
    g["h2tb"] = dscr("h2tb", [HC, 128, HC * 128], BF16)
    g["w3tb"] = dscr("w3tb", [HC, 128, D], BF16)
    g["W1b"] = dscr("W1b", [3 * GC, 128, DC * 128], BF16)

    taps_d = {}
    for t in taps:
        taps_d[t] = nc.dram_tensor("tap_" + t, [S, D], F32,
                                   kind="ExternalOutput").ap()

    g["identS"] = nc.alloc_sbuf_tensor("identS", [128, 128], F32).ap()
    g["identbS"] = nc.alloc_sbuf_tensor("identbS", [128, 128], BF16).ap()
    g["triS"] = nc.alloc_sbuf_tensor("triS", [128, 128], F32).ap()
    g["ones"] = nc.alloc_sbuf_tensor("onesS", [1, 512], F32).ap()
    g["oneb1"] = nc.alloc_sbuf_tensor("oneb1S", [128, 1], F32).ap()
    g["halfb1"] = nc.alloc_sbuf_tensor("halfb1S", [128, 1], F32).ap()
    g["onesb"] = nc.alloc_sbuf_tensor("onesbS", [1, 512], BF16).ap()

    for s in ("sT", "sT2", "sV", "sS", "sG", "sGc", "sWs", "sF4", "sPV"):
        g[s] = Sem(nc, s)
    g["sP0"] = [Sem(nc, f"sP0_{j}") for j in range(6)]
    g["stW"] = Stream(nc, "stW", 2)
    g["stW2"] = Stream(nc, "stW2", 2)
    g["stW3"] = Stream(nc, "stW3", 2)
    g["stH"] = Stream(nc, "stH", 2)
    g["stVd"] = Stream(nc, "stVd", 2)
    g["stVg"] = Stream(nc, "stVg", 4)
    g["stQ"] = Stream(nc, "stQ", 2)
    g["stX"] = Stream(nc, "stX", 2)
    g["stO"] = Stream(nc, "stO", 2)
    g["stG"] = Stream(nc, "stG", 2)

    phase_consts(g)
    phase_p0(g)

    cur = g["x0"]
    chain = [("attn_a", g["x1"]), ("ffn1", g["x2"]),
             ("attn_b", g["x3"]), ("ffn2", g["y"])]
    for name, out in chain:
        if name in stages:
            if name == "attn_a":
                phase_attn_a(g, cur, out)
            elif name in ("ffn1", "ffn2"):
                phase_ffn(g, cur, out)
            else:
                phase_attn_b(g, cur, out)
        else:
            phase_copy(g, cur, out)
        if name in taps_d:
            phase_copy(g, out, taps_d[name])
        cur = out

    for name in dumps:
        src = g[name]
        dst = nc.dram_tensor("dump_" + name, list(src.shape), src.dtype,
                             kind="ExternalOutput").ap()
        phase_copy(g, src, dst)

    for s in ("sT", "sT2", "sV", "sS", "sG"):
        assert g[s].n < 2 ** 30, s
    return nc


def phase_consts(g):
    ph = Ph(g)
    sWs = g["sWs"]
    ph.dma("sync", sWs, g["identS"], g["ident"])
    ph.dma("sync", sWs, g["identbS"], g["identb"])
    t = ph.dma("sync", sWs, g["triS"], g["tri"])
    ph.op("vector", lambda e: e.memset(g["ones"], 1.0))
    ph.op("vector", lambda e: e.memset(g["oneb1"], 1.0))
    ph.op("vector", lambda e: e.memset(g["halfb1"], 0.5))
    ph.op("vector", lambda e: e.memset(g["onesb"], 1.0))
    ph.wait("vector", t)
    ph.run()


def phase_copy(g, src, dst):
    ph = Ph(g)
    t = ph.dma("gpsimd", g["sG"], dst, src)
    ph.wait("gpsimd", t)
    ph.run()



def phase_p0(g):
    casts = []
    sems = g["sP0"]

    def add(j, dst, srcv):
        casts.append((j, lambda ph, d=dst, s=srcv, sem=sems[j]:
                      ph.dma("gpsimd", sem, d, s)))

    for c in range(DC):
        add(0, g["w1tb"].rearrange("mc p (c m) -> p mc c m", m=128)[:, :, c, :],
            g["w1T"][c * 128:(c + 1) * 128, :].rearrange("p (mc m) -> p mc m",
                                                         m=128))
        add(0, g["w2tb"].rearrange("mc p (c m) -> p mc c m", m=128)[:, :, c, :],
            g["w2T"][c * 128:(c + 1) * 128, :].rearrange("p (mc m) -> p mc m",
                                                         m=128))
    for j, (wT, dstt) in ((1, (g["h1T"], g["h1tb"])), (2, (g["h2T"], g["h2tb"]))):
        for c in range(HC):
            add(j, dstt.rearrange("mc p (c m) -> p mc c m", m=128)[:, :, c, :],
                wT[c * 128:(c + 1) * 128, :].rearrange("p (mc m) -> p mc m",
                                                       m=128))
    for c in range(HC):
        add(3, g["w3tb"][c], g["w3T"][c * 128:(c + 1) * 128, :])
    for c in range(DC):
        add(4, g["W1b"].rearrange("cc p (c m) -> p cc c m", m=128)[:, :, c, :],
            g["W1"][c * 128:(c + 1) * 128, :].rearrange("p (cc m) -> p cc m",
                                                        m=128))
    g["p0_casts"] = casts
    g["p0_total"] = {j: sum(1 for jj, _ in casts if jj == j) * 16
                     for j in range(6)}


def p0_drain(g, ph, n):
    for _ in range(n):
        if not g["p0_casts"]:
            return
        j, fn = g["p0_casts"].pop(0)
        fn(ph)


def p0_wait(g, ph, eng, jobs):
    for j in jobs:
        assert not any(jj == j for jj, _ in g["p0_casts"]), f"job {j} pending"
        ph.wait(eng, (g["sP0"][j], g["p0_total"][j]))



def emit_ln_tiles(g, ph, xin, tiles, xt, xlnf, xlnT, bufs,
                  extra_v=None, keep_stats=None, dest_base=0, dest_cols=None,
                  pre_evac_wait=None, pre_tr_wait=None, state=None,
                  tr_sink=None):
    sT = g["sT"]
    stX = g["stX"]
    bn, mv, v1, rst = bufs["bn"], bufs["mv"], bufs["v1"], bufs["rst"]
    pstr = bufs["pstr"]
    if state is None:
        state = {"apply": [], "tr": [], "evac": []}
    apply_toks, tr_toks, evac_toks = state["apply"], state["tr"], state["evac"]

    out_evacs = []
    for j, gt in enumerate(tiles):
        k = len(apply_toks)
        if k >= 4:
            ph.wait("sync", apply_toks[k - 4])
        ld = stX.dma(ph, "sync", k, xt[:, k % 4, :],
                     xin[gt * 128:(gt + 1) * 128, :])

        ph.wait("vector", ld)
        ph.op("vector", lambda e, k=k: e.bn_stats(bn[:, 0:6],
                                                  xt[:, k % 4, 0:512]))
        ph.op("vector", lambda e, k=k: e.bn_stats(bn[:, 6:12],
                                                  xt[:, k % 4, 512:1024]))
        ph.op("vector", lambda e: e.bn_aggr(mv, bn))
        st_tok = ph.op("vector", lambda e: e.tensor_scalar(
            v1, mv[:, 1:2], EPS_LN, None, ALU.add))
        ph.wait("scalar", st_tok)
        ph.op("scalar", lambda e: e.activation(rst, v1, AF.Ln))
        rst_tok = ph.op("scalar", lambda e: e.activation(rst, rst, AF.Exp,
                                                         scale=-0.5))
        if keep_stats is not None:
            ph.op("vector", lambda e, gt=gt: e.tensor_copy(
                keep_stats[0][:, gt:gt + 1], mv[:, 0:1]))
        ph.wait("vector", rst_tok)
        if k >= 4:
            assert len(tr_toks) >= k - 3, "tr_sink must be drained promptly"
            ph.wait("vector", tr_toks[k - 4])
        apply_toks.append(ph.op("vector", lambda e, k=k: e.tensor_scalar(
            xlnf[:, k % 4, :], xt[:, k % 4, :],
            mv[:, 0:1], rst, ALU.subtract, ALU.mult)))
        if keep_stats is not None:
            ph.op("vector", lambda e, gt=gt: e.tensor_copy(
                keep_stats[1][:, gt:gt + 1], rst))
        if extra_v is not None:
            extra_v(j, gt)

        def emit_tr(k=k, j=j):
            ph.wait("tensor", apply_toks[k])
            if k >= 1:
                ph.wait("tensor", evac_toks[k - 1])
            if pre_tr_wait is not None:
                pre_tr_wait(ph, j)
            tk = None
            for c in range(DC):
                tk = ph.op("tensor", lambda e, k=k, c=c: e.transpose(
                    pstr[:, c // 4, (c % 4) * 128:(c % 4 + 1) * 128],
                    xlnf[:, k % 4, c * 128:(c + 1) * 128], g["identS"]),
                    sem=sT if c == DC - 1 else None)
            tr_toks.append(tk)
            ph.wait("scalar", tk)
            if pre_evac_wait is not None:
                pre_evac_wait(ph, j)
            ek = None
            col0 = (dest_cols[j] if dest_cols is not None
                    else dest_base + j * 128)
            for h in range(2):
                ek = ph.op("scalar", lambda e, h=h, col=col0: e.activation(
                    xlnT[:, 4 * h:4 * h + 4, col:col + 128],
                    pstr[:, h, :].rearrange("p (c m) -> p c m", m=128),
                    AF.Copy))
            evac_toks.append(ek)
            out_evacs.append(ek)
        if tr_sink is not None:
            tr_sink.append(emit_tr)
        else:
            emit_tr()
    return state, out_evacs



def phase_ffn(g, xin, xout):
    nc = g["nc"]
    sT2, sF4 = g["sT2"], g["sF4"]
    stW, stO = g["stW"], g["stO"]

    from contextlib import ExitStack
    uid = g["uid"] = g.get("uid", 0) + 1
    with ExitStack() as stack:
        def sb(name, shape, dtype):
            return stack.enter_context(
                nc.sbuf_tensor(f"{name}_{uid}", shape, dtype)).ap()

        def psb(name, shape, dtype=F32):
            return stack.enter_context(
                nc.psum_tensor(f"{name}_{uid}", shape, dtype)).ap()

        xt = sb("f_xt", [128, 4, D], F32)
        xlnf = sb("f_xlnf", [128, 4, D], F32)
        xlnT = sb("f_xlnT", [128, DC, 2, 512], BF16)
        bufs = dict(bn=sb("f_bn", [128, 12], F32), mv=sb("f_mv", [128, 2], F32),
                    v1=sb("f_v1", [128, 1], F32), rst=sb("f_rst", [128, 1], F32),
                    pstr=psb("f_pstr", [128, 2, 512]))
        hT = sb("f_hT", [128, HC, 512], BF16)
        gT = sb("f_gT", [128, HC, 512], BF16)
        w12 = sb("f_w12", [128, 2, 2, 1024], BF16)
        hw = sb("f_hw", [128, 2, 2, HC * 128], BF16)
        silu = sb("f_silu", [128, 2, 512], BF16)
        w2c = sb("f_w2c", [128, 2, 512], BF16)
        w3sb = sb("f_w3", [128, 2, 2, D], BF16)
        outsb = sb("f_out", [128, 2, D], F32)
        f4x = sb("f_f4x", [128, 4, D], F32)
        pw1 = psb("f_pw1", [128, 2, 512])
        pw2 = psb("f_pw2", [128, 2, 512])
        pg = psb("f_pg", [128, 2, 512])
        ssb = sb("f_ssb", [128, 4, HC], F32)
        sw3row = sb("f_sw3row", [1, D], F32)
        sw3rep = sb("f_sw3rep", [128, D], F32)
        sevac = sb("f_sevac", [128, 2, 512], F32)

        ph = Ph(g)
        p0_drain(g, ph, 10 ** 9)
        p0_wait(g, ph, "sync", [0, 1, 2, 3])
        for idx, nm in enumerate(("s1T", "s2T", "s3T", "s4T")):
            t0 = ph.dma("sync", g["sWs"], ssb[:, idx, :], g[nm])
            ph.wait("sync", t0)
        s_tok = t0
        t0 = ph.dma("sync", g["sWs"], sw3row, g["sw3r"])
        ph.wait("sync", t0)
        ph.wait("tensor", t0)
        rep = None
        for hf in range(2):
            mmt = ph.op("tensor", lambda e, hf=hf: e.matmul(
                pw1[:, hf, :], g["ones"][0:1, 0:128],
                sw3row[0:1, hf * 512:(hf + 1) * 512], start=True, stop=True),
                sem=g["sT"])
            ph.wait("vector", mmt)
            rep = ph.op("vector", lambda e, hf=hf: e.tensor_copy(
                sw3rep[:, hf * 512:(hf + 1) * 512], pw1[:, hf, :]))
        sw3_tok = rep
        ln_state = None
        ln_evacs = {}
        f1_mm = {}
        f4_mm = {}
        f4_add = {}
        sto = {}

        def emit_ln(ch, sink=None):
            def pev(ph_, j, ch=ch):
                if j == 0 and ch >= 2:
                    ph_.wait("scalar", f1_mm[ch - 2])
            def ptw(ph_, j, ch=ch):
                if j == 0 and ch >= 1 and (ch - 1) in f4_add:
                    ph_.wait("tensor", f4_add[ch - 1])
            nonlocal ln_state
            tiles = list(range(ch * 4, ch * 4 + 4))
            dest = [t * 128 for t in range(4)]
            ln_state, ev = emit_ln_tiles(
                g, ph, xin, tiles, xt, xlnf,
                xlnT.rearrange("p c s w -> p c (s w)"), bufs,
                dest_cols=[(ch % 2) * 512 + dd for dd in dest],
                pre_evac_wait=pev, pre_tr_wait=ptw, state=ln_state,
                tr_sink=sink)
            ln_evacs[ch] = ev

        emit_ln(0)
        prev_g2_mm = None
        prev_sevac_s = None
        for ch in range(8):
            sl = ch % 2
            ev_tok = {}
            mul_tok = {}
            mmt_hist = {}
            sil_hist = {}
            sink = []
            if ch < 7:
                emit_ln(ch + 1, sink=sink)
            for mc in range(HC):
                if mc >= 12 and (mc - 12) % 4 == 0 and sink:
                    sink.pop(0)()
                if mc >= 2:
                    ph.wait("sync", mmt_hist[mc - 2])
                wtok = stW.dma(ph, "sync", mc, w12[:, mc % 2],
                               g["w12tb"][mc])
                ph.wait("tensor", wtok)
                if mc == 0:
                    ph.wait("tensor", ln_evacs[ch][-1])
                    if ch >= 1:
                        ph.wait("tensor", f4_add[ch - 1])
                    elif ch == 0:
                        ph.wait("tensor", sw3_tok)
                if mc >= 2:
                    ph.wait("tensor", ev_tok[mc - 2])
                for c in range(DC):
                    ph.op("tensor", lambda e, mc=mc, c=c, sl=sl: e.matmul(
                        pw1[:, mc % 2, :],
                        w12[:, mc % 2, 0, c * 128:(c + 1) * 128],
                        xlnT[:, c, sl, :], start=(c == 0), stop=(c == DC - 1)))
                for c in range(DC):
                    mmt_hist[mc] = ph.op("tensor", lambda e, mc=mc, c=c, sl=sl:
                                         e.matmul(
                                             pw2[:, mc % 2, :],
                                             w12[:, mc % 2, 1,
                                                 c * 128:(c + 1) * 128],
                                             xlnT[:, c, sl, :],
                                             start=(c == 0),
                                             stop=(c == DC - 1)),
                                         sem=sT2 if c == DC - 1 else None)
                ph.wait("vector", mmt_hist[mc])
                if mc >= 2:
                    ph.wait("vector", mul_tok[mc - 2])
                if mc == 0:
                    if ch >= 1:
                        ph.wait("vector", f4_mm[ch - 1])
                    if ch == 0:
                        ph.wait("vector", s_tok)
                if mc >= 2:
                    ph.wait("vector", sil_hist[mc - 2])
                elif prev_sevac_s is not None:
                    ph.wait("vector", prev_sevac_s)
                sv1 = ph.op("vector", lambda e, mc=mc: e.tensor_scalar(
                    sevac[:, mc % 2, :], pw1[:, mc % 2, :],
                    ssb[:, 0, mc:mc + 1], None, ALU.mult))
                ev_tok[mc] = ph.op("vector", lambda e, mc=mc: e.tensor_scalar(
                    w2c[:, mc % 2, :], pw2[:, mc % 2, :],
                    ssb[:, 1, mc:mc + 1], None, ALU.mult))
                ph.wait("scalar", ev_tok[mc])
                sil = sil_hist[mc] = ph.op("scalar", lambda e, mc=mc: e.activation(
                    silu[:, mc % 2, :], sevac[:, mc % 2, :], AF.Silu))
                ph.wait("vector", sil)
                mul_tok[mc] = ph.op("vector", lambda e, mc=mc: e.tensor_mul(
                    hT[:, mc, :], silu[:, mc % 2, :], w2c[:, mc % 2, :]))
            while sink:
                sink.pop(0)()
            f1_mm[ch] = mmt_hist[HC - 1]

            def gelu_layer(srcb, dst, wtb, ready_tok, pg_free_tok,
                           hw_free_tok, s_idx, sevac_free):
                mm_tok = {}
                gel_tok = {}
                for mc in range(HC):
                    pair = mc // 2
                    if mc % 2 == 0:
                        if pair >= 2:
                            ph.wait("sync", mm_tok[mc - 3])
                        elif hw_free_tok is not None:
                            ph.wait("sync", hw_free_tok)
                        wtok = stW.dma(ph, "sync", pair, hw[:, pair % 2],
                                       wtb[mc:mc + 2].rearrange(
                                           "k p d -> p k d"))
                        ph.wait("tensor", wtok)
                    if mc == 0:
                        ph.wait("tensor", ready_tok)
                        if pg_free_tok is not None:
                            ph.wait("tensor", pg_free_tok)
                    if mc >= 2:
                        ph.wait("tensor", gel_tok[mc - 2])
                    for c in range(HC):
                        mm_tok[mc] = ph.op("tensor", lambda e, mc=mc, c=c:
                                           e.matmul(
                                               pg[:, mc % 2, :],
                                               hw[:, (mc // 2) % 2, mc % 2,
                                                  c * 128:(c + 1) * 128],
                                               srcb[:, c, :], start=(c == 0),
                                               stop=(c == HC - 1)),
                                           sem=sT2 if c == HC - 1 else None)
                    ph.wait("vector", mm_tok[mc])
                    if mc >= 2:
                        ph.wait("vector", gel_tok[mc - 2])
                    elif sevac_free is not None:
                        ph.wait("vector", sevac_free)
                    sv = ph.op("vector", lambda e, mc=mc: e.tensor_scalar(
                        sevac[:, mc % 2, :], pg[:, mc % 2, :],
                        ssb[:, s_idx, mc:mc + 1], None, ALU.mult))
                    ph.wait("scalar", sv)
                    gel_tok[mc] = ph.op("scalar", lambda e, mc=mc: e.activation(
                        dst[:, mc, :], sevac[:, mc % 2, :], AF.Gelu))
                return gel_tok[HC - 1], mm_tok[HC - 1]

            pg_free = f4_add.get(ch - 1)
            g1_tok, g1_mm = gelu_layer(hT, gT, g["h1tb"], mul_tok[HC - 1],
                                       pg_free, prev_g2_mm, 2,
                                       sil_hist[HC - 1])
            g2_tok, g2_mm = gelu_layer(gT, hT, g["h2tb"], g1_tok, None,
                                       g1_mm, 3, g1_tok)
            prev_sevac_s = g2_tok
            prev_g2_mm = g2_mm

            if ch >= 1:
                ph.wait("scalar", f4_add[ch - 1])
            f4toks = []
            for t in range(4):
                gt = ch * 4 + t
                f4toks.append(ph.dma("scalar", sF4, f4x[:, t, :],
                                     xin[gt * 128:(gt + 1) * 128, :]))
            mm_tok = {}
            for kc in range(HC):
                pair = kc // 2
                if kc % 2 == 0:
                    if pair >= 2:
                        ph.wait("sync", mm_tok[kc - 3])
                    wtok = stW.dma(ph, "sync", pair, w3sb[:, pair % 2],
                                   g["w3tb"][kc:kc + 2].rearrange(
                                       "k p d -> p k d"))
                    ph.wait("tensor", wtok)
                if kc == 0:
                    ph.wait("tensor", g2_tok)
                    if ch < 7:
                        ph.wait("tensor", ln_evacs[ch + 1][-1])
                    if ch >= 1:
                        ph.wait("tensor", f4_add[ch - 1])
                for t in range(4):
                    for hf in range(2):
                        last = (t == 3 and hf == 1)
                        mm_tok[kc] = ph.op("tensor",
                                           lambda e, kc=kc, t=t, hf=hf: e.matmul(
                                               _po(pw1, pw2, pg, bufs["pstr"], t)[
                                                   :, hf * 512:(hf + 1) * 512],
                                               hT[:, kc, t * 128:(t + 1) * 128],
                                               w3sb[:, (kc // 2) % 2, kc % 2,
                                                    hf * 512:(hf + 1) * 512],
                                               start=(kc == 0),
                                               stop=(kc == HC - 1)),
                                           sem=sT2 if last else None)
            f4_mm[ch] = mm_tok[HC - 1]
            for t in range(4):
                gt = ch * 4 + t
                ph.wait("vector", mm_tok[HC - 1])
                ph.wait("vector", f4toks[3])
                if t >= 2:
                    ph.wait("vector", sto[(ch, t - 2)])
                elif ch >= 1:
                    ph.wait("vector", sto[(ch - 1, t + 2)])
                if t == 0 and ch == 0:
                    ph.wait("vector", sw3_tok)
                ph.op("vector", lambda e, t=t: e.tensor_mul(
                    outsb[:, t % 2, :], _po(pw1, pw2, pg, bufs["pstr"], t),
                    sw3rep))
                add_tok = ph.op("vector", lambda e, t=t: e.tensor_add(
                    outsb[:, t % 2, :], outsb[:, t % 2, :], f4x[:, t, :]))
                ph.wait("scalar", add_tok)
                sto[(ch, t)] = stO.dma(ph, "scalar", t,
                                       xout[gt * 128:(gt + 1) * 128, :],
                                       outsb[:, t % 2, :])
                f4_add[ch] = add_tok
        stO.wait_all(ph, "scalar")
        ph.run()


def _po(pw1, pw2, pg, pstr, t):
    return [pw1, pw2, pg, pstr][t].rearrange("p a b -> p (a b)")


def phase_attn_a(g, xin, xout):
    nc = g["nc"]
    from contextlib import ExitStack
    with ExitStack() as outer:
        def osb(name, shape, dtype):
            return outer.enter_context(nc.sbuf_tensor(name, shape, dtype)).ap()

        QF = osb("a_QF", [128, S], F32)
        KBf = osb("a_KB", [128, S], F32)
        mcol = osb("a_mcol", [128, NT], F32)
        rcol = osb("a_rcol", [128, NT], F32)
        qptok = osb("a_qptok", [128, 2 * NT], F32)
        thr_rep = osb("a_thr", [128, 1], F32)
        WfqXr = osb("a_wfqxr", [128, D], F32)
        WfqYr = osb("a_wfqyr", [128, D], F32)
        brep = osb("a_brep", [128, D], F32)

        _attn_a_ph1(g, xin, QF, KBf, mcol, rcol, qptok, thr_rep,
                    WfqXr, WfqYr, brep)
        _attn_a_ph3(g, xin, xout, QF, KBf, mcol, rcol, qptok, thr_rep,
                    WfqXr, WfqYr, brep)


def _attn_a_ph1(g, xin, QF, KBf, mcol, rcol, qptok, thr_rep,
                WfqXr, WfqYr, brep):
    nc = g["nc"]
    sT, sT2, sG, sWs = g["sT"], g["sT2"], g["sG"], g["sWs"]
    stG, stH, stVd = g["stG"], g["stH"], g["stVd"]
    from contextlib import ExitStack
    with ExitStack() as stack:
        def sb(name, shape, dtype):
            return stack.enter_context(nc.sbuf_tensor(name, shape, dtype)).ap()

        def psb(name, shape, dtype=F32):
            return stack.enter_context(nc.psum_tensor(name, shape, dtype)).ap()

        xt = sb("a_xt", [128, 4, D], F32)
        xlnf = sb("a_xlnf", [128, 4, D], F32)
        xlnTr = sb("a_xlnTr", [128, DC, 1024], BF16)
        bufs = dict(bn=sb("a_bn", [128, 12], F32), mv=sb("a_mv", [128, 2], F32),
                    v1=sb("a_v1", [128, 1], F32), rst=sb("a_rst", [128, 1], F32),
                    pstr=psb("a_pstr", [128, 2, 512]))
        WqTb = sb("a_WqTb", [128, DC, D], BF16)
        WkTb = sb("a_WkTb", [128, DC, D], BF16)
        WvTb = sb("a_WvTb", [128, DC, D], BF16)
        bqsb = sb("a_bq", [128, DC], F32)
        bksb = sb("a_bk", [128, DC], F32)
        bvrow = sb("a_bvrow", [1, D], F32)
        bvb = sb("a_bvb", [1, D], BF16)
        WtqXr = sb("a_wtqxr", [128, D], F32)
        WtqYr = sb("a_wtqyr", [128, D], F32)
        btqr = sb("a_btqr", [128, 2], F32)
        tmp1k = sb("a_tmp1k", [128, D], F32)
        fall = sb("a_fall", [128, NT, 16], F32)
        fstage = sb("a_fstage", [16, S], F32)
        wrow = fstage[0:1, 0:D]
        red = sb("a_red", [128, 16], F32)
        sc = sb("a_sc", [1, 32], F32)
        kstage = sb("a_kst", [128, 1, DC, 512], BF16)
        qstage = sb("a_qst", [128, 1, DC, 512], BF16)
        vstage = sb("a_vst", [128, 1, 4, D], BF16)
        pfa = psb("a_pfa", [128, 512])
        psS = psb("a_psS", [128, 512])
        pp = psb("a_pp", [128, 2, 512])

        ph = Ph(g)
        for W, Wb in ((g["WqT"], WqTb), (g["WkT"], WkTb), (g["WvT"], WvTb)):
            ph.dma("gpsimd", sG, Wb, W.rearrange("(c p) o -> p c o", p=128))
        wb_tok = sG.tok
        t0 = ph.dma("sync", sWs, bqsb, g["bqT"]); ph.wait("sync", t0)
        t0 = ph.dma("sync", sWs, bksb, g["bkT"]); ph.wait("sync", t0)
        bv_tok = ph.dma("sync", sWs, bvrow, g["bv"]); ph.wait("sync", bv_tok)
        ph.wait("vector", bv_tok)
        bvb_tok = ph.op("vector", lambda e: e.tensor_copy(bvb, bvrow))
        last_rep = None
        for name in ("x", "y"):
            row = g["WtqX"] if name == "x" else g["WtqY"]
            t0 = ph.dma("sync", sWs, wrow, row)
            ph.wait("sync", t0)
            dstr = WtqXr if name == "x" else WtqYr
            mmt = None
            for hf in range(2):
                ph.wait("tensor", t0)
                if last_rep is not None:
                    ph.wait("tensor", last_rep)
                mmt = ph.op("tensor", lambda e, hf=hf: e.matmul(
                    pp[:, hf, :], g["ones"][0:1, 0:128],
                    wrow[0:1, hf * 512:(hf + 1) * 512], start=True, stop=True),
                    sem=sT)
                ph.wait("vector", mmt)
                last_rep = ph.op(
                    "vector", lambda e, hf=hf, dstr=dstr: e.tensor_copy(
                        dstr[:, hf * 512:(hf + 1) * 512], pp[:, hf, :]))
            ph.wait("sync", mmt)
        t0 = ph.dma("sync", sWs, wrow[0:1, 0:2], g["btq"])
        ph.wait("tensor", t0)
        ph.wait("tensor", last_rep)
        mmt = ph.op("tensor", lambda e: e.matmul(
            pp[:, 0, 0:2], g["ones"][0:1, 0:128], wrow[0:1, 0:2],
            start=True, stop=True), sem=sT)
        ph.wait("vector", mmt)
        btq_tok = ph.op("vector", lambda e: e.tensor_copy(btqr, pp[:, 0, 0:2]))
        ph.op("vector", lambda e: e.memset(fall[:, :, 9:10], 1.0))
        last_rep = btq_tok
        for name in ("fx", "fy", "bo", "bfq"):
            row = {"fx": g["WfqX"], "fy": g["WfqY"], "bo": g["bo"],
                   "bfq": g["bfq"]}[name]
            t0 = ph.dma("sync", sWs, wrow, row)
            ph.wait("sync", t0)
            mmt = None
            for hf in range(2):
                ph.wait("tensor", t0)
                ph.wait("tensor", last_rep)
                mmt = ph.op("tensor", lambda e, hf=hf: e.matmul(
                    pp[:, hf, :], g["ones"][0:1, 0:128],
                    wrow[0:1, hf * 512:(hf + 1) * 512], start=True,
                    stop=True), sem=sT)
                ph.wait("vector", mmt)
                if name in ("fx", "fy"):
                    dstr = WfqXr if name == "fx" else WfqYr
                    last_rep = ph.op(
                        "vector", lambda e, hf=hf, dstr=dstr: e.tensor_copy(
                            dstr[:, hf * 512:(hf + 1) * 512], pp[:, hf, :]))
                elif name == "bo":
                    last_rep = ph.op(
                        "vector", lambda e, hf=hf: e.tensor_copy(
                            brep[:, hf * 512:(hf + 1) * 512], pp[:, hf, :]))
                else:
                    ph.op("vector", lambda e, hf=hf: e.tensor_scalar(
                        tmp1k[:, hf * 512:(hf + 1) * 512], pp[:, hf, :],
                        0.1, None, ALU.mult))
                    last_rep = ph.op(
                        "vector", lambda e, hf=hf: e.tensor_add(
                            brep[:, hf * 512:(hf + 1) * 512],
                            brep[:, hf * 512:(hf + 1) * 512],
                            tmp1k[:, hf * 512:(hf + 1) * 512]))
            ph.wait("sync", mmt)
        rep_done = last_rep

        feat_toks = {}

        def extra_v(j, gt):
            ft = fall[:, gt, :]
            ph.op("vector", lambda e, gt=gt: e.tensor_mul(
                tmp1k, xlnf[:, gt % 4, :], WtqXr))
            ph.op("vector", lambda e, ft=ft: e.reduce_sum(ft[:, 10:11], tmp1k,
                                                          axis=AX.X))
            ph.op("vector", lambda e, ft=ft: e.tensor_scalar(
                ft[:, 0:1], ft[:, 10:11], btqr[:, 0:1], None, ALU.add))
            ph.op("vector", lambda e, gt=gt: e.tensor_mul(
                tmp1k, xlnf[:, gt % 4, :], WtqYr))
            ph.op("vector", lambda e, ft=ft: e.reduce_sum(ft[:, 10:11], tmp1k,
                                                          axis=AX.X))
            ph.op("vector", lambda e, ft=ft: e.tensor_scalar(
                ft[:, 1:2], ft[:, 10:11], btqr[:, 1:2], None, ALU.add))
            ph.op("vector", lambda e, ft=ft: e.tensor_mul(
                ft[:, 10:11], ft[:, 0:1], ft[:, 0:1]))
            ph.op("vector", lambda e, ft=ft: e.tensor_mul(
                ft[:, 11:12], ft[:, 1:2], ft[:, 1:2]))
            ph.op("vector", lambda e, ft=ft: e.tensor_add(
                ft[:, 2:3], ft[:, 10:11], ft[:, 11:12]))
            ph.op("vector", lambda e, ft=ft: e.tensor_scalar(
                ft[:, 3:4], ft[:, 0:1], -2.0, None, ALU.mult))
            ph.op("vector", lambda e, ft=ft: e.tensor_scalar(
                ft[:, 4:5], ft[:, 1:2], -2.0, None, ALU.mult))
            ph.op("vector", lambda e, ft=ft: e.scalar_tensor_tensor(
                ft[:, 5:6], ft[:, 0:1], -T_WIN, ft[:, 1:2],
                ALU.mult, ALU.subtract))
            ph.op("vector", lambda e, ft=ft: e.scalar_tensor_tensor(
                ft[:, 6:7], ft[:, 0:1], -T_WIN, ft[:, 1:2],
                ALU.mult, ALU.add))
            ph.op("vector", lambda e, ft=ft: e.tensor_scalar(
                ft[:, 7:8], ft[:, 5:6], -1.0, None, ALU.mult))
            fv = ph.op("vector", lambda e, ft=ft: e.tensor_scalar(
                ft[:, 8:9], ft[:, 6:7], -1.0, None, ALU.mult))
            ph.op("vector", lambda e, gt=gt, ft=ft: e.tensor_copy(
                qptok[:, 2 * gt:2 * gt + 2], ft[:, 0:2]))
            ph.op("vector", lambda e, ft=ft: e.tensor_mul(
                ft[:, 12:13], ft[:, 0:1], ft[:, 1:2]))
            ph.op("vector", lambda e, ft=ft: e.tensor_mul(
                ft[:, 13:14], ft[:, 2:3], ft[:, 0:1]))
            ph.op("vector", lambda e, ft=ft: e.tensor_mul(
                ft[:, 14:15], ft[:, 2:3], ft[:, 1:2]))
            fv = ph.op("vector", lambda e, ft=ft: e.tensor_mul(
                ft[:, 15:16], ft[:, 2:3], ft[:, 2:3]))
            feat_toks[gt] = fv

        ev_all = {}
        nproj = 24
        proj_tok = {}
        kst_tok = {}
        qst_tok = {}
        vst_tok = {}
        ln_state = None

        def pre_evac_wait(ph_, j):
            pass

        for b in range(8):
            sl = b % 2
            tiles = list(range(4 * b, 4 * b + 4))
            dest_cols = [sl * 512 + t * 128 for t in range(4)]
            def pev(ph_, j, b=b):
                if j == 0 and b >= 2:
                    ph_.wait("scalar", proj_tok[b - 2])
            ln_state, ln_evacs = emit_ln_tiles(
                g, ph, xin, tiles, xt, xlnf, xlnTr, bufs, extra_v=extra_v,
                keep_stats=(mcol, rcol), dest_cols=dest_cols,
                pre_evac_wait=pev, state=ln_state)

            ph.wait("tensor", ln_evacs[3])
            if b == 0:
                ph.wait("tensor", wb_tok)
                ph.wait("tensor", bvb_tok)
                ph.wait("tensor", rep_done)
            pidx = 0
            for which, Wb, stg, bias in (("k", WkTb, kstage, bksb),
                                         ("q", WqTb, qstage, bqsb)):
                for mc in range(DC):
                    bank = pidx % 2
                    if pidx >= 2:
                        ph.wait("tensor", ev_all[(b, pidx - 2)])
                    elif b >= 1:
                        ph.wait("tensor", ev_all[(b - 1, nproj - 2 + pidx)])
                    tk = None
                    for c in range(DC):
                        tk = ph.op("tensor", lambda e, sl=sl, mc=mc, c=c,
                                   bank=bank, Wb=Wb: e.matmul(
                                       pp[:, bank, :],
                                       Wb[:, c, mc * 128:(mc + 1) * 128],
                                       xlnTr[:, c, sl * 512:(sl + 1) * 512],
                                       start=(c == 0), stop=(c == DC - 1)),
                                   sem=sT2 if c == DC - 1 else None)
                    ph.wait("scalar", tk)
                    if b >= 1 and mc == 0:
                        ph.wait("scalar", (kst_tok if which == "k"
                                           else qst_tok)[b - 1])
                    ev_all[(b, pidx)] = ph.op(
                        "scalar", lambda e, mc=mc, bank=bank, stg=stg,
                        bias=bias: e.activation(
                            stg[:, 0, mc, :], pp[:, bank, :], AF.Identity,
                            bias=bias[:, mc:mc + 1]))
                    pidx += 1
            for t in range(4):
                for hf in range(2):
                    bank = pidx % 2
                    if pidx >= 2:
                        ph.wait("tensor", ev_all[(b, pidx - 2)])
                    for c in range(DC):
                        ph.op("tensor", lambda e, sl=sl, t=t, c=c, hf=hf,
                              bank=bank: e.matmul(
                                  pp[:, bank, :],
                                  xlnTr[:, c, sl * 512 + t * 128:
                                        sl * 512 + (t + 1) * 128],
                                  WvTb[:, c, hf * 512:(hf + 1) * 512],
                                  start=(c == 0), stop=False))
                    tk = ph.op("tensor", lambda e, bank=bank, hf=hf: e.matmul(
                        pp[:, bank, :], g["onesb"][0:1, 0:128],
                        bvb[0:1, hf * 512:(hf + 1) * 512],
                        start=False, stop=True), sem=sT2)
                    ph.wait("scalar", tk)
                    if b >= 1 and t == 0 and hf == 0:
                        ph.wait("scalar", vst_tok[b - 1])
                    ev_all[(b, pidx)] = ph.op(
                        "scalar", lambda e, t=t, hf=hf, bank=bank:
                        e.activation(vstage[:, 0, t, hf * 512:(hf + 1) * 512],
                                     pp[:, bank, :], AF.Copy))
                    pidx += 1
            proj_tok[b] = ev_all[(b, pidx - 1)]
            for stg, st, toks, dst in (
                    (kstage, stG, kst_tok,
                     g["kTd"].rearrange("c p t -> p c t")[
                         :, :, b * 512:(b + 1) * 512]),
                    (qstage, stH, qst_tok,
                     g["qTd"].rearrange("c p t -> p c t")[
                         :, :, b * 512:(b + 1) * 512]),
                    (vstage, stVd, vst_tok,
                     g["vd"][4 * b:4 * b + 4].rearrange("t p d -> p t d"))):
                ph.wait("gpsimd", proj_tok[b])
                toks[b] = st.dma(ph, "gpsimd", b, dst, stg[:, 0])
            p0_drain(g, ph, 3)

        ph.wait("tensor", feat_toks[NT - 1])
        fev = None
        for gt in range(NT):
            if gt >= 1:
                ph.wait("tensor", fev)
            trt = ph.op("tensor", lambda e, gt=gt: e.transpose(
                pfa[0:16, (gt % 4) * 128:(gt % 4 + 1) * 128],
                fall[:, gt, :], g["identS"]), sem=sT)
            ph.wait("vector", trt)
            fev = ph.op("vector", lambda e, gt=gt: e.tensor_copy(
                fstage[0:10, gt * 128:(gt + 1) * 128],
                pfa[0:10, (gt % 4) * 128:(gt % 4 + 1) * 128]))

        NN = float(S)
        N2 = NN * NN
        rd_tok = ph.op("vector", lambda e: e.reduce_sum(
            red, fall.rearrange("p t f -> p f t"), axis=AX.X))
        ph.wait("tensor", rd_tok)
        smt = ph.op("tensor", lambda e: e.matmul(
            psS[0:1, 0:16], g["oneb1"], red, start=True, stop=True), sem=sT)
        ph.wait("vector", smt)
        ph.op("vector", lambda e: e.tensor_copy(sc[0:1, 0:16], psS[0:1, 0:16]))
        V = lambda fn: ph.op("vector", fn)
        V(lambda e: e.tensor_mul(sc[:, 16:17], sc[:, 0:1], sc[:, 0:1]))
        V(lambda e: e.tensor_mul(sc[:, 17:18], sc[:, 1:2], sc[:, 1:2]))
        V(lambda e: e.tensor_add(sc[:, 16:17], sc[:, 16:17], sc[:, 17:18]))
        V(lambda e: e.tensor_scalar(sc[:, 17:18], sc[:, 16:17], 2.0, None,
                                    ALU.mult))
        V(lambda e: e.scalar_tensor_tensor(sc[:, 18:19], sc[:, 2:3], 2.0 * NN,
                                           sc[:, 17:18], ALU.mult,
                                           ALU.subtract))
        V(lambda e: e.tensor_mul(sc[:, 19:20], sc[:, 10:11], sc[:, 10:11]))
        V(lambda e: e.tensor_mul(sc[:, 20:21], sc[:, 12:13], sc[:, 12:13]))
        V(lambda e: e.tensor_mul(sc[:, 21:22], sc[:, 11:12], sc[:, 11:12]))
        V(lambda e: e.scalar_tensor_tensor(sc[:, 20:21], sc[:, 20:21], 2.0,
                                           sc[:, 19:20], ALU.mult, ALU.add))
        V(lambda e: e.tensor_add(sc[:, 20:21], sc[:, 20:21], sc[:, 21:22]))
        V(lambda e: e.tensor_mul(sc[:, 22:23], sc[:, 13:14], sc[:, 0:1]))
        V(lambda e: e.tensor_mul(sc[:, 23:24], sc[:, 14:15], sc[:, 1:2]))
        V(lambda e: e.tensor_add(sc[:, 22:23], sc[:, 22:23], sc[:, 23:24]))
        V(lambda e: e.tensor_mul(sc[:, 24:25], sc[:, 2:3], sc[:, 2:3]))
        V(lambda e: e.tensor_scalar(sc[:, 24:25], sc[:, 24:25], 2.0, None,
                                    ALU.mult))
        V(lambda e: e.scalar_tensor_tensor(sc[:, 25:26], sc[:, 15:16],
                                           2.0 * NN, sc[:, 24:25], ALU.mult,
                                           ALU.add))
        V(lambda e: e.tensor_scalar(sc[:, 22:23], sc[:, 22:23], 8.0, None,
                                    ALU.mult))
        V(lambda e: e.tensor_sub(sc[:, 25:26], sc[:, 25:26], sc[:, 22:23]))
        V(lambda e: e.tensor_scalar(sc[:, 20:21], sc[:, 20:21], 4.0, None,
                                    ALU.mult))
        V(lambda e: e.tensor_add(sc[:, 25:26], sc[:, 25:26], sc[:, 20:21]))
        V(lambda e: e.tensor_scalar(sc[:, 26:27], sc[:, 18:19], 1.0 / N2,
                                    None, ALU.mult))
        V(lambda e: e.tensor_mul(sc[:, 27:28], sc[:, 18:19], sc[:, 18:19]))
        V(lambda e: e.tensor_scalar(sc[:, 27:28], sc[:, 27:28], 1.0 / N2,
                                    None, ALU.mult))
        V(lambda e: e.tensor_sub(sc[:, 28:29], sc[:, 25:26], sc[:, 27:28]))
        vv = V(lambda e: e.tensor_scalar(sc[:, 28:29], sc[:, 28:29],
                                         1.0 / (N2 - 1.0), None, ALU.mult))
        ph.wait("scalar", vv)
        ph.op("scalar", lambda e: e.activation(sc[:, 29:30], sc[:, 28:29],
                                               AF.Ln))
        sqt = ph.op("scalar", lambda e: e.activation(sc[:, 29:30],
                                                     sc[:, 29:30], AF.Exp,
                                                     scale=0.5))
        ph.wait("vector", sqt)
        tht = V(lambda e: e.scalar_tensor_tensor(
            sc[:, 30:31], sc[:, 29:30], 1.25, sc[:, 26:27], ALU.mult,
            ALU.add))
        ph.wait("tensor", tht)
        ph.wait("tensor", smt)
        rmt = ph.op("tensor", lambda e: e.matmul(
            psS[:, 32:33], g["ones"][0:1, 0:128], sc[0:1, 30:31],
            start=True, stop=True), sem=sT)
        ph.wait("vector", rmt)
        ph.op("vector", lambda e: e.tensor_copy(thr_rep, psS[:, 32:33]))

        ph.wait("sync", fev)
        asg = [(QF, 0, 9), (QF, 1, 3), (QF, 2, 4), (QF, 3, 2),
               (QF, 32, 9), (QF, 33, 5), (QF, 64, 9), (QF, 65, 6),
               (KBf, 0, 2), (KBf, 1, 0), (KBf, 2, 1), (KBf, 3, 9),
               (KBf, 32, 7), (KBf, 33, 9), (KBf, 64, 8), (KBf, 65, 9)]
        for dstt, drow, srow_i in asg:
            t0 = ph.dma("sync", sWs, dstt[drow:drow + 1, :],
                        fstage[srow_i:srow_i + 1, :])
            ph.wait("sync", t0)
        stG.wait_all(ph, "gpsimd")
        stH.wait_all(ph, "gpsimd")
        stVd.wait_all(ph, "gpsimd")
        ph.run()


def _attn_a_ph2(g, QF, KBf, thr_rep):
    nc = g["nc"]
    sT, sT2 = g["sT"], g["sT2"]
    NPAIR = float(S) * float(S)
    from contextlib import ExitStack
    with ExitStack() as stack:
        def sb(name, shape, dtype):
            return stack.enter_context(nc.sbuf_tensor(name, shape, dtype)).ap()

        sumP = sb("s_sum", [128, 256], F32)
        sqP = sb("s_sq", [128, 256], F32)
        sqsc = sb("s_sqsc", [128, 512], F32)
        tot = sb("s_tot", [128, 4], F32)
        sc = sb("s_sc", [1, 8], F32)
        pp = stack.enter_context(nc.psum_tensor("s_pp", [128, 4, 512], F32)).ap()
        pt = stack.enter_context(nc.psum_tensor("s_pt", [128, 512], F32)).ap()

        ph = Ph(g)
        vred = {}
        ssq = {}
        for qt in range(NT):
            p0_drain(g, ph, 1)
            for jb in range(8):
                idx = qt * 8 + jb
                bank = idx % 4
                if idx >= 4:
                    ph.wait("tensor", ssq[idx - 4])
                mmt = ph.op("tensor", lambda e, qt=qt, jb=jb, bank=bank:
                            e.matmul(pp[:, bank, :], QF[0:4,
                                     qt * 128:(qt + 1) * 128],
                                     KBf[0:4, jb * 512:(jb + 1) * 512],
                                     start=True, stop=True), sem=sT)
                ph.wait("vector", mmt)
                vred[idx] = ph.op("vector", lambda e, idx=idx, bank=bank:
                                  e.reduce_sum(sumP[:, idx:idx + 1],
                                               pp[:, bank, :], axis=AX.X))
                ph.wait("scalar", vred[idx])
                ssq[idx] = ph.op("scalar", lambda e, idx=idx, bank=bank:
                                 e.activation(sqsc, pp[:, bank, :], AF.Square,
                                              accum_out=sqP[:, idx:idx + 1]))
        ph.wait("vector", ssq[255])
        v1 = ph.op("vector", lambda e: e.reduce_sum(tot[:, 0:1], sumP,
                                                    axis=AX.X))
        v2 = ph.op("vector", lambda e: e.reduce_sum(tot[:, 1:2], sqP,
                                                    axis=AX.X))
        ph.wait("tensor", v2)
        mt = ph.op("tensor", lambda e: e.matmul(
            pt[0:1, 0:2], g["oneb1"], tot[:, 0:2], start=True, stop=True),
            sem=sT)
        ph.wait("vector", mt)
        ph.op("vector", lambda e: e.tensor_copy(sc[0:1, 4:6], pt[0:1, 0:2]))
        ph.op("vector", lambda e: e.tensor_scalar(
            sc[0:1, 0:1], sc[0:1, 4:5], 1.0 / NPAIR, None, ALU.mult))
        ph.op("vector", lambda e: e.tensor_mul(sc[0:1, 1:2], sc[0:1, 4:5],
                                               sc[0:1, 4:5]))
        ph.op("vector", lambda e: e.tensor_scalar(
            sc[0:1, 1:2], sc[0:1, 1:2], 1.0 / NPAIR, None, ALU.mult))
        ph.op("vector", lambda e: e.tensor_sub(sc[0:1, 1:2], sc[0:1, 5:6],
                                               sc[0:1, 1:2]))
        vvar = ph.op("vector", lambda e: e.tensor_scalar(
            sc[0:1, 1:2], sc[0:1, 1:2], 1.0 / (NPAIR - 1.0), None, ALU.mult))
        ph.wait("scalar", vvar)
        ph.op("scalar", lambda e: e.activation(sc[0:1, 2:3], sc[0:1, 1:2],
                                               AF.Ln))
        sq_tok = ph.op("scalar", lambda e: e.activation(
            sc[0:1, 2:3], sc[0:1, 2:3], AF.Exp, scale=0.5))
        ph.wait("vector", sq_tok)
        thr_tok = ph.op("vector", lambda e: e.scalar_tensor_tensor(
            sc[0:1, 3:4], sc[0:1, 2:3], 1.25, sc[0:1, 0:1], ALU.mult, ALU.add))
        ph.wait("tensor", thr_tok)
        mt2 = ph.op("tensor", lambda e: e.matmul(
            pt[:, 4:5], g["ones"][0:1, 0:128], sc[0:1, 3:4],
            start=True, stop=True), sem=sT)
        ph.wait("vector", mt2)
        fin = ph.op("vector", lambda e: e.tensor_copy(thr_rep, pt[:, 4:5]))
        ph.run()


def _attn_a_ph3(g, xin, xout, QF, KBf, mcol, rcol, qptok, thr_rep,
                WfqXr, WfqYr, brep):
    nc = g["nc"]
    sT, sT2, sG, sWs, sF4 = g["sT"], g["sT2"], g["sG"], g["sWs"], g["sF4"]
    sPV = g["sPV"]
    stW, stW2, stQ, stX, stO = (g["stW"], g["stW2"], g["stQ"], g["stX"],
                                g["stO"])
    stVg = g["stVg"]
    from contextlib import ExitStack
    with ExitStack() as stack:
        def sb(name, shape, dtype):
            return stack.enter_context(nc.sbuf_tensor(name, shape, dtype)).ap()

        def psb(name, shape, dtype=F32):
            return stack.enter_context(nc.psum_tensor(name, shape, dtype)).ap()

        scob = sb("t_sco", [128, 2, S], F32)
        expb = sb("t_exp", [128, S], BF16)
        wT = sb("t_wT", [128, S], BF16)
        kblk = sb("t_kblk", [128, 3, DC, 512], BF16)
        vblk = sb("t_vblk", [128, 4, 4, D], BF16)
        qtl = sb("t_qtl", [128, 2, DC, 128], BF16)
        xres = sb("t_xres", [128, 2, D], F32)
        WoTb = sb("t_WoTb", [128, DC, D], BF16)
        ctxsb = sb("t_ctxsb", [128, D], BF16)
        ctxT = sb("t_ctxT", [128, DC, 128], BF16)
        outa = sb("t_outa", [128, D], F32)
        outsb = sb("t_outsb", [128, 2, D], F32)
        qbuf = sb("t_qbuf", [128, D], F32)
        tmpa = sb("t_tmpa", [128, 512], F32)
        tmpb = sb("t_tmpb", [128, 512], F32)
        nm = sb("t_nm", [128, 1], F32)
        lsum = sb("t_lsum", [128, 1], F32)
        linv = sb("t_linv", [128, 1], F32)
        smal = sb("t_smal", [128, 8], F32)
        ps_s = [psb("t_psA", [128, 512]), psb("t_psB", [128, 512])]
        psD = psb("t_pd", [128, 512])
        psU = psb("t_pu", [128, 2, 512])
        trp = psb("t_trp", [128, 1024], BF16)
        ctxp = psb("t_ctx", [128, 2, 512])

        ph = Ph(g)
        ph.dma("gpsimd", sG, WoTb, g["WoT"].rearrange("(c p) o -> p c o",
                                                      p=128))
        wo_tok = sG.tok

        pv_sched = []
        for ii in range(NT):
            ng = (ii + 4) // 4
            for vb in range(ng):
                pv_sched.append(min(4, ii + 1 - 4 * vb))
        pv_prefix = [0]
        for njt in pv_sched:
            pv_prefix.append(pv_prefix[-1] + njt)
        pv_base = sPV.n
        gv_counter = [0]
        vld = {}
        prev_store = {}
        prev_out_mm = None
        prev_mask_tok = None
        prev_ctx_evac = None
        prev_tr_evac = {}
        prev_pv = {}
        prev_outa_tok = None
        ex_hist = {}
        prev_ot = {}
        blk_mm = {}
        prev_last_blk_mm = None
        for i in range(NT):
            W = (i + 1) * 128
            nb = (W + 511) // 512
            if i >= 2 and prev_out_mm is not None:
                ph.wait("scalar", prev_out_mm)
            qtok = stQ.dma(ph, "scalar", i, qtl[:, i % 2],
                           g["qTd"].rearrange("c p t -> p c t")[
                               :, :, i * 128:(i + 1) * 128])
            if i >= 2:
                ph.wait("scalar", prev_ot[i - 2])
            xrtok = stX.dma(ph, "scalar", i, xres[:, i % 2, :],
                            xin[i * 128:(i + 1) * 128, :])
            p0_drain(g, ph, 2)
            mask_toks = []
            for jb in range(nb):
                wj = min(512, W - jb * 512)
                sl = jb % 2
                if jb >= 3:
                    ph.wait("sync", blk_mm[jb - 3])
                elif prev_last_blk_mm is not None:
                    ph.wait("sync", prev_last_blk_mm)
                ktok = stW.dma(ph, "sync", jb, kblk[:, jb % 3],
                               g["kTd"].rearrange("c p t -> p c t")[
                                   :, :, jb * 512:jb * 512 + 512])
                Gv = gv_counter[0]
                if Gv >= 4:
                    ph.wait("gpsimd", (sPV, pv_base + pv_prefix[Gv - 3]))
                vld[Gv] = stVg.dma(ph, "gpsimd", Gv, vblk[:, Gv % 4],
                                   g["vd"][4 * jb:4 * jb + 4].rearrange(
                                       "t p d -> p t d"))
                gv_counter[0] += 1
                ph.wait("tensor", ktok)
                ph.wait("tensor", qtok)
                if prev_mask_tok is not None and jb <= 1:
                    ph.wait("tensor", prev_mask_tok)
                if len(mask_toks) >= 2:
                    ph.wait("tensor", mask_toks[jb - 2])

                for c in range(DC):
                    ph.op("tensor", lambda e, sl=sl, c=c, i=i, jb=jb, wj=wj:
                          e.matmul(ps_s[sl][:, 0:wj], qtl[:, i % 2, c, :],
                                   kblk[:, jb % 3, c, 0:wj],
                                   start=(c == 0), stop=(c == DC - 1)))
                if mask_toks:
                    ph.wait("tensor", mask_toks[jb - 1])
                ph.op("tensor", lambda e, i=i, jb=jb, wj=wj: e.matmul(
                    psD[:, 0:wj], QF[0:4, i * 128:(i + 1) * 128],
                    KBf[0:4, jb * 512:jb * 512 + wj],
                    start=True, stop=True))
                ph.op("tensor", lambda e, i=i, jb=jb, wj=wj: e.matmul(
                    psU[:, 0, 0:wj], QF[32:34, i * 128:(i + 1) * 128],
                    KBf[32:34, jb * 512:jb * 512 + wj],
                    start=True, stop=True))
                mmt = ph.op("tensor", lambda e, i=i, jb=jb, wj=wj:
                            e.matmul(
                                psU[:, 1, 0:wj],
                                QF[64:66, i * 128:(i + 1) * 128],
                                KBf[64:66, jb * 512:jb * 512 + wj],
                                start=True, stop=True), sem=sT2)
                blk_mm[jb] = mmt
                ph.wait("vector", mmt)
                if jb == 0 and (i - 2) in ex_hist:
                    ph.wait("vector", ex_hist[i - 2])
                ph.op("vector", lambda e, wj=wj: e.tensor_scalar(
                    tmpa[:, 0:wj], psD[:, 0:wj], thr_rep, NEG,
                    ALU.is_gt, ALU.mult))
                ph.op("vector", lambda e, wj=wj: e.tensor_scalar(
                    tmpb[:, 0:wj], psU[:, 1, 0:wj], 0.0, None,
                    ALU.is_gt))
                ph.op("vector", lambda e, wj=wj: e.scalar_tensor_tensor(
                    tmpb[:, 0:wj], psU[:, 0, 0:wj], 0.0, tmpb[:, 0:wj],
                    ALU.is_gt, ALU.mult))
                ph.op("vector", lambda e, sl=sl, jb=jb, wj=wj, i=i:
                      e.scalar_tensor_tensor(
                          scob[:, i % 2, jb * 512:jb * 512 + wj],
                          ps_s[sl][:, 0:wj],
                          INV_SQRT_D, tmpa[:, 0:wj], ALU.mult, ALU.add))
                mk = ph.op("vector", lambda e, sl=sl, jb=jb, wj=wj, i=i:
                           e.scalar_tensor_tensor(
                               scob[:, i % 2, jb * 512:jb * 512 + wj],
                               tmpb[:, 0:wj],
                               NEG, scob[:, i % 2, jb * 512:jb * 512 + wj],
                               ALU.mult, ALU.add))
                mask_toks.append(mk)
            prev_last_blk_mm = blk_mm[nb - 1]
            dg = ph.op("vector", lambda e, i=i: e.tensor_add(
                scob[:, i % 2, i * 128:(i + 1) * 128],
                scob[:, i % 2, i * 128:(i + 1) * 128], g["triS"]))
            prev_mask_tok = dg
            nm_tok = ph.op("vector", lambda e, W=W, i=i: e.reduce_max(
                nm, scob[:, i % 2, 0:W], axis=AX.X, negate=True))
            ph.wait("scalar", nm_tok)
            ex_tok = ph.op("scalar", lambda e, W=W, i=i: e.activation(
                expb[:, 0:W], scob[:, i % 2, 0:W], AF.Exp, bias=nm,
                accum_out=lsum))
            ex_hist[i] = ex_tok
            ph.wait("vector", ex_tok)
            lv_tok = ph.op("vector", lambda e: e.reciprocal(linv, lsum))
            trt = {}
            trev = {}
            for jb0 in range(0, i + 1, 8):
                jend = min(jb0 + 8, i + 1)
                ph.wait("tensor", ex_tok)
                if jb0 >= 8:
                    ph.wait("tensor", trev[jb0 - 1])
                elif i >= 1 and prev_ctx_evac is not None:
                    ph.wait("tensor", prev_ctx_evac)
                for jt in range(jb0, jend):
                    trt[jt] = ph.op("tensor", lambda e, jt=jt, reg=jt % 8:
                                    e.transpose(
                                        trp[:, reg * 128:(reg + 1) * 128],
                                        expb[:, jt * 128:(jt + 1) * 128],
                                        g["identbS"]),
                                    sem=sT if jt == jend - 1 else None)
                ph.wait("scalar", trt[jend - 1])
                if i >= 1 and prev_pv and jb0 == 0:
                    ph.wait("scalar", prev_pv[max(prev_pv)])
                for jt in range(jb0, jend):
                    trev[jt] = ph.op("scalar", lambda e, jt=jt, reg=jt % 8:
                                     e.activation(
                                         wT[:, jt * 128:(jt + 1) * 128],
                                         trp[:, reg * 128:(reg + 1) * 128],
                                         AF.Copy))
            pv = {}
            gv0 = gv_counter[0] - nb
            for jt in range(i + 1):
                vb = jt // 4
                gvb = gv0 + vb
                if jt % 4 == 0:
                    ph.wait("tensor", vld[gvb])
                ph.wait("tensor", trev[jt])
                if jt == 0 and prev_outa_tok is not None:
                    ph.wait("tensor", prev_outa_tok)
                for hf in range(2):
                    pv[(jt, hf)] = ph.op(
                        "tensor", lambda e, jt=jt, hf=hf, gvb=gvb, i=i: e.matmul(
                            ctxp[:, hf, :], wT[:, jt * 128:(jt + 1) * 128],
                            vblk[:, gvb % 4, jt % 4, hf * 512:(hf + 1) * 512],
                            start=(jt == 0), stop=(jt == i)),
                        sem=sPV if hf == 1 else None)
            assert sPV.n == pv_base + pv_prefix[gv_counter[0]], (
                sPV.n, pv_base, pv_prefix[gv_counter[0]])
            prev_pv = pv
            ph.wait("scalar", pv[(i, 1)])
            ce = None
            for hf in range(2):
                ce = ph.op("scalar", lambda e, hf=hf: e.activation(
                    ctxsb[:, hf * 512:(hf + 1) * 512], ctxp[:, hf, :],
                    AF.Copy))
            ph.wait("tensor", ce)
            ph.wait("tensor", trev[i])
            ctr = None
            for c in range(DC):
                ctr = ph.op("tensor", lambda e, c=c: e.transpose(
                    trp[:, c * 128:(c + 1) * 128],
                    ctxsb[:, c * 128:(c + 1) * 128], g["identbS"]),
                    sem=sT if c == DC - 1 else None)
            ph.wait("scalar", ctr)
            ce2 = None
            for c in range(DC):
                ce2 = ph.op("scalar", lambda e, c=c: e.activation(
                    ctxT[:, c, :], trp[:, c * 128:(c + 1) * 128],
                    AF.Copy))
            prev_ctx_evac = ce2
            for c in range(DC):
                if c == 0:
                    ph.wait("tensor", ce2)
                    if i == 0:
                        ph.wait("tensor", wo_tok)
                for hf in range(2):
                    prev_out_mm = ph.op(
                        "tensor", lambda e, c=c, hf=hf: e.matmul(
                            ctxp[:, hf, :],
                            ctxT[:, c, :],
                            WoTb[:, c, hf * 512:(hf + 1) * 512],
                            start=(c == 0), stop=(c == DC - 1)),
                        sem=sT2 if (c == DC - 1 and hf == 1) else None)
            ph.wait("vector", prev_out_mm)
            ph.wait("vector", lv_tok)
            for hf in range(2):
                prev_outa_tok = ph.op("vector", lambda e, hf=hf: e.tensor_scalar(
                    outa[:, hf * 512:(hf + 1) * 512],
                    ctxp[:, hf, :],
                    linv, None, ALU.mult))
            ph.op("vector", lambda e, i=i: e.tensor_mul(
                smal[:, 1:2], qptok[:, 2 * i:2 * i + 1],
                qptok[:, 2 * i:2 * i + 1]))
            ph.op("vector", lambda e, i=i: e.tensor_mul(
                smal[:, 2:3], qptok[:, 2 * i + 1:2 * i + 2],
                qptok[:, 2 * i + 1:2 * i + 2]))
            n2t = ph.op("vector", lambda e: e.tensor_add(
                smal[:, 3:4], smal[:, 1:2], smal[:, 2:3]))
            ph.wait("scalar", n2t)
            ph.op("scalar", lambda e: e.activation(smal[:, 4:5], smal[:, 3:4],
                                                   AF.Ln))
            rn_tok = ph.op("scalar", lambda e: e.activation(
                smal[:, 4:5], smal[:, 4:5], AF.Exp, scale=0.5))
            ph.wait("vector", rn_tok)
            ph.op("vector", lambda e: e.tensor_scalar(
                smal[:, 4:5], smal[:, 4:5], 1.0e-12, None, ALU.max))
            ph.op("vector", lambda e: e.reciprocal(smal[:, 5:6], smal[:, 4:5]))
            ph.op("vector", lambda e, i=i: e.tensor_scalar(
                smal[:, 6:7], qptok[:, 2 * i:2 * i + 1], smal[:, 5:6], 0.1,
                ALU.mult, ALU.mult))
            ph.op("vector", lambda e, i=i: e.tensor_scalar(
                smal[:, 7:8], qptok[:, 2 * i + 1:2 * i + 2], smal[:, 5:6], 0.1,
                ALU.mult, ALU.mult))
            ph.op("vector", lambda e: e.tensor_scalar(
                qbuf, WfqXr, smal[:, 6:7], None, ALU.mult))
            ph.op("vector", lambda e: e.scalar_tensor_tensor(
                qbuf, WfqYr, smal[:, 7:8], qbuf, ALU.mult, ALU.add))
            ph.wait("vector", xrtok)
            ph.op("vector", lambda e: e.tensor_add(outa, outa, brep))
            ph.op("vector", lambda e: e.tensor_add(outa, outa, qbuf))
            ph.op("vector", lambda e, i=i: e.tensor_scalar(
                qbuf, xres[:, i % 2, :], mcol[:, i:i + 1], rcol[:, i:i + 1],
                ALU.subtract, ALU.mult))
            ph.op("vector", lambda e: e.tensor_add(outa, outa, qbuf))
            if i >= 2:
                ph.wait("vector", prev_store[i - 2])
            ot = ph.op("vector", lambda e, i=i: e.tensor_add(
                outsb[:, i % 2, :], outa, xres[:, i % 2, :]))
            prev_ot[i] = ot
            ph.wait("scalar", ot)
            prev_store[i] = stO.dma(ph, "scalar", i,
                                    xout[i * 128:(i + 1) * 128, :],
                                    outsb[:, i % 2, :])
        stO.wait_all(ph, "scalar")
        ph.run()


def phase_attn_b(g, xin, xout):
    nc = g["nc"]
    sT, sT2, sG, sF4 = g["sT"], g["sT2"], g["sG"], g["sF4"]
    stW, stW2, stW3, stH, stO = (g["stW"], g["stW2"], g["stW3"], g["stH"],
                                 g["stO"])
    from contextlib import ExitStack

    with ExitStack() as stack:
        def sb(name, shape, dtype):
            return stack.enter_context(nc.sbuf_tensor(name, shape, dtype)).ap()

        def psb(name, shape, dtype=F32):
            return stack.enter_context(nc.psum_tensor(name, shape, dtype)).ap()

        HW = 2048
        xt = sb("b_xt", [128, 4, D], F32)
        xlnf = sb("b_xlnf", [128, 4, D], F32)
        xlnT = sb("b_xlnT", [128, DC, S], BF16)
        bufs = dict(bn=sb("b_bn", [128, 12], F32), mv=sb("b_mv", [128, 2], F32),
                    v1=sb("b_v1", [128, 1], F32), rst=sb("b_rst", [128, 1], F32),
                    pstr=psb("b_pstr", [128, 2, 512]))
        Hb = sb("b_Hb", [128, 2, HW], F32)
        Ftb = sb("b_Ftb", [128, 2, HW], F32)
        Itb = sb("b_Itb", [128, 2, HW], F32)
        Sgb = sb("b_Sgb", [128, 2, HW], F32)
        Hsc = sb("b_Hsc", [128, 2, HW], F32)
        hbb = sb("b_hbb", [128, 2, HW], BF16)
        w1g = sb("b_w1g", [128, 2, 3, DC * 128], BF16)
        b1sb = sb("b_b1", [128, 3 * GC], F32)
        b1ng = sb("b_b1ng", [128, 2 * GC], F32)
        inith = sb("b_inith", [128, GC], F32)
        ginit = sb("b_ginit", [128, GC], F32)
        gtmp = sb("b_gtmp", [128, GC], F32)
        pp = psb("b_pp", [128, 6, 512])

        ph = Ph(g)
        p0_wait(g, ph, "sync", [4])
        b1_tok = ph.dma("sync", g["sWs"], b1sb, g["b1T"])
        ph.wait("sync", b1_tok)
        ih_tok = ph.dma("sync", g["sWs"], inith, g["inithT"])
        ph.wait("vector", b1_tok)
        b1n_tok = ph.op("vector", lambda e: e.tensor_scalar(
            b1ng, b1sb[:, GC:3 * GC], -1.0, None, ALU.mult))
        ph.wait("scalar", ih_tok)
        tk = ph.op("scalar", lambda e: e.activation(gtmp, inith, AF.Sigmoid))
        ph.wait("vector", tk)
        ph.op("vector", lambda e: e.tensor_sub(ginit, inith, gtmp))
        ph.op("vector", lambda e: e.tensor_scalar(ginit, ginit, 0.5, 0.0,
                                                  ALU.add, ALU.max))
        gi_tok = ph.op("vector", lambda e: e.tensor_add(ginit, ginit, gtmp))

        ln_state = [None]
        ln_evacs = []

        def emit_ln_group(grp):
            st, ev = emit_ln_tiles(g, ph, xin, list(range(4 * grp, 4 * grp + 4)),
                                   xt, xlnf, xlnT, bufs,
                                   dest_cols=[(4 * grp + t) * 128
                                              for t in range(4)],
                                   state=ln_state[0])
            ln_state[0] = st
            ln_evacs.extend(ev)

        mm_tok = {}
        ev_tok = {}
        evI_tok = {}
        sig_tok = {}
        g_tok = {}
        v_tok = {}
        scan_tok = {}
        cast_tok = {}
        st_tok = {}
        tails = []

        def emit_tail(U):
            cc, hh = U // 2, U % 2
            p = hh
            ph.wait("vector", g_tok[U])
            v_tok[U] = ph.op("vector", lambda e, p=p: e.tensor_mul(
                Ftb[:, p, :], Ftb[:, p, :], Hb[:, p, :]))
            if U >= 2:
                ph.wait("vector", cast_tok[U - 2])
            if U == 0:
                ph.wait("vector", gi_tok)
            init = (ginit[:, cc:cc + 1] if hh == 0
                    else Hsc[:, 0, HW - 1:HW])
            scan_tok[U] = ph.op("vector", lambda e, p=p, init=init:
                                e.tensor_tensor_scan(
                                    Hsc[:, p, :], Itb[:, p, :], Ftb[:, p, :],
                                    init, ALU.mult, ALU.add))

        def emit_cast(U):
            p = U % 2
            ph.wait("scalar", scan_tok[U])
            if U >= 2:
                ph.wait("scalar", st_tok[U - 2])
            cast_tok[U] = ph.op("scalar", lambda e, p=p: e.activation(
                hbb[:, p, :], Hsc[:, p, :], AF.Copy))

        def emit_store(U):
            cc, hh = U // 2, U % 2
            ph.wait("gpsimd", cast_tok[U])
            st_tok[U] = stH.dma(ph, "gpsimd", U,
                                g["hd"][cc][:, hh * HW:(hh + 1) * HW],
                                hbb[:, hh, :])

        for cc in range(GC):
            wtoks = []
            for gi, st in ((0, stW), (1, stW2), (2, stW3)):
                if cc >= 2:
                    ph.wait("sync", mm_tok[(cc - 2, 7, 2)])
                wtoks.append(st.dma(ph, "sync", cc, w1g[:, cc % 2, gi, :],
                                    g["W1b"][gi * GC + cc]))
            for nb in range(8):
                hh = nb // 4
                U = cc * 2 + hh
                p = hh
                if cc == 0:
                    emit_ln_group(nb)
                if nb % 4 == 0:
                    if U >= 1:
                        emit_tail(U - 1)
                    if U >= 2:
                        ph.wait("scalar", v_tok[U - 2])
                        ph.wait("scalar", scan_tok[U - 2])
                        ph.wait("scalar", g_tok[U - 2])
                for gi in range(3):
                    bank = gi * 2 + (nb % 2)
                    if nb == 0:
                        ph.wait("tensor", wtoks[gi])
                    if cc == 0 and gi == 0:
                        ph.wait("tensor", ln_evacs[4 * nb + 3])
                    if nb >= 2:
                        ph.wait("tensor", ev_tok[(cc, nb - 2, gi)])
                    elif cc >= 1:
                        ph.wait("tensor", ev_tok[(cc - 1, nb + 6, gi)])
                    for c in range(DC):
                        mm_tok[(cc, nb, gi)] = ph.op(
                            "tensor", lambda e, cc=cc, gi=gi, nb=nb, c=c,
                            bank=bank: e.matmul(
                                pp[:, bank, :],
                                w1g[:, cc % 2, gi, c * 128:(c + 1) * 128],
                                xlnT[:, c, nb * 512:(nb + 1) * 512],
                                start=(c == 0), stop=(c == DC - 1)),
                            sem=sT2 if c == DC - 1 else None)
                    col = (nb % 4) * 512
                    gcol = gi * GC + cc
                    ph.wait("scalar", mm_tok[(cc, nb, gi)])
                    if cc == 0 and nb == 0 and gi == 0:
                        ph.wait("scalar", b1_tok)
                        ph.wait("scalar", b1n_tok)
                    if gi == 0:
                        ev_tok[(cc, nb, gi)] = ph.op(
                            "scalar", lambda e, p=p, col=col, gcol=gcol,
                            bank=bank: e.activation(
                                Hb[:, p, col:col + 512], pp[:, bank, :],
                                AF.Identity, bias=b1sb[:, gcol:gcol + 1]))
                    else:
                        dst = Ftb if gi == 1 else Itb
                        ncol = (gi - 1) * GC + cc
                        ev_tok[(cc, nb, gi)] = ph.op(
                            "scalar", lambda e, p=p, col=col, ncol=ncol,
                            bank=bank, dst=dst: e.activation(
                                dst[:, p, col:col + 512], pp[:, bank, :],
                                AF.Exp, scale=-1.0,
                                bias=b1ng[:, ncol:ncol + 1]))
                if nb % 4 == 3:
                    evI_tok[U] = ev_tok[(cc, nb, 2)]
                    ph.op("scalar", lambda e, p=p: e.activation(
                        Ftb[:, p, :], Ftb[:, p, :], AF.Ln, bias=g["oneb1"]))
                    lnI_tok = ph.op("scalar", lambda e, p=p: e.activation(
                        Itb[:, p, :], Itb[:, p, :], AF.Ln, bias=g["oneb1"]))
                    ph.wait("vector", lnI_tok)
                    d2_tok = ph.op("vector", lambda e, p=p: e.tensor_sub(
                        Itb[:, p, :], Itb[:, p, :], Ftb[:, p, :]))
                    ph.wait("scalar", d2_tok)
                    c_tok = ph.op("scalar", lambda e, p=p: e.activation(
                        Itb[:, p, :], Itb[:, p, :], AF.Sigmoid))
                    sig_tok[U] = ph.op("scalar", lambda e, p=p: e.activation(
                        Sgb[:, p, :], Hb[:, p, :], AF.Sigmoid))
                    if U >= 1:
                        emit_cast(U - 1)
                    ph.wait("vector", c_tok)
                    ph.op("vector", lambda e, p=p: e.tensor_scalar(
                        Ftb[:, p, :], Itb[:, p, :], -1.0, 1.0,
                        ALU.mult, ALU.add))
                    ph.wait("vector", sig_tok[U])
                    ph.op("vector", lambda e, p=p: e.tensor_sub(
                        Hb[:, p, :], Hb[:, p, :], Sgb[:, p, :]))
                    ph.op("vector", lambda e, p=p: e.tensor_scalar(
                        Hb[:, p, :], Hb[:, p, :], 0.5, 0.0,
                        ALU.add, ALU.max))
                    g_tok[U] = ph.op("vector", lambda e, p=p: e.tensor_add(
                        Hb[:, p, :], Hb[:, p, :], Sgb[:, p, :]))
                    if U >= 1:
                        emit_store(U - 1)
        emit_tail(31)
        emit_cast(31)
        emit_store(31)
        stH.wait_all(ph, "gpsimd")
        ph.run()

    with ExitStack() as stack:
        def sb(name, shape, dtype):
            return stack.enter_context(nc.sbuf_tensor(name, shape, dtype)).ap()

        def psb(name, shape, dtype=F32):
            return stack.enter_context(nc.psum_tensor(name, shape, dtype)).ap()

        W2b = sb("b_W2b", [128, GC, D], BF16)
        hsb = sb("b_hsb", [128, 2, 512], BF16)
        b2row = sb("b_b2row", [1, D], F32)
        b2rep = sb("b_b2rep", [128, D], F32)
        xres = sb("b_xres", [128, 4, D], F32)
        outsb = sb("b_outsb", [128, 2, D], F32)
        po = psb("b_po", [128, 8, 512])

        ph = Ph(g)
        w2_toks = []
        for q in range(4):
            w2_toks.append(g["stVg"].dma(
                ph, "gpsimd", q, W2b[:, 4 * q:4 * q + 4, :],
                g["W2"][512 * q:512 * (q + 1), :].rearrange(
                    "(c p) o -> p c o", p=128)))
        b2_tok = ph.dma("sync", g["sWs"], b2row, g["b2"])
        ph.wait("tensor", b2_tok)
        rep_toks = []
        for hf in range(2):
            rep_toks.append(ph.op("tensor", lambda e, hf=hf: e.matmul(
                po[:, hf, :], g["ones"][0:1, 0:128],
                b2row[0:1, hf * 512:(hf + 1) * 512], start=True, stop=True),
                sem=sT))
        ph.wait("vector", rep_toks[-1])
        brep_tok = None
        for hf in range(2):
            brep_tok = ph.op("vector", lambda e, hf=hf: e.tensor_copy(
                b2rep[:, hf * 512:(hf + 1) * 512], po[:, hf, :]))

        mm_tok = {}
        add_tok = {}
        sto = {}
        for tg in range(8):
            f4toks = []
            for t in range(4):
                gt = tg * 4 + t
                if tg > 0:
                    ph.wait("scalar", add_tok[(tg - 1, t)])
                f4toks.append(ph.dma("scalar", sF4, xres[:, t, :],
                                     xin[gt * 128:(gt + 1) * 128, :]))
            for cc in range(GC):
                kidx = tg * GC + cc
                if kidx >= 2:
                    ph.wait("sync", mm_tok[kidx - 2])
                htok = stW.dma(ph, "sync", kidx, hsb[:, kidx % 2, :],
                               g["hd"][cc][:, tg * 512:(tg + 1) * 512])
                ph.wait("tensor", htok)
                if cc % 4 == 0 and tg == 0:
                    ph.wait("tensor", w2_toks[cc // 4])
                if cc == 0:
                    if tg == 0:
                        ph.wait("tensor", brep_tok)
                    else:
                        ph.wait("tensor", add_tok[(tg - 1, 3)])
                for t in range(4):
                    for hf in range(2):
                        last = (t == 3 and hf == 1)
                        mm_tok[kidx] = ph.op(
                            "tensor", lambda e, cc=cc, t=t, hf=hf, kidx=kidx:
                            e.matmul(
                                po[:, 2 * t + hf, :],
                                hsb[:, kidx % 2, t * 128:(t + 1) * 128],
                                W2b[:, cc, hf * 512:(hf + 1) * 512],
                                start=(cc == 0), stop=(cc == GC - 1)),
                            sem=sT2 if last else None)
            for t in range(4):
                gt = tg * 4 + t
                ph.wait("vector", mm_tok[tg * GC + GC - 1])
                ph.wait("vector", f4toks[3])
                if t >= 2:
                    ph.wait("vector", sto[(tg, t - 2)])
                elif tg > 0:
                    ph.wait("vector", sto[(tg - 1, t + 2)])
                ph.op("vector", lambda e, t=t: e.tensor_add(
                    outsb[:, t % 2, :],
                    po.rearrange("p a b -> p (a b)")[:, t * 1024:(t + 1) * 1024],
                    b2rep))
                atok = ph.op("vector", lambda e, t=t: e.tensor_add(
                    outsb[:, t % 2, :], outsb[:, t % 2, :], xres[:, t, :]))
                add_tok[(tg, t)] = atok
                ph.wait("scalar", atok)
                sto[(tg, t)] = stO.dma(ph, "scalar", t,
                                       xout[gt * 128:(gt + 1) * 128, :],
                                       outsb[:, t % 2, :])
        stO.wait_all(ph, "scalar")
        ph.run()



def _prep_shared(inputs):
    f = np.ascontiguousarray
    ii = {}
    ii["WqT"] = f(inputs["L0_attn_a_Wq"].T)
    ii["WkT"] = f(inputs["L0_attn_a_Wk"].T)
    ii["WvT"] = f(inputs["L0_attn_a_Wv"].T)
    ii["WoT"] = f(inputs["L0_attn_a_Wo"].T)
    ii["bqT"] = f(inputs["L0_attn_a_bq"].reshape(DC, 128).T)
    ii["bkT"] = f(inputs["L0_attn_a_bk"].reshape(DC, 128).T)
    ii["bv"] = f(inputs["L0_attn_a_bv"].reshape(1, D))
    ii["bo"] = f(inputs["L0_attn_a_bo"].reshape(1, D))
    ii["bfq"] = f(inputs["L0_attn_a_bfq"].reshape(1, D))
    ii["WtqX"] = f(inputs["L0_attn_a_Wtq"][0:1, :])
    ii["WtqY"] = f(inputs["L0_attn_a_Wtq"][1:2, :])
    ii["btq"] = f(inputs["L0_attn_a_btq"].reshape(1, 2))
    ii["WfqX"] = f(inputs["L0_attn_a_Wfq"][:, 0].reshape(1, D))
    ii["WfqY"] = f(inputs["L0_attn_a_Wfq"][:, 1].reshape(1, D))
    ii["W1"] = f(inputs["L0_attn_b_W1"])
    ii["b1T"] = f(inputs["L0_attn_b_b1"].reshape(3 * GC, 128).T)
    ii["W2"] = f(inputs["L0_attn_b_W2"])
    ii["b2"] = f(inputs["L0_attn_b_b2"].reshape(1, D))
    ii["inithT"] = f(inputs["L0_attn_b_init_h"].reshape(GC, 128).T)
    ii["w1T"] = f(inputs["L0_ffn_w1_w"].T)
    ii["w2T"] = f(inputs["L0_ffn_w2_w"].T)
    ii["h1T"] = f(inputs["L0_ffn_h1_w"].T)
    ii["h2T"] = f(inputs["L0_ffn_h2_w"].T)
    ii["w3T"] = f(inputs["L0_ffn_w3_w"].T)
    ii["s1T"] = f(inputs["L0_ffn_w1_s"].reshape(HC, 128).T)
    ii["s2T"] = f(inputs["L0_ffn_w2_s"].reshape(HC, 128).T)
    ii["s3T"] = f(inputs["L0_ffn_h1_s"].reshape(HC, 128).T)
    ii["s4T"] = f(inputs["L0_ffn_h2_s"].reshape(HC, 128).T)
    ii["sw3r"] = f(inputs["L0_ffn_w3_s"].reshape(1, D))
    ii["ident"] = np.eye(128, dtype=np.float32)
    ii["identb"] = np.eye(128, dtype=ml_dtypes.bfloat16)
    jj, kk = np.meshgrid(np.arange(128), np.arange(128), indexing="ij")
    ii["tri"] = np.where(kk > jj, np.float32(NEG),
                         np.float32(0.0)).astype(np.float32)
    return ii


def run_cores(inputs, stages=ALL_STAGES, taps=(), cores=8, trace=False,
              dumps=()):
    xfull = np.ascontiguousarray(
        inputs["L0_attn_a_hidden_states"], dtype=np.float32)
    shared = _prep_shared(inputs)
    nc = build_program(stages=stages, taps=taps, dumps=dumps)
    in_maps = []
    for b in range(cores):
        m = dict(shared)
        m["x0"] = np.ascontiguousarray(xfull[b])
        in_maps.append(m)
    res = run_bass_kernel_spmd(nc, in_maps, core_ids=list(range(cores)),
                               trace=trace)
    return res


def kernel(**inputs):
    res = run_cores(inputs)
    out = np.stack([np.asarray(res.results[b]["y"]) for b in range(8)], axis=0)
    return out.astype(np.float32)


# version 26
# speedup vs baseline: 1.0153x; 1.0153x over previous
import numpy as np
import ml_dtypes
import concourse.bass as bass
import concourse.mybir as mybir
from concourse.bass_utils import run_bass_kernel_spmd

F32 = mybir.dt.float32
BF16 = mybir.dt.bfloat16
AF = mybir.ActivationFunctionType
ALU = mybir.AluOpType
AX = mybir.AxisListType

D = 1024
S = 4096
NT = S // 128
DC = D // 128
DH = 4096
HC = DH // 128
DG = 2048
GC = DG // 128
NEG = -1.0e30
T_WIN = 2.4414063e-4
EPS_LN = 1.0e-5
INV_SQRT_D = 1.0 / 32.0

ALL_STAGES = ("attn_a", "ffn1", "attn_b", "ffn2")
ENGS = ("sync", "tensor", "vector", "scalar", "gpsimd")
ENG_SEM = {"vector": "sV", "scalar": "sS"}


class Sem:
    def __init__(self, nc, name):
        self.h = nc.alloc_semaphore(name=name)
        self.n = 0

    @property
    def tok(self):
        return (self, self.n)


class Stream:

    def __init__(self, nc, name, nslots):
        self.sems = [Sem(nc, f"{name}{i}") for i in range(nslots)]

    def dma(self, ph, eng, slot, out, in_):
        s = self.sems[slot % len(self.sems)]
        ph.wait(eng, s.tok)
        ph.dma(eng, s, out, in_)
        return s.tok

    def wait_all(self, ph, eng):
        for s in self.sems:
            ph.wait(eng, s.tok)


class Ph:

    def __init__(self, g):
        self.nc = g["nc"]
        self.g = g
        self.q = {k: [] for k in ENGS}

    def wait(self, eng, tok):
        sem, v = tok
        assert v >= 0
        self.q[eng].append(lambda e, s=sem.h, v=v: e.wait_ge(s, v))

    def op(self, eng, fn, sem=None, k=1):
        if eng in ENG_SEM:
            au = self.g[ENG_SEM[eng]]
            self.wait(eng, au.tok)
            if sem is None:
                sem = au
        if sem is None:
            self.q[eng].append(lambda e, fn=fn: fn(e))
            return None
        def run(e, fn=fn, sem=sem, k=k):
            return fn(e).then_inc(sem.h, k)
        self.q[eng].append(run)
        sem.n += k
        return sem.tok

    def dma(self, eng, sem, out, in_):
        def run(e, o=out, i=in_, s=sem):
            return e.dma_start(o, i).then_inc(s.h, 16)
        self.q[eng].append(run)
        sem.n += 16
        return sem.tok

    def run(self):
        with self.nc.Block() as blk:
            for name, dec in (("sync", blk.sync), ("tensor", blk.tensor),
                              ("vector", blk.vector), ("scalar", blk.scalar),
                              ("gpsimd", blk.gpsimd)):
                if self.q[name]:
                    fns = self.q[name]
                    dec(lambda e, fns=fns: [fn(e) for fn in fns])



def build_program(stages=ALL_STAGES, taps=(), dumps=()):
    nc = bass.Bass("TRN2", target_bir_lowering=False)

    def din(name, shape, dtype=F32):
        return nc.dram_tensor(name, list(shape), dtype, kind="ExternalInput").ap()

    def dscr(name, shape, dtype):
        return nc.dram_tensor(name, list(shape), dtype, kind="Internal").ap()

    g = dict(nc=nc)
    g["x0"] = din("x0", [S, D])
    for n, sh in [("WqT", [D, D]), ("WkT", [D, D]), ("WvT", [D, D]), ("WoT", [D, D]),
                  ("bqT", [128, DC]), ("bkT", [128, DC]), ("bv", [1, D]),
                  ("bo", [1, D]), ("bfq", [1, D]), ("WtqX", [1, D]), ("WtqY", [1, D]),
                  ("btq", [1, 2]), ("WfqX", [1, D]), ("WfqY", [1, D]),
                  ("W1", [D, 3 * DG]), ("b1T", [128, 3 * GC]), ("W2", [DG, D]),
                  ("b2", [1, D]), ("inithT", [128, GC]),
                  ("w1T", [D, DH]), ("w2T", [D, DH]), ("h1T", [DH, DH]),
                  ("h2T", [DH, DH]), ("w3T", [DH, D]),
                  ("s1T", [128, HC]), ("s2T", [128, HC]), ("s3T", [128, HC]),
                  ("s4T", [128, HC]), ("sw3r", [1, D]),
                  ("ident", [128, 128]), ("tri", [128, 128])]:
        g[n] = din(n, sh)
    g["identb"] = din("identb", [128, 128], BF16)
    g["y"] = nc.dram_tensor("y", [S, D], F32, kind="ExternalOutput").ap()

    g["x1"] = dscr("x1", [S, D], F32)
    g["x2"] = dscr("x2", [S, D], F32)
    g["x3"] = dscr("x3", [S, D], F32)
    g["qTd"] = dscr("qTd", [DC, 128, S], BF16)
    g["kTd"] = dscr("kTd", [DC, 128, S], BF16)
    g["vd"] = dscr("vd", [NT, 128, D], BF16)
    g["hd"] = dscr("hd", [GC, 128, S], BF16)
    w12 = dscr("w12tb", [HC, 128, 2, DC * 128], BF16)
    g["w12tb"] = w12
    g["w1tb"] = w12[:, :, 0, :]
    g["w2tb"] = w12[:, :, 1, :]
    g["h1tb"] = dscr("h1tb", [HC, 128, HC * 128], BF16)
    g["h2tb"] = dscr("h2tb", [HC, 128, HC * 128], BF16)
    g["w3tb"] = dscr("w3tb", [HC, 128, D], BF16)
    g["W1b"] = dscr("W1b", [3 * GC, 128, DC * 128], BF16)

    taps_d = {}
    for t in taps:
        taps_d[t] = nc.dram_tensor("tap_" + t, [S, D], F32,
                                   kind="ExternalOutput").ap()

    g["identS"] = nc.alloc_sbuf_tensor("identS", [128, 128], F32).ap()
    g["identbS"] = nc.alloc_sbuf_tensor("identbS", [128, 128], BF16).ap()
    g["triS"] = nc.alloc_sbuf_tensor("triS", [128, 128], F32).ap()
    g["ones"] = nc.alloc_sbuf_tensor("onesS", [1, 512], F32).ap()
    g["oneb1"] = nc.alloc_sbuf_tensor("oneb1S", [128, 1], F32).ap()
    g["halfb1"] = nc.alloc_sbuf_tensor("halfb1S", [128, 1], F32).ap()
    g["onesb"] = nc.alloc_sbuf_tensor("onesbS", [1, 512], BF16).ap()

    for s in ("sT", "sT2", "sV", "sS", "sG", "sGc", "sWs", "sF4", "sPV"):
        g[s] = Sem(nc, s)
    g["sP0"] = [Sem(nc, f"sP0_{j}") for j in range(6)]
    g["stW"] = Stream(nc, "stW", 2)
    g["stW2"] = Stream(nc, "stW2", 2)
    g["stW3"] = Stream(nc, "stW3", 2)
    g["stH"] = Stream(nc, "stH", 2)
    g["stVd"] = Stream(nc, "stVd", 2)
    g["stVg"] = Stream(nc, "stVg", 4)
    g["stQ"] = Stream(nc, "stQ", 2)
    g["stX"] = Stream(nc, "stX", 2)
    g["stO"] = Stream(nc, "stO", 2)
    g["stG"] = Stream(nc, "stG", 2)

    phase_consts(g)
    phase_p0(g)

    cur = g["x0"]
    chain = [("attn_a", g["x1"]), ("ffn1", g["x2"]),
             ("attn_b", g["x3"]), ("ffn2", g["y"])]
    for name, out in chain:
        if name in stages:
            if name == "attn_a":
                phase_attn_a(g, cur, out)
            elif name in ("ffn1", "ffn2"):
                phase_ffn(g, cur, out)
            else:
                phase_attn_b(g, cur, out)
        else:
            phase_copy(g, cur, out)
        if name in taps_d:
            phase_copy(g, out, taps_d[name])
        cur = out

    for name in dumps:
        src = g[name]
        dst = nc.dram_tensor("dump_" + name, list(src.shape), src.dtype,
                             kind="ExternalOutput").ap()
        phase_copy(g, src, dst)

    for s in ("sT", "sT2", "sV", "sS", "sG"):
        assert g[s].n < 2 ** 30, s
    return nc


def phase_consts(g):
    ph = Ph(g)
    sWs = g["sWs"]
    ph.dma("sync", sWs, g["identS"], g["ident"])
    ph.dma("sync", sWs, g["identbS"], g["identb"])
    t = ph.dma("sync", sWs, g["triS"], g["tri"])
    ph.op("vector", lambda e: e.memset(g["ones"], 1.0))
    ph.op("vector", lambda e: e.memset(g["oneb1"], 1.0))
    ph.op("vector", lambda e: e.memset(g["halfb1"], 0.5))
    ph.op("vector", lambda e: e.memset(g["onesb"], 1.0))
    ph.wait("vector", t)
    ph.run()


def phase_copy(g, src, dst):
    ph = Ph(g)
    t = ph.dma("gpsimd", g["sG"], dst, src)
    ph.wait("gpsimd", t)
    ph.run()



def phase_p0(g):
    casts = []
    sems = g["sP0"]

    def add(j, dst, srcv):
        casts.append((j, lambda ph, d=dst, s=srcv, sem=sems[j]:
                      ph.dma("gpsimd", sem, d, s)))

    for c in range(DC):
        add(0, g["w1tb"].rearrange("mc p (c m) -> p mc c m", m=128)[:, :, c, :],
            g["w1T"][c * 128:(c + 1) * 128, :].rearrange("p (mc m) -> p mc m",
                                                         m=128))
        add(0, g["w2tb"].rearrange("mc p (c m) -> p mc c m", m=128)[:, :, c, :],
            g["w2T"][c * 128:(c + 1) * 128, :].rearrange("p (mc m) -> p mc m",
                                                         m=128))
    for j, (wT, dstt) in ((1, (g["h1T"], g["h1tb"])), (2, (g["h2T"], g["h2tb"]))):
        for c in range(HC):
            add(j, dstt.rearrange("mc p (c m) -> p mc c m", m=128)[:, :, c, :],
                wT[c * 128:(c + 1) * 128, :].rearrange("p (mc m) -> p mc m",
                                                       m=128))
    for c in range(HC):
        add(3, g["w3tb"][c], g["w3T"][c * 128:(c + 1) * 128, :])
    for c in range(DC):
        add(4, g["W1b"].rearrange("cc p (c m) -> p cc c m", m=128)[:, :, c, :],
            g["W1"][c * 128:(c + 1) * 128, :].rearrange("p (cc m) -> p cc m",
                                                        m=128))
    g["p0_casts"] = casts
    g["p0_total"] = {j: sum(1 for jj, _ in casts if jj == j) * 16
                     for j in range(6)}


def p0_drain(g, ph, n):
    for _ in range(n):
        if not g["p0_casts"]:
            return
        j, fn = g["p0_casts"].pop(0)
        fn(ph)


def p0_wait(g, ph, eng, jobs):
    for j in jobs:
        assert not any(jj == j for jj, _ in g["p0_casts"]), f"job {j} pending"
        ph.wait(eng, (g["sP0"][j], g["p0_total"][j]))



def emit_ln_tiles(g, ph, xin, tiles, xt, xlnf, xlnT, bufs,
                  extra_v=None, keep_stats=None, dest_base=0, dest_cols=None,
                  pre_evac_wait=None, pre_tr_wait=None, state=None,
                  tr_sink=None):
    sT = g["sT"]
    stX = g["stX"]
    bn, mv, v1, rst = bufs["bn"], bufs["mv"], bufs["v1"], bufs["rst"]
    pstr = bufs["pstr"]
    if state is None:
        state = {"apply": [], "tr": [], "evac": []}
    apply_toks, tr_toks, evac_toks = state["apply"], state["tr"], state["evac"]

    out_evacs = []
    for j, gt in enumerate(tiles):
        k = len(apply_toks)
        if k >= 4:
            ph.wait("sync", apply_toks[k - 4])
        ld = stX.dma(ph, "sync", k, xt[:, k % 4, :],
                     xin[gt * 128:(gt + 1) * 128, :])

        ph.wait("vector", ld)
        ph.op("vector", lambda e, k=k: e.bn_stats(bn[:, 0:6],
                                                  xt[:, k % 4, 0:512]))
        ph.op("vector", lambda e, k=k: e.bn_stats(bn[:, 6:12],
                                                  xt[:, k % 4, 512:1024]))
        ph.op("vector", lambda e: e.bn_aggr(mv, bn))
        st_tok = ph.op("vector", lambda e: e.tensor_scalar(
            v1, mv[:, 1:2], EPS_LN, None, ALU.add))
        ph.wait("scalar", st_tok)
        ph.op("scalar", lambda e: e.activation(rst, v1, AF.Ln))
        rst_tok = ph.op("scalar", lambda e: e.activation(rst, rst, AF.Exp,
                                                         scale=-0.5))
        if keep_stats is not None:
            ph.op("vector", lambda e, gt=gt: e.tensor_copy(
                keep_stats[0][:, gt:gt + 1], mv[:, 0:1]))
        ph.wait("vector", rst_tok)
        if k >= 4:
            assert len(tr_toks) >= k - 3, "tr_sink must be drained promptly"
            ph.wait("vector", tr_toks[k - 4])
        apply_toks.append(ph.op("vector", lambda e, k=k: e.tensor_scalar(
            xlnf[:, k % 4, :], xt[:, k % 4, :],
            mv[:, 0:1], rst, ALU.subtract, ALU.mult)))
        if keep_stats is not None:
            ph.op("vector", lambda e, gt=gt: e.tensor_copy(
                keep_stats[1][:, gt:gt + 1], rst))
        if extra_v is not None:
            extra_v(j, gt)

        def emit_tr(k=k, j=j):
            ph.wait("tensor", apply_toks[k])
            if k >= 1:
                ph.wait("tensor", evac_toks[k - 1])
            if pre_tr_wait is not None:
                pre_tr_wait(ph, j)
            tk = None
            for c in range(DC):
                tk = ph.op("tensor", lambda e, k=k, c=c: e.transpose(
                    pstr[:, c // 4, (c % 4) * 128:(c % 4 + 1) * 128],
                    xlnf[:, k % 4, c * 128:(c + 1) * 128], g["identS"]),
                    sem=sT if c == DC - 1 else None)
            tr_toks.append(tk)
            ph.wait("scalar", tk)
            if pre_evac_wait is not None:
                pre_evac_wait(ph, j)
            ek = None
            col0 = (dest_cols[j] if dest_cols is not None
                    else dest_base + j * 128)
            for h in range(2):
                ek = ph.op("scalar", lambda e, h=h, col=col0: e.activation(
                    xlnT[:, 4 * h:4 * h + 4, col:col + 128],
                    pstr[:, h, :].rearrange("p (c m) -> p c m", m=128),
                    AF.Copy))
            evac_toks.append(ek)
            out_evacs.append(ek)
        if tr_sink is not None:
            tr_sink.append(emit_tr)
        else:
            emit_tr()
    return state, out_evacs



def phase_ffn(g, xin, xout):
    nc = g["nc"]
    sT2, sF4 = g["sT2"], g["sF4"]
    stW, stO = g["stW"], g["stO"]

    from contextlib import ExitStack
    uid = g["uid"] = g.get("uid", 0) + 1
    with ExitStack() as stack:
        def sb(name, shape, dtype):
            return stack.enter_context(
                nc.sbuf_tensor(f"{name}_{uid}", shape, dtype)).ap()

        def psb(name, shape, dtype=F32):
            return stack.enter_context(
                nc.psum_tensor(f"{name}_{uid}", shape, dtype)).ap()

        xt = sb("f_xt", [128, 4, D], F32)
        xlnf = sb("f_xlnf", [128, 4, D], F32)
        xlnT = sb("f_xlnT", [128, DC, 2, 512], BF16)
        bufs = dict(bn=sb("f_bn", [128, 12], F32), mv=sb("f_mv", [128, 2], F32),
                    v1=sb("f_v1", [128, 1], F32), rst=sb("f_rst", [128, 1], F32),
                    pstr=psb("f_pstr", [128, 2, 512]))
        hT = sb("f_hT", [128, HC, 512], BF16)
        gT = sb("f_gT", [128, HC, 512], BF16)
        w12 = sb("f_w12", [128, 2, 2, 1024], BF16)
        hw = sb("f_hw", [128, 2, 2, HC * 128], BF16)
        silu = sb("f_silu", [128, 2, 512], BF16)
        w2c = sb("f_w2c", [128, 2, 512], BF16)
        w3sb = sb("f_w3", [128, 2, 2, D], BF16)
        outsb = sb("f_out", [128, 2, D], F32)
        f4x = sb("f_f4x", [128, 4, D], F32)
        pw1 = psb("f_pw1", [128, 2, 512])
        pw2 = psb("f_pw2", [128, 2, 512])
        pg = psb("f_pg", [128, 2, 512])
        ssb = sb("f_ssb", [128, 4, HC], F32)
        sw3row = sb("f_sw3row", [1, D], F32)
        sw3rep = sb("f_sw3rep", [128, D], F32)
        sevac = sb("f_sevac", [128, 2, 512], F32)

        ph = Ph(g)
        p0_drain(g, ph, 10 ** 9)
        p0_wait(g, ph, "sync", [0, 1, 2, 3])
        for idx, nm in enumerate(("s1T", "s2T", "s3T", "s4T")):
            t0 = ph.dma("sync", g["sWs"], ssb[:, idx, :], g[nm])
            ph.wait("sync", t0)
        s_tok = t0
        t0 = ph.dma("sync", g["sWs"], sw3row, g["sw3r"])
        ph.wait("sync", t0)
        ph.wait("tensor", t0)
        rep = None
        for hf in range(2):
            mmt = ph.op("tensor", lambda e, hf=hf: e.matmul(
                pw1[:, hf, :], g["ones"][0:1, 0:128],
                sw3row[0:1, hf * 512:(hf + 1) * 512], start=True, stop=True),
                sem=g["sT"])
            ph.wait("vector", mmt)
            rep = ph.op("vector", lambda e, hf=hf: e.tensor_copy(
                sw3rep[:, hf * 512:(hf + 1) * 512], pw1[:, hf, :]))
        sw3_tok = rep
        ln_state = None
        ln_evacs = {}
        f1_mm = {}
        f4_mm = {}
        f4_add = {}
        sto = {}

        def emit_ln(ch, sink=None):
            def pev(ph_, j, ch=ch):
                if j == 0 and ch >= 2:
                    ph_.wait("scalar", f1_mm[ch - 2])
            def ptw(ph_, j, ch=ch):
                if j == 0 and ch >= 1 and (ch - 1) in f4_add:
                    ph_.wait("tensor", f4_add[ch - 1])
            nonlocal ln_state
            tiles = list(range(ch * 4, ch * 4 + 4))
            dest = [t * 128 for t in range(4)]
            ln_state, ev = emit_ln_tiles(
                g, ph, xin, tiles, xt, xlnf,
                xlnT.rearrange("p c s w -> p c (s w)"), bufs,
                dest_cols=[(ch % 2) * 512 + dd for dd in dest],
                pre_evac_wait=pev, pre_tr_wait=ptw, state=ln_state,
                tr_sink=sink)
            ln_evacs[ch] = ev

        emit_ln(0)
        prev_g2_mm = None
        prev_sevac_s = None
        for ch in range(8):
            sl = ch % 2
            ev_tok = {}
            mul_tok = {}
            mmt_hist = {}
            sil_hist = {}
            sink = []
            if ch < 7:
                emit_ln(ch + 1, sink=sink)
            for mc in range(HC):
                if mc >= 12 and (mc - 12) % 4 == 0 and sink:
                    sink.pop(0)()
                if mc >= 2:
                    ph.wait("sync", mmt_hist[mc - 2])
                wtok = stW.dma(ph, "sync", mc, w12[:, mc % 2],
                               g["w12tb"][mc])
                ph.wait("tensor", wtok)
                if mc == 0:
                    ph.wait("tensor", ln_evacs[ch][-1])
                    if ch >= 1:
                        ph.wait("tensor", f4_add[ch - 1])
                    elif ch == 0:
                        ph.wait("tensor", sw3_tok)
                if mc >= 2:
                    ph.wait("tensor", ev_tok[mc - 2])
                for c in range(DC):
                    ph.op("tensor", lambda e, mc=mc, c=c, sl=sl: e.matmul(
                        pw1[:, mc % 2, :],
                        w12[:, mc % 2, 0, c * 128:(c + 1) * 128],
                        xlnT[:, c, sl, :], start=(c == 0), stop=(c == DC - 1)))
                for c in range(DC):
                    mmt_hist[mc] = ph.op("tensor", lambda e, mc=mc, c=c, sl=sl:
                                         e.matmul(
                                             pw2[:, mc % 2, :],
                                             w12[:, mc % 2, 1,
                                                 c * 128:(c + 1) * 128],
                                             xlnT[:, c, sl, :],
                                             start=(c == 0),
                                             stop=(c == DC - 1)),
                                         sem=sT2 if c == DC - 1 else None)
                ph.wait("vector", mmt_hist[mc])
                if mc >= 2:
                    ph.wait("vector", mul_tok[mc - 2])
                if mc == 0:
                    if ch >= 1:
                        ph.wait("vector", f4_mm[ch - 1])
                    if ch == 0:
                        ph.wait("vector", s_tok)
                if mc >= 2:
                    ph.wait("vector", sil_hist[mc - 2])
                elif prev_sevac_s is not None:
                    ph.wait("vector", prev_sevac_s)
                sv1 = ph.op("vector", lambda e, mc=mc: e.tensor_scalar(
                    sevac[:, mc % 2, :], pw1[:, mc % 2, :],
                    ssb[:, 0, mc:mc + 1], None, ALU.mult))
                ev_tok[mc] = ph.op("vector", lambda e, mc=mc: e.tensor_scalar(
                    w2c[:, mc % 2, :], pw2[:, mc % 2, :],
                    ssb[:, 1, mc:mc + 1], None, ALU.mult))
                ph.wait("scalar", ev_tok[mc])
                sil = sil_hist[mc] = ph.op("scalar", lambda e, mc=mc: e.activation(
                    silu[:, mc % 2, :], sevac[:, mc % 2, :], AF.Silu))
                ph.wait("vector", sil)
                mul_tok[mc] = ph.op("vector", lambda e, mc=mc: e.tensor_mul(
                    hT[:, mc, :], silu[:, mc % 2, :], w2c[:, mc % 2, :]))
            while sink:
                sink.pop(0)()
            f1_mm[ch] = mmt_hist[HC - 1]

            def gelu_layer(srcb, dst, wtb, ready_tok, pg_free_tok,
                           hw_free_tok, s_idx, sevac_free):
                mm_tok = {}
                gel_tok = {}
                for mc in range(HC):
                    pair = mc // 2
                    if mc % 2 == 0:
                        if pair >= 2:
                            ph.wait("sync", mm_tok[mc - 3])
                        elif hw_free_tok is not None:
                            ph.wait("sync", hw_free_tok)
                        wtok = stW.dma(ph, "sync", pair, hw[:, pair % 2],
                                       wtb[mc:mc + 2].rearrange(
                                           "k p d -> p k d"))
                        ph.wait("tensor", wtok)
                    if mc == 0:
                        ph.wait("tensor", ready_tok)
                        if pg_free_tok is not None:
                            ph.wait("tensor", pg_free_tok)
                    if mc >= 2:
                        ph.wait("tensor", gel_tok[mc - 2])
                    for c in range(HC):
                        mm_tok[mc] = ph.op("tensor", lambda e, mc=mc, c=c:
                                           e.matmul(
                                               pg[:, mc % 2, :],
                                               hw[:, (mc // 2) % 2, mc % 2,
                                                  c * 128:(c + 1) * 128],
                                               srcb[:, c, :], start=(c == 0),
                                               stop=(c == HC - 1)),
                                           sem=sT2 if c == HC - 1 else None)
                    ph.wait("vector", mm_tok[mc])
                    if mc >= 2:
                        ph.wait("vector", gel_tok[mc - 2])
                    elif sevac_free is not None:
                        ph.wait("vector", sevac_free)
                    sv = ph.op("vector", lambda e, mc=mc: e.tensor_scalar(
                        sevac[:, mc % 2, :], pg[:, mc % 2, :],
                        ssb[:, s_idx, mc:mc + 1], None, ALU.mult))
                    ph.wait("scalar", sv)
                    gel_tok[mc] = ph.op("scalar", lambda e, mc=mc: e.activation(
                        dst[:, mc, :], sevac[:, mc % 2, :], AF.Gelu))
                return gel_tok[HC - 1], mm_tok[HC - 1]

            pg_free = f4_add.get(ch - 1)
            g1_tok, g1_mm = gelu_layer(hT, gT, g["h1tb"], mul_tok[HC - 1],
                                       pg_free, prev_g2_mm, 2,
                                       sil_hist[HC - 1])
            g2_tok, g2_mm = gelu_layer(gT, hT, g["h2tb"], g1_tok, None,
                                       g1_mm, 3, g1_tok)
            prev_sevac_s = g2_tok
            prev_g2_mm = g2_mm

            if ch >= 1:
                ph.wait("scalar", f4_add[ch - 1])
            f4toks = []
            for t in range(4):
                gt = ch * 4 + t
                f4toks.append(ph.dma("scalar", sF4, f4x[:, t, :],
                                     xin[gt * 128:(gt + 1) * 128, :]))
            mm_tok = {}
            for kc in range(HC):
                pair = kc // 2
                if kc % 2 == 0:
                    if pair >= 2:
                        ph.wait("sync", mm_tok[kc - 3])
                    wtok = stW.dma(ph, "sync", pair, w3sb[:, pair % 2],
                                   g["w3tb"][kc:kc + 2].rearrange(
                                       "k p d -> p k d"))
                    ph.wait("tensor", wtok)
                if kc == 0:
                    ph.wait("tensor", g2_tok)
                    if ch < 7:
                        ph.wait("tensor", ln_evacs[ch + 1][-1])
                    if ch >= 1:
                        ph.wait("tensor", f4_add[ch - 1])
                for t in range(4):
                    for hf in range(2):
                        last = (t == 3 and hf == 1)
                        mm_tok[kc] = ph.op("tensor",
                                           lambda e, kc=kc, t=t, hf=hf: e.matmul(
                                               _po(pw1, pw2, pg, bufs["pstr"], t)[
                                                   :, hf * 512:(hf + 1) * 512],
                                               hT[:, kc, t * 128:(t + 1) * 128],
                                               w3sb[:, (kc // 2) % 2, kc % 2,
                                                    hf * 512:(hf + 1) * 512],
                                               start=(kc == 0),
                                               stop=(kc == HC - 1)),
                                           sem=sT2 if last else None)
            f4_mm[ch] = mm_tok[HC - 1]
            for t in range(4):
                gt = ch * 4 + t
                ph.wait("vector", mm_tok[HC - 1])
                ph.wait("vector", f4toks[3])
                if t >= 2:
                    ph.wait("vector", sto[(ch, t - 2)])
                elif ch >= 1:
                    ph.wait("vector", sto[(ch - 1, t + 2)])
                if t == 0 and ch == 0:
                    ph.wait("vector", sw3_tok)
                ph.op("vector", lambda e, t=t: e.tensor_mul(
                    outsb[:, t % 2, :], _po(pw1, pw2, pg, bufs["pstr"], t),
                    sw3rep))
                add_tok = ph.op("vector", lambda e, t=t: e.tensor_add(
                    outsb[:, t % 2, :], outsb[:, t % 2, :], f4x[:, t, :]))
                ph.wait("scalar", add_tok)
                sto[(ch, t)] = stO.dma(ph, "scalar", t,
                                       xout[gt * 128:(gt + 1) * 128, :],
                                       outsb[:, t % 2, :])
                f4_add[ch] = add_tok
        stO.wait_all(ph, "scalar")
        ph.run()


def _po(pw1, pw2, pg, pstr, t):
    return [pw1, pw2, pg, pstr][t].rearrange("p a b -> p (a b)")


def phase_attn_a(g, xin, xout):
    nc = g["nc"]
    from contextlib import ExitStack
    with ExitStack() as outer:
        def osb(name, shape, dtype):
            return outer.enter_context(nc.sbuf_tensor(name, shape, dtype)).ap()

        QF = osb("a_QF", [128, S], F32)
        KBf = osb("a_KB", [128, S], F32)
        mcol = osb("a_mcol", [128, NT], F32)
        rcol = osb("a_rcol", [128, NT], F32)
        qptok = osb("a_qptok", [128, 2 * NT], F32)
        thr_rep = osb("a_thr", [128, 1], F32)
        WfqXr = osb("a_wfqxr", [128, D], F32)
        WfqYr = osb("a_wfqyr", [128, D], F32)
        brep = osb("a_brep", [128, D], F32)

        _attn_a_ph1(g, xin, QF, KBf, mcol, rcol, qptok, thr_rep,
                    WfqXr, WfqYr, brep)
        _attn_a_ph3(g, xin, xout, QF, KBf, mcol, rcol, qptok, thr_rep,
                    WfqXr, WfqYr, brep)


def _attn_a_ph1(g, xin, QF, KBf, mcol, rcol, qptok, thr_rep,
                WfqXr, WfqYr, brep):
    nc = g["nc"]
    sT, sT2, sG, sWs = g["sT"], g["sT2"], g["sG"], g["sWs"]
    stG, stH, stVd = g["stG"], g["stH"], g["stVd"]
    from contextlib import ExitStack
    with ExitStack() as stack:
        def sb(name, shape, dtype):
            return stack.enter_context(nc.sbuf_tensor(name, shape, dtype)).ap()

        def psb(name, shape, dtype=F32):
            return stack.enter_context(nc.psum_tensor(name, shape, dtype)).ap()

        xt = sb("a_xt", [128, 4, D], F32)
        xlnf = sb("a_xlnf", [128, 4, D], F32)
        xlnTr = sb("a_xlnTr", [128, DC, 1024], BF16)
        bufs = dict(bn=sb("a_bn", [128, 12], F32), mv=sb("a_mv", [128, 2], F32),
                    v1=sb("a_v1", [128, 1], F32), rst=sb("a_rst", [128, 1], F32),
                    pstr=psb("a_pstr", [128, 2, 512]))
        WqTb = sb("a_WqTb", [128, DC, D], BF16)
        WkTb = sb("a_WkTb", [128, DC, D], BF16)
        WvTb = sb("a_WvTb", [128, DC, D], BF16)
        bqsb = sb("a_bq", [128, DC], F32)
        bksb = sb("a_bk", [128, DC], F32)
        bvrow = sb("a_bvrow", [1, D], F32)
        bvb = sb("a_bvb", [1, D], BF16)
        WtqXr = sb("a_wtqxr", [128, D], F32)
        WtqYr = sb("a_wtqyr", [128, D], F32)
        btqr = sb("a_btqr", [128, 2], F32)
        tmp1k = sb("a_tmp1k", [128, D], F32)
        fall = sb("a_fall", [128, NT, 16], F32)
        fstage = sb("a_fstage", [16, S], F32)
        wrow = fstage[0:1, 0:D]
        red = sb("a_red", [128, 16], F32)
        sc = sb("a_sc", [1, 32], F32)
        kstage = sb("a_kst", [128, 1, DC, 512], BF16)
        qstage = sb("a_qst", [128, 1, DC, 512], BF16)
        vstage = sb("a_vst", [128, 1, 4, D], BF16)
        pfa = psb("a_pfa", [128, 512])
        psS = psb("a_psS", [128, 512])
        pp = psb("a_pp", [128, 2, 512])

        ph = Ph(g)
        for W, Wb in ((g["WqT"], WqTb), (g["WkT"], WkTb), (g["WvT"], WvTb)):
            ph.dma("gpsimd", sG, Wb, W.rearrange("(c p) o -> p c o", p=128))
        wb_tok = sG.tok
        t0 = ph.dma("sync", sWs, bqsb, g["bqT"]); ph.wait("sync", t0)
        t0 = ph.dma("sync", sWs, bksb, g["bkT"]); ph.wait("sync", t0)
        bv_tok = ph.dma("sync", sWs, bvrow, g["bv"]); ph.wait("sync", bv_tok)
        ph.wait("vector", bv_tok)
        bvb_tok = ph.op("vector", lambda e: e.tensor_copy(bvb, bvrow))
        last_rep = None
        for name in ("x", "y"):
            row = g["WtqX"] if name == "x" else g["WtqY"]
            t0 = ph.dma("sync", sWs, wrow, row)
            ph.wait("sync", t0)
            dstr = WtqXr if name == "x" else WtqYr
            mmt = None
            for hf in range(2):
                ph.wait("tensor", t0)
                if last_rep is not None:
                    ph.wait("tensor", last_rep)
                mmt = ph.op("tensor", lambda e, hf=hf: e.matmul(
                    pp[:, hf, :], g["ones"][0:1, 0:128],
                    wrow[0:1, hf * 512:(hf + 1) * 512], start=True, stop=True),
                    sem=sT)
                ph.wait("vector", mmt)
                last_rep = ph.op(
                    "vector", lambda e, hf=hf, dstr=dstr: e.tensor_copy(
                        dstr[:, hf * 512:(hf + 1) * 512], pp[:, hf, :]))
            ph.wait("sync", mmt)
        t0 = ph.dma("sync", sWs, wrow[0:1, 0:2], g["btq"])
        ph.wait("tensor", t0)
        ph.wait("tensor", last_rep)
        mmt = ph.op("tensor", lambda e: e.matmul(
            pp[:, 0, 0:2], g["ones"][0:1, 0:128], wrow[0:1, 0:2],
            start=True, stop=True), sem=sT)
        ph.wait("vector", mmt)
        btq_tok = ph.op("vector", lambda e: e.tensor_copy(btqr, pp[:, 0, 0:2]))
        ph.op("vector", lambda e: e.memset(fall[:, :, 9:10], 1.0))
        last_rep = btq_tok
        for name in ("fx", "fy", "bo", "bfq"):
            row = {"fx": g["WfqX"], "fy": g["WfqY"], "bo": g["bo"],
                   "bfq": g["bfq"]}[name]
            t0 = ph.dma("sync", sWs, wrow, row)
            ph.wait("sync", t0)
            mmt = None
            for hf in range(2):
                ph.wait("tensor", t0)
                ph.wait("tensor", last_rep)
                mmt = ph.op("tensor", lambda e, hf=hf: e.matmul(
                    pp[:, hf, :], g["ones"][0:1, 0:128],
                    wrow[0:1, hf * 512:(hf + 1) * 512], start=True,
                    stop=True), sem=sT)
                ph.wait("vector", mmt)
                if name in ("fx", "fy"):
                    dstr = WfqXr if name == "fx" else WfqYr
                    last_rep = ph.op(
                        "vector", lambda e, hf=hf, dstr=dstr: e.tensor_copy(
                            dstr[:, hf * 512:(hf + 1) * 512], pp[:, hf, :]))
                elif name == "bo":
                    last_rep = ph.op(
                        "vector", lambda e, hf=hf: e.tensor_copy(
                            brep[:, hf * 512:(hf + 1) * 512], pp[:, hf, :]))
                else:
                    ph.op("vector", lambda e, hf=hf: e.tensor_scalar(
                        tmp1k[:, hf * 512:(hf + 1) * 512], pp[:, hf, :],
                        0.1, None, ALU.mult))
                    last_rep = ph.op(
                        "vector", lambda e, hf=hf: e.tensor_add(
                            brep[:, hf * 512:(hf + 1) * 512],
                            brep[:, hf * 512:(hf + 1) * 512],
                            tmp1k[:, hf * 512:(hf + 1) * 512]))
            ph.wait("sync", mmt)
        rep_done = last_rep
        wrow_last_mm = mmt

        feat_toks = {}

        def extra_v(j, gt):
            ft = fall[:, gt, :]
            ph.op("vector", lambda e, gt=gt: e.tensor_mul(
                tmp1k, xlnf[:, gt % 4, :], WtqXr))
            ph.op("vector", lambda e, ft=ft: e.reduce_sum(ft[:, 10:11], tmp1k,
                                                          axis=AX.X))
            ph.op("vector", lambda e, ft=ft: e.tensor_scalar(
                ft[:, 0:1], ft[:, 10:11], btqr[:, 0:1], None, ALU.add))
            ph.op("vector", lambda e, gt=gt: e.tensor_mul(
                tmp1k, xlnf[:, gt % 4, :], WtqYr))
            ph.op("vector", lambda e, ft=ft: e.reduce_sum(ft[:, 10:11], tmp1k,
                                                          axis=AX.X))
            ph.op("vector", lambda e, ft=ft: e.tensor_scalar(
                ft[:, 1:2], ft[:, 10:11], btqr[:, 1:2], None, ALU.add))
            ph.op("vector", lambda e, ft=ft: e.tensor_mul(
                ft[:, 10:11], ft[:, 0:1], ft[:, 0:1]))
            ph.op("vector", lambda e, ft=ft: e.tensor_mul(
                ft[:, 11:12], ft[:, 1:2], ft[:, 1:2]))
            ph.op("vector", lambda e, ft=ft: e.tensor_add(
                ft[:, 2:3], ft[:, 10:11], ft[:, 11:12]))
            ph.op("vector", lambda e, ft=ft: e.tensor_scalar(
                ft[:, 3:4], ft[:, 0:1], -2.0, None, ALU.mult))
            ph.op("vector", lambda e, ft=ft: e.tensor_scalar(
                ft[:, 4:5], ft[:, 1:2], -2.0, None, ALU.mult))
            ph.op("vector", lambda e, ft=ft: e.scalar_tensor_tensor(
                ft[:, 5:6], ft[:, 0:1], -T_WIN, ft[:, 1:2],
                ALU.mult, ALU.subtract))
            ph.op("vector", lambda e, ft=ft: e.scalar_tensor_tensor(
                ft[:, 6:7], ft[:, 0:1], -T_WIN, ft[:, 1:2],
                ALU.mult, ALU.add))
            ph.op("vector", lambda e, ft=ft: e.tensor_scalar(
                ft[:, 7:8], ft[:, 5:6], -1.0, None, ALU.mult))
            fv = ph.op("vector", lambda e, ft=ft: e.tensor_scalar(
                ft[:, 8:9], ft[:, 6:7], -1.0, None, ALU.mult))
            ph.op("vector", lambda e, gt=gt, ft=ft: e.tensor_copy(
                qptok[:, 2 * gt:2 * gt + 2], ft[:, 0:2]))
            ph.op("vector", lambda e, ft=ft: e.tensor_mul(
                ft[:, 12:13], ft[:, 0:1], ft[:, 1:2]))
            ph.op("vector", lambda e, ft=ft: e.tensor_mul(
                ft[:, 13:14], ft[:, 2:3], ft[:, 0:1]))
            ph.op("vector", lambda e, ft=ft: e.tensor_mul(
                ft[:, 14:15], ft[:, 2:3], ft[:, 1:2]))
            fv = ph.op("vector", lambda e, ft=ft: e.tensor_mul(
                ft[:, 15:16], ft[:, 2:3], ft[:, 2:3]))
            feat_toks[gt] = fv

        ev_all = {}
        nproj = 24
        proj_tok = {}
        kst_tok = {}
        qst_tok = {}
        vst_tok = {}
        ln_state = None

        def pre_evac_wait(ph_, j):
            pass

        for b in range(8):
            sl = b % 2
            tiles = list(range(4 * b, 4 * b + 4))
            dest_cols = [sl * 512 + t * 128 for t in range(4)]
            def pev(ph_, j, b=b):
                if j == 0 and b >= 2:
                    ph_.wait("scalar", proj_tok[b - 2])
            ln_state, ln_evacs = emit_ln_tiles(
                g, ph, xin, tiles, xt, xlnf, xlnTr, bufs, extra_v=extra_v,
                keep_stats=(mcol, rcol), dest_cols=dest_cols,
                pre_evac_wait=pev, state=ln_state)

            ph.wait("tensor", ln_evacs[3])
            if b == 0:
                ph.wait("tensor", wb_tok)
                ph.wait("tensor", bvb_tok)
                ph.wait("tensor", rep_done)
            pidx = 0
            for which, Wb, stg, bias in (("k", WkTb, kstage, bksb),
                                         ("q", WqTb, qstage, bqsb)):
                for mc in range(DC):
                    bank = pidx % 2
                    if pidx >= 2:
                        ph.wait("tensor", ev_all[(b, pidx - 2)])
                    elif b >= 1:
                        ph.wait("tensor", ev_all[(b - 1, nproj - 2 + pidx)])
                    tk = None
                    for c in range(DC):
                        tk = ph.op("tensor", lambda e, sl=sl, mc=mc, c=c,
                                   bank=bank, Wb=Wb: e.matmul(
                                       pp[:, bank, :],
                                       Wb[:, c, mc * 128:(mc + 1) * 128],
                                       xlnTr[:, c, sl * 512:(sl + 1) * 512],
                                       start=(c == 0), stop=(c == DC - 1)),
                                   sem=sT2 if c == DC - 1 else None)
                    ph.wait("scalar", tk)
                    if b >= 1 and mc == 0:
                        ph.wait("scalar", (kst_tok if which == "k"
                                           else qst_tok)[b - 1])
                    ev_all[(b, pidx)] = ph.op(
                        "scalar", lambda e, mc=mc, bank=bank, stg=stg,
                        bias=bias: e.activation(
                            stg[:, 0, mc, :], pp[:, bank, :], AF.Identity,
                            bias=bias[:, mc:mc + 1]))
                    pidx += 1
            for t in range(4):
                for hf in range(2):
                    bank = pidx % 2
                    if pidx >= 2:
                        ph.wait("tensor", ev_all[(b, pidx - 2)])
                    for c in range(DC):
                        ph.op("tensor", lambda e, sl=sl, t=t, c=c, hf=hf,
                              bank=bank: e.matmul(
                                  pp[:, bank, :],
                                  xlnTr[:, c, sl * 512 + t * 128:
                                        sl * 512 + (t + 1) * 128],
                                  WvTb[:, c, hf * 512:(hf + 1) * 512],
                                  start=(c == 0), stop=False))
                    tk = ph.op("tensor", lambda e, bank=bank, hf=hf: e.matmul(
                        pp[:, bank, :], g["onesb"][0:1, 0:128],
                        bvb[0:1, hf * 512:(hf + 1) * 512],
                        start=False, stop=True), sem=sT2)
                    ph.wait("scalar", tk)
                    if b >= 1 and t == 0 and hf == 0:
                        ph.wait("scalar", vst_tok[b - 1])
                    ev_all[(b, pidx)] = ph.op(
                        "scalar", lambda e, t=t, hf=hf, bank=bank:
                        e.activation(vstage[:, 0, t, hf * 512:(hf + 1) * 512],
                                     pp[:, bank, :], AF.Copy))
                    pidx += 1
            proj_tok[b] = ev_all[(b, pidx - 1)]
            for stg, st, toks, dst in (
                    (kstage, stG, kst_tok,
                     g["kTd"].rearrange("c p t -> p c t")[
                         :, :, b * 512:(b + 1) * 512]),
                    (qstage, stH, qst_tok,
                     g["qTd"].rearrange("c p t -> p c t")[
                         :, :, b * 512:(b + 1) * 512]),
                    (vstage, stVd, vst_tok,
                     g["vd"][4 * b:4 * b + 4].rearrange("t p d -> p t d"))):
                ph.wait("gpsimd", proj_tok[b])
                toks[b] = st.dma(ph, "gpsimd", b, dst, stg[:, 0])
            p0_drain(g, ph, 3)

        ph.wait("tensor", feat_toks[NT - 1])
        ph.wait("vector", wrow_last_mm)
        fev = None
        for gt in range(NT):
            if gt >= 1:
                ph.wait("tensor", fev)
            trt = ph.op("tensor", lambda e, gt=gt: e.transpose(
                pfa[0:16, (gt % 4) * 128:(gt % 4 + 1) * 128],
                fall[:, gt, :], g["identS"]), sem=sT)
            ph.wait("vector", trt)
            fev = ph.op("vector", lambda e, gt=gt: e.tensor_copy(
                fstage[0:10, gt * 128:(gt + 1) * 128],
                pfa[0:10, (gt % 4) * 128:(gt % 4 + 1) * 128]))

        NN = float(S)
        N2 = NN * NN
        rd_tok = ph.op("vector", lambda e: e.reduce_sum(
            red, fall.rearrange("p t f -> p f t"), axis=AX.X))
        ph.wait("tensor", rd_tok)
        smt = ph.op("tensor", lambda e: e.matmul(
            psS[0:1, 0:16], g["oneb1"], red, start=True, stop=True), sem=sT)
        ph.wait("vector", smt)
        ph.op("vector", lambda e: e.tensor_copy(sc[0:1, 0:16], psS[0:1, 0:16]))
        V = lambda fn: ph.op("vector", fn)
        V(lambda e: e.tensor_mul(sc[:, 16:17], sc[:, 0:1], sc[:, 0:1]))
        V(lambda e: e.tensor_mul(sc[:, 17:18], sc[:, 1:2], sc[:, 1:2]))
        V(lambda e: e.tensor_add(sc[:, 16:17], sc[:, 16:17], sc[:, 17:18]))
        V(lambda e: e.tensor_scalar(sc[:, 17:18], sc[:, 16:17], 2.0, None,
                                    ALU.mult))
        V(lambda e: e.scalar_tensor_tensor(sc[:, 18:19], sc[:, 2:3], 2.0 * NN,
                                           sc[:, 17:18], ALU.mult,
                                           ALU.subtract))
        V(lambda e: e.tensor_mul(sc[:, 19:20], sc[:, 10:11], sc[:, 10:11]))
        V(lambda e: e.tensor_mul(sc[:, 20:21], sc[:, 12:13], sc[:, 12:13]))
        V(lambda e: e.tensor_mul(sc[:, 21:22], sc[:, 11:12], sc[:, 11:12]))
        V(lambda e: e.scalar_tensor_tensor(sc[:, 20:21], sc[:, 20:21], 2.0,
                                           sc[:, 19:20], ALU.mult, ALU.add))
        V(lambda e: e.tensor_add(sc[:, 20:21], sc[:, 20:21], sc[:, 21:22]))
        V(lambda e: e.tensor_mul(sc[:, 22:23], sc[:, 13:14], sc[:, 0:1]))
        V(lambda e: e.tensor_mul(sc[:, 23:24], sc[:, 14:15], sc[:, 1:2]))
        V(lambda e: e.tensor_add(sc[:, 22:23], sc[:, 22:23], sc[:, 23:24]))
        V(lambda e: e.tensor_mul(sc[:, 24:25], sc[:, 2:3], sc[:, 2:3]))
        V(lambda e: e.tensor_scalar(sc[:, 24:25], sc[:, 24:25], 2.0, None,
                                    ALU.mult))
        V(lambda e: e.scalar_tensor_tensor(sc[:, 25:26], sc[:, 15:16],
                                           2.0 * NN, sc[:, 24:25], ALU.mult,
                                           ALU.add))
        V(lambda e: e.tensor_scalar(sc[:, 22:23], sc[:, 22:23], 8.0, None,
                                    ALU.mult))
        V(lambda e: e.tensor_sub(sc[:, 25:26], sc[:, 25:26], sc[:, 22:23]))
        V(lambda e: e.tensor_scalar(sc[:, 20:21], sc[:, 20:21], 4.0, None,
                                    ALU.mult))
        V(lambda e: e.tensor_add(sc[:, 25:26], sc[:, 25:26], sc[:, 20:21]))
        V(lambda e: e.tensor_scalar(sc[:, 26:27], sc[:, 18:19], 1.0 / N2,
                                    None, ALU.mult))
        V(lambda e: e.tensor_mul(sc[:, 27:28], sc[:, 18:19], sc[:, 18:19]))
        V(lambda e: e.tensor_scalar(sc[:, 27:28], sc[:, 27:28], 1.0 / N2,
                                    None, ALU.mult))
        V(lambda e: e.tensor_sub(sc[:, 28:29], sc[:, 25:26], sc[:, 27:28]))
        vv = V(lambda e: e.tensor_scalar(sc[:, 28:29], sc[:, 28:29],
                                         1.0 / (N2 - 1.0), None, ALU.mult))
        ph.wait("scalar", vv)
        ph.op("scalar", lambda e: e.activation(sc[:, 29:30], sc[:, 28:29],
                                               AF.Ln))
        sqt = ph.op("scalar", lambda e: e.activation(sc[:, 29:30],
                                                     sc[:, 29:30], AF.Exp,
                                                     scale=0.5))
        ph.wait("vector", sqt)
        tht = V(lambda e: e.scalar_tensor_tensor(
            sc[:, 30:31], sc[:, 29:30], 1.25, sc[:, 26:27], ALU.mult,
            ALU.add))
        ph.wait("tensor", tht)
        ph.wait("tensor", smt)
        rmt = ph.op("tensor", lambda e: e.matmul(
            psS[:, 32:33], g["ones"][0:1, 0:128], sc[0:1, 30:31],
            start=True, stop=True), sem=sT)
        ph.wait("vector", rmt)
        ph.op("vector", lambda e: e.tensor_copy(thr_rep, psS[:, 32:33]))

        ph.wait("sync", fev)
        asg = [(QF, 0, 9), (QF, 1, 3), (QF, 2, 4), (QF, 3, 2),
               (QF, 32, 9), (QF, 33, 5), (QF, 64, 9), (QF, 65, 6),
               (KBf, 0, 2), (KBf, 1, 0), (KBf, 2, 1), (KBf, 3, 9),
               (KBf, 32, 7), (KBf, 33, 9), (KBf, 64, 8), (KBf, 65, 9)]
        for dstt, drow, srow_i in asg:
            t0 = ph.dma("sync", sWs, dstt[drow:drow + 1, :],
                        fstage[srow_i:srow_i + 1, :])
            ph.wait("sync", t0)
        stG.wait_all(ph, "gpsimd")
        stH.wait_all(ph, "gpsimd")
        stVd.wait_all(ph, "gpsimd")
        ph.run()


def _attn_a_ph2(g, QF, KBf, thr_rep):
    nc = g["nc"]
    sT, sT2 = g["sT"], g["sT2"]
    NPAIR = float(S) * float(S)
    from contextlib import ExitStack
    with ExitStack() as stack:
        def sb(name, shape, dtype):
            return stack.enter_context(nc.sbuf_tensor(name, shape, dtype)).ap()

        sumP = sb("s_sum", [128, 256], F32)
        sqP = sb("s_sq", [128, 256], F32)
        sqsc = sb("s_sqsc", [128, 512], F32)
        tot = sb("s_tot", [128, 4], F32)
        sc = sb("s_sc", [1, 8], F32)
        pp = stack.enter_context(nc.psum_tensor("s_pp", [128, 4, 512], F32)).ap()
        pt = stack.enter_context(nc.psum_tensor("s_pt", [128, 512], F32)).ap()

        ph = Ph(g)
        vred = {}
        ssq = {}
        for qt in range(NT):
            p0_drain(g, ph, 1)
            for jb in range(8):
                idx = qt * 8 + jb
                bank = idx % 4
                if idx >= 4:
                    ph.wait("tensor", ssq[idx - 4])
                mmt = ph.op("tensor", lambda e, qt=qt, jb=jb, bank=bank:
                            e.matmul(pp[:, bank, :], QF[0:4,
                                     qt * 128:(qt + 1) * 128],
                                     KBf[0:4, jb * 512:(jb + 1) * 512],
                                     start=True, stop=True), sem=sT)
                ph.wait("vector", mmt)
                vred[idx] = ph.op("vector", lambda e, idx=idx, bank=bank:
                                  e.reduce_sum(sumP[:, idx:idx + 1],
                                               pp[:, bank, :], axis=AX.X))
                ph.wait("scalar", vred[idx])
                ssq[idx] = ph.op("scalar", lambda e, idx=idx, bank=bank:
                                 e.activation(sqsc, pp[:, bank, :], AF.Square,
                                              accum_out=sqP[:, idx:idx + 1]))
        ph.wait("vector", ssq[255])
        v1 = ph.op("vector", lambda e: e.reduce_sum(tot[:, 0:1], sumP,
                                                    axis=AX.X))
        v2 = ph.op("vector", lambda e: e.reduce_sum(tot[:, 1:2], sqP,
                                                    axis=AX.X))
        ph.wait("tensor", v2)
        mt = ph.op("tensor", lambda e: e.matmul(
            pt[0:1, 0:2], g["oneb1"], tot[:, 0:2], start=True, stop=True),
            sem=sT)
        ph.wait("vector", mt)
        ph.op("vector", lambda e: e.tensor_copy(sc[0:1, 4:6], pt[0:1, 0:2]))
        ph.op("vector", lambda e: e.tensor_scalar(
            sc[0:1, 0:1], sc[0:1, 4:5], 1.0 / NPAIR, None, ALU.mult))
        ph.op("vector", lambda e: e.tensor_mul(sc[0:1, 1:2], sc[0:1, 4:5],
                                               sc[0:1, 4:5]))
        ph.op("vector", lambda e: e.tensor_scalar(
            sc[0:1, 1:2], sc[0:1, 1:2], 1.0 / NPAIR, None, ALU.mult))
        ph.op("vector", lambda e: e.tensor_sub(sc[0:1, 1:2], sc[0:1, 5:6],
                                               sc[0:1, 1:2]))
        vvar = ph.op("vector", lambda e: e.tensor_scalar(
            sc[0:1, 1:2], sc[0:1, 1:2], 1.0 / (NPAIR - 1.0), None, ALU.mult))
        ph.wait("scalar", vvar)
        ph.op("scalar", lambda e: e.activation(sc[0:1, 2:3], sc[0:1, 1:2],
                                               AF.Ln))
        sq_tok = ph.op("scalar", lambda e: e.activation(
            sc[0:1, 2:3], sc[0:1, 2:3], AF.Exp, scale=0.5))
        ph.wait("vector", sq_tok)
        thr_tok = ph.op("vector", lambda e: e.scalar_tensor_tensor(
            sc[0:1, 3:4], sc[0:1, 2:3], 1.25, sc[0:1, 0:1], ALU.mult, ALU.add))
        ph.wait("tensor", thr_tok)
        mt2 = ph.op("tensor", lambda e: e.matmul(
            pt[:, 4:5], g["ones"][0:1, 0:128], sc[0:1, 3:4],
            start=True, stop=True), sem=sT)
        ph.wait("vector", mt2)
        fin = ph.op("vector", lambda e: e.tensor_copy(thr_rep, pt[:, 4:5]))
        ph.run()


def _attn_a_ph3(g, xin, xout, QF, KBf, mcol, rcol, qptok, thr_rep,
                WfqXr, WfqYr, brep):
    nc = g["nc"]
    sT, sT2, sG, sWs, sF4 = g["sT"], g["sT2"], g["sG"], g["sWs"], g["sF4"]
    sPV = g["sPV"]
    stW, stW2, stQ, stX, stO = (g["stW"], g["stW2"], g["stQ"], g["stX"],
                                g["stO"])
    stVg = g["stVg"]
    from contextlib import ExitStack
    with ExitStack() as stack:
        def sb(name, shape, dtype):
            return stack.enter_context(nc.sbuf_tensor(name, shape, dtype)).ap()

        def psb(name, shape, dtype=F32):
            return stack.enter_context(nc.psum_tensor(name, shape, dtype)).ap()

        scob = sb("t_sco", [128, 2, S], F32)
        expb = sb("t_exp", [128, S], BF16)
        wT = sb("t_wT", [128, S], BF16)
        kblk = sb("t_kblk", [128, 3, DC, 512], BF16)
        vblk = sb("t_vblk", [128, 4, 4, D], BF16)
        qtl = sb("t_qtl", [128, 2, DC, 128], BF16)
        xres = sb("t_xres", [128, 2, D], F32)
        WoTb = sb("t_WoTb", [128, DC, D], BF16)
        ctxsb = sb("t_ctxsb", [128, D], BF16)
        ctxT = sb("t_ctxT", [128, DC, 128], BF16)
        outa = sb("t_outa", [128, D], F32)
        outsb = sb("t_outsb", [128, 2, D], F32)
        qbuf = sb("t_qbuf", [128, D], F32)
        tmpa = sb("t_tmpa", [128, 512], F32)
        tmpb = sb("t_tmpb", [128, 512], F32)
        nm = sb("t_nm", [128, 1], F32)
        lsum = sb("t_lsum", [128, 1], F32)
        linv = sb("t_linv", [128, 1], F32)
        smal = sb("t_smal", [128, 8], F32)
        ps_s = [psb("t_psA", [128, 512]), psb("t_psB", [128, 512])]
        psD = psb("t_pd", [128, 512])
        psU = psb("t_pu", [128, 2, 512])
        trp = psb("t_trp", [128, 1024], BF16)
        ctxp = psb("t_ctx", [128, 2, 512])

        ph = Ph(g)
        ph.dma("gpsimd", sG, WoTb, g["WoT"].rearrange("(c p) o -> p c o",
                                                      p=128))
        wo_tok = sG.tok

        pv_sched = []
        for ii in range(NT):
            ng = (ii + 4) // 4
            for vb in range(ng):
                pv_sched.append(min(4, ii + 1 - 4 * vb))
        pv_prefix = [0]
        for njt in pv_sched:
            pv_prefix.append(pv_prefix[-1] + njt)
        pv_base = sPV.n
        gv_counter = [0]
        vld = {}
        prev_store = {}
        prev_out_mm = None
        prev_mask_tok = None
        prev_ctx_evac = None
        prev_tr_evac = {}
        prev_pv = {}
        prev_outa_tok = None
        ex_hist = {}
        prev_ot = {}
        blk_mm = {}
        slot_last = {}
        for i in range(NT):
            W = (i + 1) * 128
            nb = (W + 511) // 512
            if i >= 2 and prev_out_mm is not None:
                ph.wait("scalar", prev_out_mm)
            qtok = stQ.dma(ph, "scalar", i, qtl[:, i % 2],
                           g["qTd"].rearrange("c p t -> p c t")[
                               :, :, i * 128:(i + 1) * 128])
            if i >= 2:
                ph.wait("scalar", prev_ot[i - 2])
            xrtok = stX.dma(ph, "scalar", i, xres[:, i % 2, :],
                            xin[i * 128:(i + 1) * 128, :])
            p0_drain(g, ph, 2)
            mask_toks = []
            for jb in range(nb):
                wj = min(512, W - jb * 512)
                sl = jb % 2
                if slot_last.get(jb % 3) is not None:
                    ph.wait("sync", slot_last[jb % 3])
                ktok = stW.dma(ph, "sync", jb, kblk[:, jb % 3],
                               g["kTd"].rearrange("c p t -> p c t")[
                                   :, :, jb * 512:jb * 512 + 512])
                Gv = gv_counter[0]
                if Gv >= 4:
                    ph.wait("gpsimd", (sPV, pv_base + pv_prefix[Gv - 3]))
                vld[Gv] = stVg.dma(ph, "gpsimd", Gv, vblk[:, Gv % 4],
                                   g["vd"][4 * jb:4 * jb + 4].rearrange(
                                       "t p d -> p t d"))
                gv_counter[0] += 1
                ph.wait("tensor", ktok)
                ph.wait("tensor", qtok)
                if prev_mask_tok is not None and jb <= 1:
                    ph.wait("tensor", prev_mask_tok)
                if len(mask_toks) >= 2:
                    ph.wait("tensor", mask_toks[jb - 2])

                for c in range(DC):
                    ph.op("tensor", lambda e, sl=sl, c=c, i=i, jb=jb, wj=wj:
                          e.matmul(ps_s[sl][:, 0:wj], qtl[:, i % 2, c, :],
                                   kblk[:, jb % 3, c, 0:wj],
                                   start=(c == 0), stop=(c == DC - 1)))
                if mask_toks:
                    ph.wait("tensor", mask_toks[jb - 1])
                ph.op("tensor", lambda e, i=i, jb=jb, wj=wj: e.matmul(
                    psD[:, 0:wj], QF[0:4, i * 128:(i + 1) * 128],
                    KBf[0:4, jb * 512:jb * 512 + wj],
                    start=True, stop=True))
                ph.op("tensor", lambda e, i=i, jb=jb, wj=wj: e.matmul(
                    psU[:, 0, 0:wj], QF[32:34, i * 128:(i + 1) * 128],
                    KBf[32:34, jb * 512:jb * 512 + wj],
                    start=True, stop=True))
                mmt = ph.op("tensor", lambda e, i=i, jb=jb, wj=wj:
                            e.matmul(
                                psU[:, 1, 0:wj],
                                QF[64:66, i * 128:(i + 1) * 128],
                                KBf[64:66, jb * 512:jb * 512 + wj],
                                start=True, stop=True), sem=sT2)
                blk_mm[jb] = mmt
                slot_last[jb % 3] = mmt
                ph.wait("vector", mmt)
                if jb == 0 and (i - 2) in ex_hist:
                    ph.wait("vector", ex_hist[i - 2])
                ph.op("vector", lambda e, wj=wj: e.tensor_scalar(
                    tmpa[:, 0:wj], psD[:, 0:wj], thr_rep, NEG,
                    ALU.is_gt, ALU.mult))
                ph.op("vector", lambda e, wj=wj: e.tensor_scalar(
                    tmpb[:, 0:wj], psU[:, 1, 0:wj], 0.0, None,
                    ALU.is_gt))
                ph.op("vector", lambda e, wj=wj: e.scalar_tensor_tensor(
                    tmpb[:, 0:wj], psU[:, 0, 0:wj], 0.0, tmpb[:, 0:wj],
                    ALU.is_gt, ALU.mult))
                ph.op("vector", lambda e, sl=sl, jb=jb, wj=wj, i=i:
                      e.scalar_tensor_tensor(
                          scob[:, i % 2, jb * 512:jb * 512 + wj],
                          ps_s[sl][:, 0:wj],
                          INV_SQRT_D, tmpa[:, 0:wj], ALU.mult, ALU.add))
                mk = ph.op("vector", lambda e, sl=sl, jb=jb, wj=wj, i=i:
                           e.scalar_tensor_tensor(
                               scob[:, i % 2, jb * 512:jb * 512 + wj],
                               tmpb[:, 0:wj],
                               NEG, scob[:, i % 2, jb * 512:jb * 512 + wj],
                               ALU.mult, ALU.add))
                mask_toks.append(mk)
            dg = ph.op("vector", lambda e, i=i: e.tensor_add(
                scob[:, i % 2, i * 128:(i + 1) * 128],
                scob[:, i % 2, i * 128:(i + 1) * 128], g["triS"]))
            prev_mask_tok = dg
            nm_tok = ph.op("vector", lambda e, W=W, i=i: e.reduce_max(
                nm, scob[:, i % 2, 0:W], axis=AX.X, negate=True))
            ph.wait("scalar", nm_tok)
            ex_tok = ph.op("scalar", lambda e, W=W, i=i: e.activation(
                expb[:, 0:W], scob[:, i % 2, 0:W], AF.Exp, bias=nm,
                accum_out=lsum))
            ex_hist[i] = ex_tok
            ph.wait("vector", ex_tok)
            lv_tok = ph.op("vector", lambda e: e.reciprocal(linv, lsum))
            trt = {}
            trev = {}
            for jb0 in range(0, i + 1, 8):
                jend = min(jb0 + 8, i + 1)
                ph.wait("tensor", ex_tok)
                if jb0 >= 8:
                    ph.wait("tensor", trev[jb0 - 1])
                elif i >= 1 and prev_ctx_evac is not None:
                    ph.wait("tensor", prev_ctx_evac)
                for jt in range(jb0, jend):
                    trt[jt] = ph.op("tensor", lambda e, jt=jt, reg=jt % 8:
                                    e.transpose(
                                        trp[:, reg * 128:(reg + 1) * 128],
                                        expb[:, jt * 128:(jt + 1) * 128],
                                        g["identbS"]),
                                    sem=sT if jt == jend - 1 else None)
                ph.wait("scalar", trt[jend - 1])
                if i >= 1 and prev_pv and jb0 == 0:
                    ph.wait("scalar", prev_pv[max(prev_pv)])
                wev = ph.op("scalar", lambda e, jb0=jb0, jend=jend:
                            e.activation(
                                wT[:, jb0 * 128:jend * 128],
                                trp[:, 0:(jend - jb0) * 128],
                                AF.Copy))
                for jt in range(jb0, jend):
                    trev[jt] = wev
            pv = {}
            gv0 = gv_counter[0] - nb
            for jt in range(i + 1):
                vb = jt // 4
                gvb = gv0 + vb
                if jt % 4 == 0:
                    ph.wait("tensor", vld[gvb])
                ph.wait("tensor", trev[jt])
                if jt == 0 and prev_outa_tok is not None:
                    ph.wait("tensor", prev_outa_tok)
                for hf in range(2):
                    pv[(jt, hf)] = ph.op(
                        "tensor", lambda e, jt=jt, hf=hf, gvb=gvb, i=i: e.matmul(
                            ctxp[:, hf, :], wT[:, jt * 128:(jt + 1) * 128],
                            vblk[:, gvb % 4, jt % 4, hf * 512:(hf + 1) * 512],
                            start=(jt == 0), stop=(jt == i)),
                        sem=sPV if hf == 1 else None)
            assert sPV.n == pv_base + pv_prefix[gv_counter[0]], (
                sPV.n, pv_base, pv_prefix[gv_counter[0]])
            prev_pv = pv
            ph.wait("scalar", pv[(i, 1)])
            ce = None
            for hf in range(2):
                ce = ph.op("scalar", lambda e, hf=hf: e.activation(
                    ctxsb[:, hf * 512:(hf + 1) * 512], ctxp[:, hf, :],
                    AF.Copy))
            ph.wait("tensor", ce)
            ph.wait("tensor", trev[i])
            ctr = None
            for c in range(DC):
                ctr = ph.op("tensor", lambda e, c=c: e.transpose(
                    trp[:, c * 128:(c + 1) * 128],
                    ctxsb[:, c * 128:(c + 1) * 128], g["identbS"]),
                    sem=sT if c == DC - 1 else None)
            ph.wait("scalar", ctr)
            ce2 = None
            for h in range(2):
                ce2 = ph.op("scalar", lambda e, h=h: e.activation(
                    ctxT[:, 4 * h:4 * h + 4, :],
                    trp[:, h * 512:(h + 1) * 512].rearrange(
                        "p (c m) -> p c m", m=128),
                    AF.Copy))
            prev_ctx_evac = ce2
            for c in range(DC):
                if c == 0:
                    ph.wait("tensor", ce2)
                    if i == 0:
                        ph.wait("tensor", wo_tok)
                for hf in range(2):
                    prev_out_mm = ph.op(
                        "tensor", lambda e, c=c, hf=hf: e.matmul(
                            ctxp[:, hf, :],
                            ctxT[:, c, :],
                            WoTb[:, c, hf * 512:(hf + 1) * 512],
                            start=(c == 0), stop=(c == DC - 1)),
                        sem=sT2 if (c == DC - 1 and hf == 1) else None)
            ph.wait("vector", prev_out_mm)
            ph.wait("vector", lv_tok)
            for hf in range(2):
                prev_outa_tok = ph.op("vector", lambda e, hf=hf: e.tensor_scalar(
                    outa[:, hf * 512:(hf + 1) * 512],
                    ctxp[:, hf, :],
                    linv, None, ALU.mult))
            ph.op("vector", lambda e, i=i: e.tensor_mul(
                smal[:, 1:2], qptok[:, 2 * i:2 * i + 1],
                qptok[:, 2 * i:2 * i + 1]))
            ph.op("vector", lambda e, i=i: e.tensor_mul(
                smal[:, 2:3], qptok[:, 2 * i + 1:2 * i + 2],
                qptok[:, 2 * i + 1:2 * i + 2]))
            n2t = ph.op("vector", lambda e: e.tensor_add(
                smal[:, 3:4], smal[:, 1:2], smal[:, 2:3]))
            ph.wait("scalar", n2t)
            ph.op("scalar", lambda e: e.activation(smal[:, 4:5], smal[:, 3:4],
                                                   AF.Ln))
            rn_tok = ph.op("scalar", lambda e: e.activation(
                smal[:, 4:5], smal[:, 4:5], AF.Exp, scale=0.5))
            ph.wait("vector", rn_tok)
            ph.op("vector", lambda e: e.tensor_scalar(
                smal[:, 4:5], smal[:, 4:5], 1.0e-12, None, ALU.max))
            ph.op("vector", lambda e: e.reciprocal(smal[:, 5:6], smal[:, 4:5]))
            ph.op("vector", lambda e, i=i: e.tensor_scalar(
                smal[:, 6:7], qptok[:, 2 * i:2 * i + 1], smal[:, 5:6], 0.1,
                ALU.mult, ALU.mult))
            ph.op("vector", lambda e, i=i: e.tensor_scalar(
                smal[:, 7:8], qptok[:, 2 * i + 1:2 * i + 2], smal[:, 5:6], 0.1,
                ALU.mult, ALU.mult))
            ph.op("vector", lambda e: e.tensor_scalar(
                qbuf, WfqXr, smal[:, 6:7], None, ALU.mult))
            ph.op("vector", lambda e: e.scalar_tensor_tensor(
                qbuf, WfqYr, smal[:, 7:8], qbuf, ALU.mult, ALU.add))
            ph.wait("vector", xrtok)
            ph.op("vector", lambda e: e.tensor_add(outa, outa, brep))
            ph.op("vector", lambda e: e.tensor_add(outa, outa, qbuf))
            ph.op("vector", lambda e, i=i: e.tensor_scalar(
                qbuf, xres[:, i % 2, :], mcol[:, i:i + 1], rcol[:, i:i + 1],
                ALU.subtract, ALU.mult))
            ph.op("vector", lambda e: e.tensor_add(outa, outa, qbuf))
            if i >= 2:
                ph.wait("vector", prev_store[i - 2])
            ot = ph.op("vector", lambda e, i=i: e.tensor_add(
                outsb[:, i % 2, :], outa, xres[:, i % 2, :]))
            prev_ot[i] = ot
            ph.wait("scalar", ot)
            prev_store[i] = stO.dma(ph, "scalar", i,
                                    xout[i * 128:(i + 1) * 128, :],
                                    outsb[:, i % 2, :])
        stO.wait_all(ph, "scalar")
        ph.run()


def phase_attn_b(g, xin, xout):
    nc = g["nc"]
    sT, sT2, sG, sF4 = g["sT"], g["sT2"], g["sG"], g["sF4"]
    stW, stW2, stW3, stH, stO = (g["stW"], g["stW2"], g["stW3"], g["stH"],
                                 g["stO"])
    from contextlib import ExitStack

    with ExitStack() as stack:
        def sb(name, shape, dtype):
            return stack.enter_context(nc.sbuf_tensor(name, shape, dtype)).ap()

        def psb(name, shape, dtype=F32):
            return stack.enter_context(nc.psum_tensor(name, shape, dtype)).ap()

        HW = 2048
        xt = sb("b_xt", [128, 4, D], F32)
        xlnf = sb("b_xlnf", [128, 4, D], F32)
        xlnT = sb("b_xlnT", [128, DC, S], BF16)
        bufs = dict(bn=sb("b_bn", [128, 12], F32), mv=sb("b_mv", [128, 2], F32),
                    v1=sb("b_v1", [128, 1], F32), rst=sb("b_rst", [128, 1], F32),
                    pstr=psb("b_pstr", [128, 2, 512]))
        Hb = sb("b_Hb", [128, 2, HW], F32)
        Ftb = sb("b_Ftb", [128, 2, HW], F32)
        Itb = sb("b_Itb", [128, 2, HW], F32)
        Sgb = sb("b_Sgb", [128, 2, HW], F32)
        Hsc = sb("b_Hsc", [128, 2, HW], F32)
        hbb = sb("b_hbb", [128, 2, HW], BF16)
        w1g = sb("b_w1g", [128, 2, 3, DC * 128], BF16)
        b1sb = sb("b_b1", [128, 3 * GC], F32)
        b1ng = sb("b_b1ng", [128, 2 * GC], F32)
        inith = sb("b_inith", [128, GC], F32)
        ginit = sb("b_ginit", [128, GC], F32)
        gtmp = sb("b_gtmp", [128, GC], F32)
        pp = psb("b_pp", [128, 6, 512])

        ph = Ph(g)
        p0_wait(g, ph, "sync", [4])
        b1_tok = ph.dma("sync", g["sWs"], b1sb, g["b1T"])
        ph.wait("sync", b1_tok)
        ih_tok = ph.dma("sync", g["sWs"], inith, g["inithT"])
        ph.wait("vector", b1_tok)
        b1n_tok = ph.op("vector", lambda e: e.tensor_scalar(
            b1ng, b1sb[:, GC:3 * GC], -1.0, None, ALU.mult))
        ph.wait("scalar", ih_tok)
        tk = ph.op("scalar", lambda e: e.activation(gtmp, inith, AF.Sigmoid))
        ph.wait("vector", tk)
        ph.op("vector", lambda e: e.tensor_sub(ginit, inith, gtmp))
        ph.op("vector", lambda e: e.tensor_scalar(ginit, ginit, 0.5, 0.0,
                                                  ALU.add, ALU.max))
        gi_tok = ph.op("vector", lambda e: e.tensor_add(ginit, ginit, gtmp))

        ln_state = [None]
        ln_evacs = []

        def emit_ln_group(grp):
            st, ev = emit_ln_tiles(g, ph, xin, list(range(4 * grp, 4 * grp + 4)),
                                   xt, xlnf, xlnT, bufs,
                                   dest_cols=[(4 * grp + t) * 128
                                              for t in range(4)],
                                   state=ln_state[0])
            ln_state[0] = st
            ln_evacs.extend(ev)

        mm_tok = {}
        ev_tok = {}
        evI_tok = {}
        sig_tok = {}
        g_tok = {}
        v_tok = {}
        scan_tok = {}
        cast_tok = {}
        st_tok = {}
        tails = []

        def emit_tail(U):
            cc, hh = U // 2, U % 2
            p = hh
            ph.wait("vector", g_tok[U])
            v_tok[U] = ph.op("vector", lambda e, p=p: e.tensor_mul(
                Ftb[:, p, :], Ftb[:, p, :], Hb[:, p, :]))
            if U >= 2:
                ph.wait("vector", cast_tok[U - 2])
            if U == 0:
                ph.wait("vector", gi_tok)
            init = (ginit[:, cc:cc + 1] if hh == 0
                    else Hsc[:, 0, HW - 1:HW])
            scan_tok[U] = ph.op("vector", lambda e, p=p, init=init:
                                e.tensor_tensor_scan(
                                    Hsc[:, p, :], Itb[:, p, :], Ftb[:, p, :],
                                    init, ALU.mult, ALU.add))

        def emit_cast(U):
            p = U % 2
            ph.wait("scalar", scan_tok[U])
            if U >= 2:
                ph.wait("scalar", st_tok[U - 2])
            cast_tok[U] = ph.op("scalar", lambda e, p=p: e.activation(
                hbb[:, p, :], Hsc[:, p, :], AF.Copy))

        def emit_store(U):
            cc, hh = U // 2, U % 2
            ph.wait("gpsimd", cast_tok[U])
            st_tok[U] = stH.dma(ph, "gpsimd", U,
                                g["hd"][cc][:, hh * HW:(hh + 1) * HW],
                                hbb[:, hh, :])

        for cc in range(GC):
            wtoks = []
            for gi, st in ((0, stW), (1, stW2), (2, stW3)):
                if cc >= 2:
                    ph.wait("sync", mm_tok[(cc - 2, 7, 2)])
                wtoks.append(st.dma(ph, "sync", cc, w1g[:, cc % 2, gi, :],
                                    g["W1b"][gi * GC + cc]))
            for nb in range(8):
                hh = nb // 4
                U = cc * 2 + hh
                p = hh
                if cc == 0:
                    emit_ln_group(nb)
                if nb % 4 == 0:
                    if U >= 1:
                        emit_tail(U - 1)
                    if U >= 2:
                        ph.wait("scalar", v_tok[U - 2])
                        ph.wait("scalar", scan_tok[U - 2])
                        ph.wait("scalar", g_tok[U - 2])
                for gi in range(3):
                    bank = gi * 2 + (nb % 2)
                    if nb == 0:
                        ph.wait("tensor", wtoks[gi])
                    if cc == 0 and gi == 0:
                        ph.wait("tensor", ln_evacs[4 * nb + 3])
                    if nb >= 2:
                        ph.wait("tensor", ev_tok[(cc, nb - 2, gi)])
                    elif cc >= 1:
                        ph.wait("tensor", ev_tok[(cc - 1, nb + 6, gi)])
                    for c in range(DC):
                        mm_tok[(cc, nb, gi)] = ph.op(
                            "tensor", lambda e, cc=cc, gi=gi, nb=nb, c=c,
                            bank=bank: e.matmul(
                                pp[:, bank, :],
                                w1g[:, cc % 2, gi, c * 128:(c + 1) * 128],
                                xlnT[:, c, nb * 512:(nb + 1) * 512],
                                start=(c == 0), stop=(c == DC - 1)),
                            sem=sT2 if c == DC - 1 else None)
                    col = (nb % 4) * 512
                    gcol = gi * GC + cc
                    ph.wait("scalar", mm_tok[(cc, nb, gi)])
                    if cc == 0 and nb == 0 and gi == 0:
                        ph.wait("scalar", b1_tok)
                        ph.wait("scalar", b1n_tok)
                    if gi == 0:
                        ev_tok[(cc, nb, gi)] = ph.op(
                            "scalar", lambda e, p=p, col=col, gcol=gcol,
                            bank=bank: e.activation(
                                Hb[:, p, col:col + 512], pp[:, bank, :],
                                AF.Identity, bias=b1sb[:, gcol:gcol + 1]))
                    else:
                        dst = Ftb if gi == 1 else Itb
                        ncol = (gi - 1) * GC + cc
                        ev_tok[(cc, nb, gi)] = ph.op(
                            "scalar", lambda e, p=p, col=col, ncol=ncol,
                            bank=bank, dst=dst: e.activation(
                                dst[:, p, col:col + 512], pp[:, bank, :],
                                AF.Exp, scale=-1.0,
                                bias=b1ng[:, ncol:ncol + 1]))
                if nb % 4 == 3:
                    evI_tok[U] = ev_tok[(cc, nb, 2)]
                    ph.op("scalar", lambda e, p=p: e.activation(
                        Ftb[:, p, :], Ftb[:, p, :], AF.Ln, bias=g["oneb1"]))
                    lnI_tok = ph.op("scalar", lambda e, p=p: e.activation(
                        Itb[:, p, :], Itb[:, p, :], AF.Ln, bias=g["oneb1"]))
                    ph.wait("vector", lnI_tok)
                    d2_tok = ph.op("vector", lambda e, p=p: e.tensor_sub(
                        Itb[:, p, :], Itb[:, p, :], Ftb[:, p, :]))
                    ph.wait("scalar", d2_tok)
                    c_tok = ph.op("scalar", lambda e, p=p: e.activation(
                        Itb[:, p, :], Itb[:, p, :], AF.Sigmoid))
                    sig_tok[U] = ph.op("scalar", lambda e, p=p: e.activation(
                        Sgb[:, p, :], Hb[:, p, :], AF.Sigmoid))
                    if U >= 1:
                        emit_cast(U - 1)
                    ph.wait("vector", c_tok)
                    ph.op("vector", lambda e, p=p: e.tensor_scalar(
                        Ftb[:, p, :], Itb[:, p, :], -1.0, 1.0,
                        ALU.mult, ALU.add))
                    ph.wait("vector", sig_tok[U])
                    ph.op("vector", lambda e, p=p: e.tensor_sub(
                        Hb[:, p, :], Hb[:, p, :], Sgb[:, p, :]))
                    ph.op("vector", lambda e, p=p: e.tensor_scalar(
                        Hb[:, p, :], Hb[:, p, :], 0.5, 0.0,
                        ALU.add, ALU.max))
                    g_tok[U] = ph.op("vector", lambda e, p=p: e.tensor_add(
                        Hb[:, p, :], Hb[:, p, :], Sgb[:, p, :]))
                    if U >= 1:
                        emit_store(U - 1)
        emit_tail(31)
        emit_cast(31)
        emit_store(31)
        stH.wait_all(ph, "gpsimd")
        ph.run()

    with ExitStack() as stack:
        def sb(name, shape, dtype):
            return stack.enter_context(nc.sbuf_tensor(name, shape, dtype)).ap()

        def psb(name, shape, dtype=F32):
            return stack.enter_context(nc.psum_tensor(name, shape, dtype)).ap()

        W2b = sb("b_W2b", [128, GC, D], BF16)
        hsb = sb("b_hsb", [128, 2, 512], BF16)
        b2row = sb("b_b2row", [1, D], F32)
        b2rep = sb("b_b2rep", [128, D], F32)
        xres = sb("b_xres", [128, 4, D], F32)
        outsb = sb("b_outsb", [128, 2, D], F32)
        po = psb("b_po", [128, 8, 512])

        ph = Ph(g)
        w2_toks = []
        for q in range(4):
            w2_toks.append(g["stVg"].dma(
                ph, "gpsimd", q, W2b[:, 4 * q:4 * q + 4, :],
                g["W2"][512 * q:512 * (q + 1), :].rearrange(
                    "(c p) o -> p c o", p=128)))
        b2_tok = ph.dma("sync", g["sWs"], b2row, g["b2"])
        ph.wait("tensor", b2_tok)
        rep_toks = []
        for hf in range(2):
            rep_toks.append(ph.op("tensor", lambda e, hf=hf: e.matmul(
                po[:, hf, :], g["ones"][0:1, 0:128],
                b2row[0:1, hf * 512:(hf + 1) * 512], start=True, stop=True),
                sem=sT))
        ph.wait("vector", rep_toks[-1])
        brep_tok = None
        for hf in range(2):
            brep_tok = ph.op("vector", lambda e, hf=hf: e.tensor_copy(
                b2rep[:, hf * 512:(hf + 1) * 512], po[:, hf, :]))

        mm_tok = {}
        add_tok = {}
        sto = {}
        for tg in range(8):
            f4toks = []
            for t in range(4):
                gt = tg * 4 + t
                if tg > 0:
                    ph.wait("scalar", add_tok[(tg - 1, t)])
                f4toks.append(ph.dma("scalar", sF4, xres[:, t, :],
                                     xin[gt * 128:(gt + 1) * 128, :]))
            for cc in range(GC):
                kidx = tg * GC + cc
                if kidx >= 2:
                    ph.wait("sync", mm_tok[kidx - 2])
                htok = stW.dma(ph, "sync", kidx, hsb[:, kidx % 2, :],
                               g["hd"][cc][:, tg * 512:(tg + 1) * 512])
                ph.wait("tensor", htok)
                if cc % 4 == 0 and tg == 0:
                    ph.wait("tensor", w2_toks[cc // 4])
                if cc == 0:
                    if tg == 0:
                        ph.wait("tensor", brep_tok)
                    else:
                        ph.wait("tensor", add_tok[(tg - 1, 3)])
                for t in range(4):
                    for hf in range(2):
                        last = (t == 3 and hf == 1)
                        mm_tok[kidx] = ph.op(
                            "tensor", lambda e, cc=cc, t=t, hf=hf, kidx=kidx:
                            e.matmul(
                                po[:, 2 * t + hf, :],
                                hsb[:, kidx % 2, t * 128:(t + 1) * 128],
                                W2b[:, cc, hf * 512:(hf + 1) * 512],
                                start=(cc == 0), stop=(cc == GC - 1)),
                            sem=sT2 if last else None)
            for t in range(4):
                gt = tg * 4 + t
                ph.wait("vector", mm_tok[tg * GC + GC - 1])
                ph.wait("vector", f4toks[3])
                if t >= 2:
                    ph.wait("vector", sto[(tg, t - 2)])
                elif tg > 0:
                    ph.wait("vector", sto[(tg - 1, t + 2)])
                ph.op("vector", lambda e, t=t: e.tensor_add(
                    outsb[:, t % 2, :],
                    po.rearrange("p a b -> p (a b)")[:, t * 1024:(t + 1) * 1024],
                    b2rep))
                atok = ph.op("vector", lambda e, t=t: e.tensor_add(
                    outsb[:, t % 2, :], outsb[:, t % 2, :], xres[:, t, :]))
                add_tok[(tg, t)] = atok
                ph.wait("scalar", atok)
                sto[(tg, t)] = stO.dma(ph, "scalar", t,
                                       xout[gt * 128:(gt + 1) * 128, :],
                                       outsb[:, t % 2, :])
        stO.wait_all(ph, "scalar")
        ph.run()



def _prep_shared(inputs):
    f = np.ascontiguousarray
    ii = {}
    ii["WqT"] = f(inputs["L0_attn_a_Wq"].T)
    ii["WkT"] = f(inputs["L0_attn_a_Wk"].T)
    ii["WvT"] = f(inputs["L0_attn_a_Wv"].T)
    ii["WoT"] = f(inputs["L0_attn_a_Wo"].T)
    ii["bqT"] = f(inputs["L0_attn_a_bq"].reshape(DC, 128).T)
    ii["bkT"] = f(inputs["L0_attn_a_bk"].reshape(DC, 128).T)
    ii["bv"] = f(inputs["L0_attn_a_bv"].reshape(1, D))
    ii["bo"] = f(inputs["L0_attn_a_bo"].reshape(1, D))
    ii["bfq"] = f(inputs["L0_attn_a_bfq"].reshape(1, D))
    ii["WtqX"] = f(inputs["L0_attn_a_Wtq"][0:1, :])
    ii["WtqY"] = f(inputs["L0_attn_a_Wtq"][1:2, :])
    ii["btq"] = f(inputs["L0_attn_a_btq"].reshape(1, 2))
    ii["WfqX"] = f(inputs["L0_attn_a_Wfq"][:, 0].reshape(1, D))
    ii["WfqY"] = f(inputs["L0_attn_a_Wfq"][:, 1].reshape(1, D))
    ii["W1"] = f(inputs["L0_attn_b_W1"])
    ii["b1T"] = f(inputs["L0_attn_b_b1"].reshape(3 * GC, 128).T)
    ii["W2"] = f(inputs["L0_attn_b_W2"])
    ii["b2"] = f(inputs["L0_attn_b_b2"].reshape(1, D))
    ii["inithT"] = f(inputs["L0_attn_b_init_h"].reshape(GC, 128).T)
    ii["w1T"] = f(inputs["L0_ffn_w1_w"].T)
    ii["w2T"] = f(inputs["L0_ffn_w2_w"].T)
    ii["h1T"] = f(inputs["L0_ffn_h1_w"].T)
    ii["h2T"] = f(inputs["L0_ffn_h2_w"].T)
    ii["w3T"] = f(inputs["L0_ffn_w3_w"].T)
    ii["s1T"] = f(inputs["L0_ffn_w1_s"].reshape(HC, 128).T)
    ii["s2T"] = f(inputs["L0_ffn_w2_s"].reshape(HC, 128).T)
    ii["s3T"] = f(inputs["L0_ffn_h1_s"].reshape(HC, 128).T)
    ii["s4T"] = f(inputs["L0_ffn_h2_s"].reshape(HC, 128).T)
    ii["sw3r"] = f(inputs["L0_ffn_w3_s"].reshape(1, D))
    ii["ident"] = np.eye(128, dtype=np.float32)
    ii["identb"] = np.eye(128, dtype=ml_dtypes.bfloat16)
    jj, kk = np.meshgrid(np.arange(128), np.arange(128), indexing="ij")
    ii["tri"] = np.where(kk > jj, np.float32(NEG),
                         np.float32(0.0)).astype(np.float32)
    return ii


def run_cores(inputs, stages=ALL_STAGES, taps=(), cores=8, trace=False,
              dumps=()):
    xfull = np.ascontiguousarray(
        inputs["L0_attn_a_hidden_states"], dtype=np.float32)
    shared = _prep_shared(inputs)
    nc = build_program(stages=stages, taps=taps, dumps=dumps)
    in_maps = []
    for b in range(cores):
        m = dict(shared)
        m["x0"] = np.ascontiguousarray(xfull[b])
        in_maps.append(m)
    res = run_bass_kernel_spmd(nc, in_maps, core_ids=list(range(cores)),
                               trace=trace)
    return res


def kernel(**inputs):
    res = run_cores(inputs)
    out = np.stack([np.asarray(res.results[b]["y"]) for b in range(8)], axis=0)
    return out.astype(np.float32)


# version 27 (final)
# speedup vs baseline: 1.1130x; 1.0962x over previous
import numpy as np
import ml_dtypes
import concourse.bass as bass
import concourse.mybir as mybir
from concourse.bass_utils import run_bass_kernel_spmd

F32 = mybir.dt.float32
BF16 = mybir.dt.bfloat16
AF = mybir.ActivationFunctionType
ALU = mybir.AluOpType
AX = mybir.AxisListType

D = 1024
S = 4096
NT = S // 128
DC = D // 128
DH = 4096
HC = DH // 128
DG = 2048
GC = DG // 128
NEG = -1.0e30
T_WIN = 2.4414063e-4
EPS_LN = 1.0e-5
INV_SQRT_D = 1.0 / 32.0

ALL_STAGES = ("attn_a", "ffn1", "attn_b", "ffn2")
ENGS = ("sync", "tensor", "vector", "scalar", "gpsimd")
ENG_SEM = {"vector": "sV", "scalar": "sS"}


class Sem:
    def __init__(self, nc, name):
        self.h = nc.alloc_semaphore(name=name)
        self.n = 0

    @property
    def tok(self):
        return (self, self.n)


class Stream:

    def __init__(self, nc, name, nslots):
        self.sems = [Sem(nc, f"{name}{i}") for i in range(nslots)]

    def dma(self, ph, eng, slot, out, in_):
        s = self.sems[slot % len(self.sems)]
        ph.wait(eng, s.tok)
        ph.dma(eng, s, out, in_)
        return s.tok

    def wait_all(self, ph, eng):
        for s in self.sems:
            ph.wait(eng, s.tok)


class Ph:

    def __init__(self, g):
        self.nc = g["nc"]
        self.g = g
        self.q = {k: [] for k in ENGS}

    def wait(self, eng, tok):
        sem, v = tok
        assert v >= 0
        self.q[eng].append(lambda e, s=sem.h, v=v: e.wait_ge(s, v))

    def op(self, eng, fn, sem=None, k=1):
        if eng in ENG_SEM:
            au = self.g[ENG_SEM[eng]]
            self.wait(eng, au.tok)
            if sem is None:
                sem = au
        if sem is None:
            self.q[eng].append(lambda e, fn=fn: fn(e))
            return None
        def run(e, fn=fn, sem=sem, k=k):
            return fn(e).then_inc(sem.h, k)
        self.q[eng].append(run)
        sem.n += k
        return sem.tok

    def dma(self, eng, sem, out, in_):
        def run(e, o=out, i=in_, s=sem):
            return e.dma_start(o, i).then_inc(s.h, 16)
        self.q[eng].append(run)
        sem.n += 16
        return sem.tok

    def run(self):
        with self.nc.Block() as blk:
            for name, dec in (("sync", blk.sync), ("tensor", blk.tensor),
                              ("vector", blk.vector), ("scalar", blk.scalar),
                              ("gpsimd", blk.gpsimd)):
                if self.q[name]:
                    fns = self.q[name]
                    dec(lambda e, fns=fns: [fn(e) for fn in fns])



def build_program(stages=ALL_STAGES, taps=(), dumps=()):
    nc = bass.Bass("TRN2", target_bir_lowering=False)

    def din(name, shape, dtype=F32):
        return nc.dram_tensor(name, list(shape), dtype, kind="ExternalInput").ap()

    def dscr(name, shape, dtype):
        return nc.dram_tensor(name, list(shape), dtype, kind="Internal").ap()

    g = dict(nc=nc)
    g["x0"] = din("x0", [S, D])
    for n, sh in [("WqT", [D, D]), ("WkT", [D, D]), ("WvT", [D, D]), ("WoT", [D, D]),
                  ("bqT", [128, DC]), ("bkT", [128, DC]), ("bv", [1, D]),
                  ("bo", [1, D]), ("bfq", [1, D]), ("WtqX", [1, D]), ("WtqY", [1, D]),
                  ("btq", [1, 2]), ("WfqX", [1, D]), ("WfqY", [1, D]),
                  ("W1", [D, 3 * DG]), ("b1T", [128, 3 * GC]), ("W2", [DG, D]),
                  ("b2", [1, D]), ("inithT", [128, GC]),
                  ("w1T", [D, DH]), ("w2T", [D, DH]), ("h1T", [DH, DH]),
                  ("h2T", [DH, DH]), ("w3T", [DH, D]),
                  ("s1T", [128, HC]), ("s2T", [128, HC]), ("s3T", [128, HC]),
                  ("s4T", [128, HC]), ("sw3r", [1, D]),
                  ("ident", [128, 128]), ("tri", [128, 128])]:
        g[n] = din(n, sh)
    g["identb"] = din("identb", [128, 128], BF16)
    g["y"] = nc.dram_tensor("y", [S, D], F32, kind="ExternalOutput").ap()

    g["x1"] = dscr("x1", [S, D], F32)
    g["x2"] = dscr("x2", [S, D], F32)
    g["x3"] = dscr("x3", [S, D], F32)
    g["qTd"] = dscr("qTd", [DC, 128, S], BF16)
    g["kTd"] = dscr("kTd", [DC, 128, S], BF16)
    g["vd"] = dscr("vd", [NT, 128, D], BF16)
    g["hd"] = dscr("hd", [GC, 128, S], BF16)
    w12 = dscr("w12tb", [HC, 128, 2, DC * 128], BF16)
    g["w12tb"] = w12
    g["w1tb"] = w12[:, :, 0, :]
    g["w2tb"] = w12[:, :, 1, :]
    g["h1tb"] = dscr("h1tb", [HC, 128, HC * 128], BF16)
    g["h2tb"] = dscr("h2tb", [HC, 128, HC * 128], BF16)
    g["w3tb"] = dscr("w3tb", [HC, 128, D], BF16)
    g["W1b"] = dscr("W1b", [3 * GC, 128, DC * 128], BF16)

    taps_d = {}
    for t in taps:
        taps_d[t] = nc.dram_tensor("tap_" + t, [S, D], F32,
                                   kind="ExternalOutput").ap()

    g["identS"] = nc.alloc_sbuf_tensor("identS", [128, 128], F32).ap()
    g["identbS"] = nc.alloc_sbuf_tensor("identbS", [128, 128], BF16).ap()
    g["triS"] = nc.alloc_sbuf_tensor("triS", [128, 128], F32).ap()
    g["ones"] = nc.alloc_sbuf_tensor("onesS", [1, 512], F32).ap()
    g["oneb1"] = nc.alloc_sbuf_tensor("oneb1S", [128, 1], F32).ap()
    g["halfb1"] = nc.alloc_sbuf_tensor("halfb1S", [128, 1], F32).ap()
    g["onesb"] = nc.alloc_sbuf_tensor("onesbS", [1, 512], BF16).ap()

    for s in ("sT", "sT2", "sV", "sS", "sG", "sGc", "sWs", "sF4", "sPV"):
        g[s] = Sem(nc, s)
    g["sP0"] = [Sem(nc, f"sP0_{j}") for j in range(6)]
    g["stW"] = Stream(nc, "stW", 2)
    g["stW2"] = Stream(nc, "stW2", 2)
    g["stW3"] = Stream(nc, "stW3", 2)
    g["stH"] = Stream(nc, "stH", 2)
    g["stVd"] = Stream(nc, "stVd", 2)
    g["stVg"] = Stream(nc, "stVg", 4)
    g["stQ"] = Stream(nc, "stQ", 2)
    g["stX"] = Stream(nc, "stX", 2)
    g["stO"] = Stream(nc, "stO", 2)
    g["stG"] = Stream(nc, "stG", 2)

    phase_consts(g)
    phase_p0(g)

    cur = g["x0"]
    chain = [("attn_a", g["x1"]), ("ffn1", g["x2"]),
             ("attn_b", g["x3"]), ("ffn2", g["y"])]
    for name, out in chain:
        if name in stages:
            if name == "attn_a":
                phase_attn_a(g, cur, out)
            elif name in ("ffn1", "ffn2"):
                phase_ffn(g, cur, out)
            else:
                phase_attn_b(g, cur, out)
        else:
            phase_copy(g, cur, out)
        if name in taps_d:
            phase_copy(g, out, taps_d[name])
        cur = out

    for name in dumps:
        src = g[name]
        dst = nc.dram_tensor("dump_" + name, list(src.shape), src.dtype,
                             kind="ExternalOutput").ap()
        phase_copy(g, src, dst)

    for s in ("sT", "sT2", "sV", "sS", "sG"):
        assert g[s].n < 2 ** 30, s
    return nc


def phase_consts(g):
    ph = Ph(g)
    sWs = g["sWs"]
    ph.dma("sync", sWs, g["identS"], g["ident"])
    ph.dma("sync", sWs, g["identbS"], g["identb"])
    t = ph.dma("sync", sWs, g["triS"], g["tri"])
    ph.op("vector", lambda e: e.memset(g["ones"], 1.0))
    ph.op("vector", lambda e: e.memset(g["oneb1"], 1.0))
    ph.op("vector", lambda e: e.memset(g["halfb1"], 0.5))
    ph.op("vector", lambda e: e.memset(g["onesb"], 1.0))
    ph.wait("vector", t)
    ph.run()


def phase_copy(g, src, dst):
    ph = Ph(g)
    t = ph.dma("gpsimd", g["sG"], dst, src)
    ph.wait("gpsimd", t)
    ph.run()



def phase_p0(g):
    casts = []
    sems = g["sP0"]

    def add(j, dst, srcv):
        casts.append((j, lambda ph, d=dst, s=srcv, sem=sems[j]:
                      ph.dma("gpsimd", sem, d, s)))

    for c in range(DC):
        add(0, g["w1tb"].rearrange("mc p (c m) -> p mc c m", m=128)[:, :, c, :],
            g["w1T"][c * 128:(c + 1) * 128, :].rearrange("p (mc m) -> p mc m",
                                                         m=128))
        add(0, g["w2tb"].rearrange("mc p (c m) -> p mc c m", m=128)[:, :, c, :],
            g["w2T"][c * 128:(c + 1) * 128, :].rearrange("p (mc m) -> p mc m",
                                                         m=128))
    for j, (wT, dstt) in ((1, (g["h1T"], g["h1tb"])), (2, (g["h2T"], g["h2tb"]))):
        for c in range(HC):
            add(j, dstt.rearrange("mc p (c m) -> p mc c m", m=128)[:, :, c, :],
                wT[c * 128:(c + 1) * 128, :].rearrange("p (mc m) -> p mc m",
                                                       m=128))
    for c in range(HC):
        add(3, g["w3tb"][c], g["w3T"][c * 128:(c + 1) * 128, :])
    for c in range(DC):
        add(4, g["W1b"].rearrange("cc p (c m) -> p cc c m", m=128)[:, :, c, :],
            g["W1"][c * 128:(c + 1) * 128, :].rearrange("p (cc m) -> p cc m",
                                                        m=128))
    g["p0_casts"] = casts
    g["p0_total"] = {j: sum(1 for jj, _ in casts if jj == j) * 16
                     for j in range(6)}


def p0_drain(g, ph, n):
    for _ in range(n):
        if not g["p0_casts"]:
            return
        j, fn = g["p0_casts"].pop(0)
        fn(ph)


def p0_wait(g, ph, eng, jobs):
    for j in jobs:
        assert not any(jj == j for jj, _ in g["p0_casts"]), f"job {j} pending"
        ph.wait(eng, (g["sP0"][j], g["p0_total"][j]))



def emit_ln_tiles(g, ph, xin, tiles, xt, xlnf, xlnT, bufs,
                  extra_v=None, keep_stats=None, dest_base=0, dest_cols=None,
                  pre_evac_wait=None, pre_tr_wait=None, state=None,
                  tr_sink=None):
    sT = g["sT"]
    stX = g["stX"]
    bn, mv, v1, rst = bufs["bn"], bufs["mv"], bufs["v1"], bufs["rst"]
    pstr = bufs["pstr"]
    if state is None:
        state = {"apply": [], "tr": [], "evac": []}
    apply_toks, tr_toks, evac_toks = state["apply"], state["tr"], state["evac"]

    out_evacs = []
    for j, gt in enumerate(tiles):
        k = len(apply_toks)
        if k >= 4:
            ph.wait("sync", apply_toks[k - 4])
        ld = stX.dma(ph, "sync", k, xt[:, k % 4, :],
                     xin[gt * 128:(gt + 1) * 128, :])

        ph.wait("vector", ld)
        ph.op("vector", lambda e, k=k: e.bn_stats(bn[:, 0:6],
                                                  xt[:, k % 4, 0:512]))
        ph.op("vector", lambda e, k=k: e.bn_stats(bn[:, 6:12],
                                                  xt[:, k % 4, 512:1024]))
        ph.op("vector", lambda e: e.bn_aggr(mv, bn))
        st_tok = ph.op("vector", lambda e: e.tensor_scalar(
            v1, mv[:, 1:2], EPS_LN, None, ALU.add))
        ph.wait("scalar", st_tok)
        ph.op("scalar", lambda e: e.activation(rst, v1, AF.Ln))
        rst_tok = ph.op("scalar", lambda e: e.activation(rst, rst, AF.Exp,
                                                         scale=-0.5))
        if keep_stats is not None:
            ph.op("vector", lambda e, gt=gt: e.tensor_copy(
                keep_stats[0][:, gt:gt + 1], mv[:, 0:1]))
        ph.wait("vector", rst_tok)
        if k >= 4:
            assert len(tr_toks) >= k - 3, "tr_sink must be drained promptly"
            ph.wait("vector", tr_toks[k - 4])
        apply_toks.append(ph.op("vector", lambda e, k=k: e.tensor_scalar(
            xlnf[:, k % 4, :], xt[:, k % 4, :],
            mv[:, 0:1], rst, ALU.subtract, ALU.mult)))
        if keep_stats is not None:
            ph.op("vector", lambda e, gt=gt: e.tensor_copy(
                keep_stats[1][:, gt:gt + 1], rst))
        if extra_v is not None:
            extra_v(j, gt)

        def emit_tr(k=k, j=j):
            ph.wait("tensor", apply_toks[k])
            if k >= 1:
                ph.wait("tensor", evac_toks[k - 1])
            if pre_tr_wait is not None:
                pre_tr_wait(ph, j)
            tk = None
            for c in range(DC):
                tk = ph.op("tensor", lambda e, k=k, c=c: e.transpose(
                    pstr[:, c // 4, (c % 4) * 128:(c % 4 + 1) * 128],
                    xlnf[:, k % 4, c * 128:(c + 1) * 128], g["identS"]),
                    sem=sT if c == DC - 1 else None)
            tr_toks.append(tk)
            ph.wait("scalar", tk)
            if pre_evac_wait is not None:
                pre_evac_wait(ph, j)
            ek = None
            col0 = (dest_cols[j] if dest_cols is not None
                    else dest_base + j * 128)
            for h in range(2):
                ek = ph.op("scalar", lambda e, h=h, col=col0: e.activation(
                    xlnT[:, 4 * h:4 * h + 4, col:col + 128],
                    pstr[:, h, :].rearrange("p (c m) -> p c m", m=128),
                    AF.Copy))
            evac_toks.append(ek)
            out_evacs.append(ek)
        if tr_sink is not None:
            tr_sink.append(emit_tr)
        else:
            emit_tr()
    return state, out_evacs



def phase_ffn(g, xin, xout):
    nc = g["nc"]
    sT2, sF4 = g["sT2"], g["sF4"]
    stW, stO = g["stW"], g["stO"]

    from contextlib import ExitStack
    uid = g["uid"] = g.get("uid", 0) + 1
    with ExitStack() as stack:
        def sb(name, shape, dtype):
            return stack.enter_context(
                nc.sbuf_tensor(f"{name}_{uid}", shape, dtype)).ap()

        def psb(name, shape, dtype=F32):
            return stack.enter_context(
                nc.psum_tensor(f"{name}_{uid}", shape, dtype)).ap()

        xt = sb("f_xt", [128, 4, D], F32)
        xlnf = sb("f_xlnf", [128, 4, D], F32)
        xlnT = sb("f_xlnT", [128, DC, 2, 512], BF16)
        bufs = dict(bn=sb("f_bn", [128, 12], F32), mv=sb("f_mv", [128, 2], F32),
                    v1=sb("f_v1", [128, 1], F32), rst=sb("f_rst", [128, 1], F32),
                    pstr=psb("f_pstr", [128, 2, 512]))
        hT = sb("f_hT", [128, HC, 512], BF16)
        gT = sb("f_gT", [128, HC, 512], BF16)
        w12 = sb("f_w12", [128, 2, 2, 1024], BF16)
        hw = sb("f_hw", [128, 2, 2, HC * 128], BF16)
        silu = sb("f_silu", [128, 2, 512], BF16)
        w2c = sb("f_w2c", [128, 2, 512], BF16)
        w3sb = sb("f_w3", [128, 2, 2, D], BF16)
        outsb = sb("f_out", [128, 2, D], F32)
        f4x = sb("f_f4x", [128, 4, D], F32)
        pw1 = psb("f_pw1", [128, 2, 512])
        pw2 = psb("f_pw2", [128, 2, 512])
        pg = psb("f_pg", [128, 2, 512])
        ssb = sb("f_ssb", [128, 4, HC], F32)
        sw3row = sb("f_sw3row", [1, D], F32)
        sw3rep = sb("f_sw3rep", [128, D], F32)
        sevac = sb("f_sevac", [128, 2, 512], F32)

        ph = Ph(g)
        p0_drain(g, ph, 10 ** 9)
        p0_wait(g, ph, "sync", [0, 1, 2, 3])
        for idx, nm in enumerate(("s1T", "s2T", "s3T", "s4T")):
            t0 = ph.dma("sync", g["sWs"], ssb[:, idx, :], g[nm])
            ph.wait("sync", t0)
        s_tok = t0
        t0 = ph.dma("sync", g["sWs"], sw3row, g["sw3r"])
        ph.wait("sync", t0)
        ph.wait("tensor", t0)
        rep = None
        for hf in range(2):
            mmt = ph.op("tensor", lambda e, hf=hf: e.matmul(
                pw1[:, hf, :], g["ones"][0:1, 0:128],
                sw3row[0:1, hf * 512:(hf + 1) * 512], start=True, stop=True),
                sem=g["sT"])
            ph.wait("vector", mmt)
            rep = ph.op("vector", lambda e, hf=hf: e.tensor_copy(
                sw3rep[:, hf * 512:(hf + 1) * 512], pw1[:, hf, :]))
        sw3_tok = rep
        ln_state = None
        ln_evacs = {}
        f1_mm = {}
        f4_mm = {}
        f4_add = {}
        sto = {}

        def emit_ln(ch, sink=None):
            def pev(ph_, j, ch=ch):
                if j == 0 and ch >= 2:
                    ph_.wait("scalar", f1_mm[ch - 2])
            def ptw(ph_, j, ch=ch):
                if j == 0 and ch >= 1 and (ch - 1) in f4_add:
                    ph_.wait("tensor", f4_add[ch - 1])
            nonlocal ln_state
            tiles = list(range(ch * 4, ch * 4 + 4))
            dest = [t * 128 for t in range(4)]
            ln_state, ev = emit_ln_tiles(
                g, ph, xin, tiles, xt, xlnf,
                xlnT.rearrange("p c s w -> p c (s w)"), bufs,
                dest_cols=[(ch % 2) * 512 + dd for dd in dest],
                pre_evac_wait=pev, pre_tr_wait=ptw, state=ln_state,
                tr_sink=sink)
            ln_evacs[ch] = ev

        emit_ln(0)
        prev_g2_mm = None
        prev_sevac_s = None
        for ch in range(8):
            sl = ch % 2
            ev_tok = {}
            mul_tok = {}
            mmt_hist = {}
            sil_hist = {}
            sink = []
            if ch < 7:
                emit_ln(ch + 1, sink=sink)
            for mc in range(HC):
                if mc >= 12 and (mc - 12) % 4 == 0 and sink:
                    sink.pop(0)()
                if mc >= 2:
                    ph.wait("sync", mmt_hist[mc - 2])
                wtok = stW.dma(ph, "sync", mc, w12[:, mc % 2],
                               g["w12tb"][mc])
                ph.wait("tensor", wtok)
                if mc == 0:
                    ph.wait("tensor", ln_evacs[ch][-1])
                    if ch >= 1:
                        ph.wait("tensor", f4_add[ch - 1])
                    elif ch == 0:
                        ph.wait("tensor", sw3_tok)
                if mc >= 2:
                    ph.wait("tensor", ev_tok[mc - 2])
                for c in range(DC):
                    ph.op("tensor", lambda e, mc=mc, c=c, sl=sl: e.matmul(
                        pw1[:, mc % 2, :],
                        w12[:, mc % 2, 0, c * 128:(c + 1) * 128],
                        xlnT[:, c, sl, :], start=(c == 0), stop=(c == DC - 1)))
                for c in range(DC):
                    mmt_hist[mc] = ph.op("tensor", lambda e, mc=mc, c=c, sl=sl:
                                         e.matmul(
                                             pw2[:, mc % 2, :],
                                             w12[:, mc % 2, 1,
                                                 c * 128:(c + 1) * 128],
                                             xlnT[:, c, sl, :],
                                             start=(c == 0),
                                             stop=(c == DC - 1)),
                                         sem=sT2 if c == DC - 1 else None)
                ph.wait("vector", mmt_hist[mc])
                if mc >= 2:
                    ph.wait("vector", mul_tok[mc - 2])
                if mc == 0:
                    if ch >= 1:
                        ph.wait("vector", f4_mm[ch - 1])
                    if ch == 0:
                        ph.wait("vector", s_tok)
                if mc >= 2:
                    ph.wait("vector", sil_hist[mc - 2])
                elif prev_sevac_s is not None:
                    ph.wait("vector", prev_sevac_s)
                sv1 = ph.op("vector", lambda e, mc=mc: e.tensor_scalar(
                    sevac[:, mc % 2, :], pw1[:, mc % 2, :],
                    ssb[:, 0, mc:mc + 1], None, ALU.mult))
                ev_tok[mc] = ph.op("vector", lambda e, mc=mc: e.tensor_scalar(
                    w2c[:, mc % 2, :], pw2[:, mc % 2, :],
                    ssb[:, 1, mc:mc + 1], None, ALU.mult))
                ph.wait("scalar", ev_tok[mc])
                sil = sil_hist[mc] = ph.op("scalar", lambda e, mc=mc: e.activation(
                    silu[:, mc % 2, :], sevac[:, mc % 2, :], AF.Silu))
                ph.wait("vector", sil)
                mul_tok[mc] = ph.op("vector", lambda e, mc=mc: e.tensor_mul(
                    hT[:, mc, :], silu[:, mc % 2, :], w2c[:, mc % 2, :]))
            while sink:
                sink.pop(0)()
            f1_mm[ch] = mmt_hist[HC - 1]

            def gelu_layer(srcb, dst, wtb, ready_tok, pg_free_tok,
                           hw_free_tok, s_idx, sevac_free):
                mm_tok = {}
                gel_tok = {}
                for mc in range(HC):
                    pair = mc // 2
                    if mc % 2 == 0:
                        if pair >= 2:
                            ph.wait("sync", mm_tok[mc - 3])
                        elif hw_free_tok is not None:
                            ph.wait("sync", hw_free_tok)
                        wtok = stW.dma(ph, "sync", pair, hw[:, pair % 2],
                                       wtb[mc:mc + 2].rearrange(
                                           "k p d -> p k d"))
                        ph.wait("tensor", wtok)
                    if mc == 0:
                        ph.wait("tensor", ready_tok)
                        if pg_free_tok is not None:
                            ph.wait("tensor", pg_free_tok)
                    if mc >= 2:
                        ph.wait("tensor", gel_tok[mc - 2])
                    for c in range(HC):
                        mm_tok[mc] = ph.op("tensor", lambda e, mc=mc, c=c:
                                           e.matmul(
                                               pg[:, mc % 2, :],
                                               hw[:, (mc // 2) % 2, mc % 2,
                                                  c * 128:(c + 1) * 128],
                                               srcb[:, c, :], start=(c == 0),
                                               stop=(c == HC - 1)),
                                           sem=sT2 if c == HC - 1 else None)
                    ph.wait("vector", mm_tok[mc])
                    if mc >= 2:
                        ph.wait("vector", gel_tok[mc - 2])
                    elif sevac_free is not None:
                        ph.wait("vector", sevac_free)
                    sv = ph.op("vector", lambda e, mc=mc: e.tensor_scalar(
                        sevac[:, mc % 2, :], pg[:, mc % 2, :],
                        ssb[:, s_idx, mc:mc + 1], None, ALU.mult))
                    ph.wait("scalar", sv)
                    gel_tok[mc] = ph.op("scalar", lambda e, mc=mc: e.activation(
                        dst[:, mc, :], sevac[:, mc % 2, :], AF.Gelu))
                return gel_tok[HC - 1], mm_tok[HC - 1]

            pg_free = f4_add.get(ch - 1)
            g1_tok, g1_mm = gelu_layer(hT, gT, g["h1tb"], mul_tok[HC - 1],
                                       pg_free, prev_g2_mm, 2,
                                       sil_hist[HC - 1])
            g2_tok, g2_mm = gelu_layer(gT, hT, g["h2tb"], g1_tok, None,
                                       g1_mm, 3, g1_tok)
            prev_sevac_s = g2_tok
            prev_g2_mm = g2_mm

            if ch >= 1:
                ph.wait("scalar", f4_add[ch - 1])
            f4toks = []
            for t in range(4):
                gt = ch * 4 + t
                f4toks.append(ph.dma("scalar", sF4, f4x[:, t, :],
                                     xin[gt * 128:(gt + 1) * 128, :]))
            mm_tok = {}
            for kc in range(HC):
                pair = kc // 2
                if kc % 2 == 0:
                    if pair >= 2:
                        ph.wait("sync", mm_tok[kc - 3])
                    wtok = stW.dma(ph, "sync", pair, w3sb[:, pair % 2],
                                   g["w3tb"][kc:kc + 2].rearrange(
                                       "k p d -> p k d"))
                    ph.wait("tensor", wtok)
                if kc == 0:
                    ph.wait("tensor", g2_tok)
                    if ch < 7:
                        ph.wait("tensor", ln_evacs[ch + 1][-1])
                    if ch >= 1:
                        ph.wait("tensor", f4_add[ch - 1])
                for t in range(4):
                    for hf in range(2):
                        last = (t == 3 and hf == 1)
                        mm_tok[kc] = ph.op("tensor",
                                           lambda e, kc=kc, t=t, hf=hf: e.matmul(
                                               _po(pw1, pw2, pg, bufs["pstr"], t)[
                                                   :, hf * 512:(hf + 1) * 512],
                                               hT[:, kc, t * 128:(t + 1) * 128],
                                               w3sb[:, (kc // 2) % 2, kc % 2,
                                                    hf * 512:(hf + 1) * 512],
                                               start=(kc == 0),
                                               stop=(kc == HC - 1)),
                                           sem=sT2 if last else None)
            f4_mm[ch] = mm_tok[HC - 1]
            for t in range(4):
                gt = ch * 4 + t
                ph.wait("vector", mm_tok[HC - 1])
                ph.wait("vector", f4toks[3])
                if t >= 2:
                    ph.wait("vector", sto[(ch, t - 2)])
                elif ch >= 1:
                    ph.wait("vector", sto[(ch - 1, t + 2)])
                if t == 0 and ch == 0:
                    ph.wait("vector", sw3_tok)
                ph.op("vector", lambda e, t=t: e.tensor_mul(
                    outsb[:, t % 2, :], _po(pw1, pw2, pg, bufs["pstr"], t),
                    sw3rep))
                add_tok = ph.op("vector", lambda e, t=t: e.tensor_add(
                    outsb[:, t % 2, :], outsb[:, t % 2, :], f4x[:, t, :]))
                ph.wait("scalar", add_tok)
                sto[(ch, t)] = stO.dma(ph, "scalar", t,
                                       xout[gt * 128:(gt + 1) * 128, :],
                                       outsb[:, t % 2, :])
                f4_add[ch] = add_tok
        stO.wait_all(ph, "scalar")
        ph.run()


def _po(pw1, pw2, pg, pstr, t):
    return [pw1, pw2, pg, pstr][t].rearrange("p a b -> p (a b)")


def phase_attn_a(g, xin, xout):
    nc = g["nc"]
    from contextlib import ExitStack
    with ExitStack() as outer:
        def osb(name, shape, dtype):
            return outer.enter_context(nc.sbuf_tensor(name, shape, dtype)).ap()

        QF = osb("a_QF", [128, S], F32)
        KBf = osb("a_KB", [128, S], F32)
        mcol = osb("a_mcol", [128, NT], F32)
        rcol = osb("a_rcol", [128, NT], F32)
        qptok = osb("a_qptok", [128, 2 * NT], F32)
        thr_rep = osb("a_thr", [128, 1], F32)
        WfqXr = osb("a_wfqxr", [128, D], F32)
        WfqYr = osb("a_wfqyr", [128, D], F32)
        brep = osb("a_brep", [128, D], F32)

        _attn_a_ph1(g, xin, QF, KBf, mcol, rcol, qptok, thr_rep,
                    WfqXr, WfqYr, brep)
        _attn_a_ph3(g, xin, xout, QF, KBf, mcol, rcol, qptok, thr_rep,
                    WfqXr, WfqYr, brep)


def _attn_a_ph1(g, xin, QF, KBf, mcol, rcol, qptok, thr_rep,
                WfqXr, WfqYr, brep):
    nc = g["nc"]
    sT, sT2, sG, sWs = g["sT"], g["sT2"], g["sG"], g["sWs"]
    stG, stH, stVd = g["stG"], g["stH"], g["stVd"]
    from contextlib import ExitStack
    with ExitStack() as stack:
        def sb(name, shape, dtype):
            return stack.enter_context(nc.sbuf_tensor(name, shape, dtype)).ap()

        def psb(name, shape, dtype=F32):
            return stack.enter_context(nc.psum_tensor(name, shape, dtype)).ap()

        xt = sb("a_xt", [128, 4, D], F32)
        xlnf = sb("a_xlnf", [128, 4, D], F32)
        xlnTr = sb("a_xlnTr", [128, DC, 1024], BF16)
        bufs = dict(bn=sb("a_bn", [128, 12], F32), mv=sb("a_mv", [128, 2], F32),
                    v1=sb("a_v1", [128, 1], F32), rst=sb("a_rst", [128, 1], F32),
                    pstr=psb("a_pstr", [128, 2, 512]))
        WqTb = sb("a_WqTb", [128, DC, D], BF16)
        WkTb = sb("a_WkTb", [128, DC, D], BF16)
        WvTb = sb("a_WvTb", [128, DC, D], BF16)
        bqsb = sb("a_bq", [128, DC], F32)
        bksb = sb("a_bk", [128, DC], F32)
        bvrow = sb("a_bvrow", [1, D], F32)
        bvb = sb("a_bvb", [1, D], BF16)
        WtqXr = sb("a_wtqxr", [128, D], F32)
        WtqYr = sb("a_wtqyr", [128, D], F32)
        btqr = sb("a_btqr", [128, 2], F32)
        tmp1k = sb("a_tmp1k", [128, D], F32)
        fall = sb("a_fall", [128, NT, 16], F32)
        fstage = sb("a_fstage", [16, S], F32)
        wrow = fstage[0:1, 0:D]
        red = sb("a_red", [128, 16], F32)
        sc = sb("a_sc", [1, 32], F32)
        kstage = sb("a_kst", [128, 1, DC, 512], BF16)
        qstage = sb("a_qst", [128, 1, DC, 512], BF16)
        vstage = sb("a_vst", [128, 1, 4, D], BF16)
        pfa = psb("a_pfa", [128, 512])
        psS = psb("a_psS", [128, 512])
        pp = psb("a_pp", [128, 2, 512])

        ph = Ph(g)
        for W, Wb in ((g["WqT"], WqTb), (g["WkT"], WkTb), (g["WvT"], WvTb)):
            ph.dma("gpsimd", sG, Wb, W.rearrange("(c p) o -> p c o", p=128))
        wb_tok = sG.tok
        t0 = ph.dma("sync", sWs, bqsb, g["bqT"]); ph.wait("sync", t0)
        t0 = ph.dma("sync", sWs, bksb, g["bkT"]); ph.wait("sync", t0)
        bv_tok = ph.dma("sync", sWs, bvrow, g["bv"]); ph.wait("sync", bv_tok)
        ph.wait("vector", bv_tok)
        bvb_tok = ph.op("vector", lambda e: e.tensor_copy(bvb, bvrow))
        last_rep = None
        for name in ("x", "y"):
            row = g["WtqX"] if name == "x" else g["WtqY"]
            t0 = ph.dma("sync", sWs, wrow, row)
            ph.wait("sync", t0)
            dstr = WtqXr if name == "x" else WtqYr
            mmt = None
            for hf in range(2):
                ph.wait("tensor", t0)
                if last_rep is not None:
                    ph.wait("tensor", last_rep)
                mmt = ph.op("tensor", lambda e, hf=hf: e.matmul(
                    pp[:, hf, :], g["ones"][0:1, 0:128],
                    wrow[0:1, hf * 512:(hf + 1) * 512], start=True, stop=True),
                    sem=sT)
                ph.wait("vector", mmt)
                last_rep = ph.op(
                    "vector", lambda e, hf=hf, dstr=dstr: e.tensor_copy(
                        dstr[:, hf * 512:(hf + 1) * 512], pp[:, hf, :]))
            ph.wait("sync", mmt)
        t0 = ph.dma("sync", sWs, wrow[0:1, 0:2], g["btq"])
        ph.wait("tensor", t0)
        ph.wait("tensor", last_rep)
        mmt = ph.op("tensor", lambda e: e.matmul(
            pp[:, 0, 0:2], g["ones"][0:1, 0:128], wrow[0:1, 0:2],
            start=True, stop=True), sem=sT)
        ph.wait("vector", mmt)
        btq_tok = ph.op("vector", lambda e: e.tensor_copy(btqr, pp[:, 0, 0:2]))
        ph.op("vector", lambda e: e.memset(fall[:, :, 9:10], 1.0))
        last_rep = btq_tok
        ph.wait("sync", mmt)
        for name in ("fx", "fy", "bo", "bfq"):
            row = {"fx": g["WfqX"], "fy": g["WfqY"], "bo": g["bo"],
                   "bfq": g["bfq"]}[name]
            t0 = ph.dma("sync", sWs, wrow, row)
            ph.wait("sync", t0)
            mmt = None
            for hf in range(2):
                ph.wait("tensor", t0)
                ph.wait("tensor", last_rep)
                mmt = ph.op("tensor", lambda e, hf=hf: e.matmul(
                    pp[:, hf, :], g["ones"][0:1, 0:128],
                    wrow[0:1, hf * 512:(hf + 1) * 512], start=True,
                    stop=True), sem=sT)
                ph.wait("vector", mmt)
                if name in ("fx", "fy"):
                    dstr = WfqXr if name == "fx" else WfqYr
                    last_rep = ph.op(
                        "vector", lambda e, hf=hf, dstr=dstr: e.tensor_copy(
                            dstr[:, hf * 512:(hf + 1) * 512], pp[:, hf, :]))
                elif name == "bo":
                    last_rep = ph.op(
                        "vector", lambda e, hf=hf: e.tensor_copy(
                            brep[:, hf * 512:(hf + 1) * 512], pp[:, hf, :]))
                else:
                    ph.op("vector", lambda e, hf=hf: e.tensor_scalar(
                        tmp1k[:, hf * 512:(hf + 1) * 512], pp[:, hf, :],
                        0.1, None, ALU.mult))
                    last_rep = ph.op(
                        "vector", lambda e, hf=hf: e.tensor_add(
                            brep[:, hf * 512:(hf + 1) * 512],
                            brep[:, hf * 512:(hf + 1) * 512],
                            tmp1k[:, hf * 512:(hf + 1) * 512]))
            ph.wait("sync", mmt)
        rep_done = last_rep
        wrow_last_mm = mmt

        feat_toks = {}

        def extra_v(j, gt):
            ft = fall[:, gt, :]
            ph.op("vector", lambda e, gt=gt: e.tensor_mul(
                tmp1k, xlnf[:, gt % 4, :], WtqXr))
            ph.op("vector", lambda e, ft=ft: e.reduce_sum(ft[:, 10:11], tmp1k,
                                                          axis=AX.X))
            ph.op("vector", lambda e, ft=ft: e.tensor_scalar(
                ft[:, 0:1], ft[:, 10:11], btqr[:, 0:1], None, ALU.add))
            ph.op("vector", lambda e, gt=gt: e.tensor_mul(
                tmp1k, xlnf[:, gt % 4, :], WtqYr))
            ph.op("vector", lambda e, ft=ft: e.reduce_sum(ft[:, 10:11], tmp1k,
                                                          axis=AX.X))
            ph.op("vector", lambda e, ft=ft: e.tensor_scalar(
                ft[:, 1:2], ft[:, 10:11], btqr[:, 1:2], None, ALU.add))
            ph.op("vector", lambda e, ft=ft: e.tensor_mul(
                ft[:, 10:11], ft[:, 0:1], ft[:, 0:1]))
            ph.op("vector", lambda e, ft=ft: e.tensor_mul(
                ft[:, 11:12], ft[:, 1:2], ft[:, 1:2]))
            ph.op("vector", lambda e, ft=ft: e.tensor_add(
                ft[:, 2:3], ft[:, 10:11], ft[:, 11:12]))
            ph.op("vector", lambda e, ft=ft: e.tensor_scalar(
                ft[:, 3:4], ft[:, 0:1], -2.0, None, ALU.mult))
            ph.op("vector", lambda e, ft=ft: e.tensor_scalar(
                ft[:, 4:5], ft[:, 1:2], -2.0, None, ALU.mult))
            ph.op("vector", lambda e, ft=ft: e.scalar_tensor_tensor(
                ft[:, 5:6], ft[:, 0:1], -T_WIN, ft[:, 1:2],
                ALU.mult, ALU.subtract))
            ph.op("vector", lambda e, ft=ft: e.scalar_tensor_tensor(
                ft[:, 6:7], ft[:, 0:1], -T_WIN, ft[:, 1:2],
                ALU.mult, ALU.add))
            ph.op("vector", lambda e, ft=ft: e.tensor_scalar(
                ft[:, 7:8], ft[:, 5:6], -1.0, None, ALU.mult))
            fv = ph.op("vector", lambda e, ft=ft: e.tensor_scalar(
                ft[:, 8:9], ft[:, 6:7], -1.0, None, ALU.mult))
            ph.op("vector", lambda e, gt=gt, ft=ft: e.tensor_copy(
                qptok[:, 2 * gt:2 * gt + 2], ft[:, 0:2]))
            ph.op("vector", lambda e, ft=ft: e.tensor_mul(
                ft[:, 12:13], ft[:, 0:1], ft[:, 1:2]))
            ph.op("vector", lambda e, ft=ft: e.tensor_mul(
                ft[:, 13:14], ft[:, 2:3], ft[:, 0:1]))
            ph.op("vector", lambda e, ft=ft: e.tensor_mul(
                ft[:, 14:15], ft[:, 2:3], ft[:, 1:2]))
            fv = ph.op("vector", lambda e, ft=ft: e.tensor_mul(
                ft[:, 15:16], ft[:, 2:3], ft[:, 2:3]))
            feat_toks[gt] = fv

        ev_all = {}
        nproj = 24
        proj_tok = {}
        kst_tok = {}
        qst_tok = {}
        vst_tok = {}
        ln_state = None

        def pre_evac_wait(ph_, j):
            pass

        for b in range(8):
            sl = b % 2
            tiles = list(range(4 * b, 4 * b + 4))
            dest_cols = [sl * 512 + t * 128 for t in range(4)]
            def pev(ph_, j, b=b):
                if j == 0 and b >= 2:
                    ph_.wait("scalar", proj_tok[b - 2])
            ln_state, ln_evacs = emit_ln_tiles(
                g, ph, xin, tiles, xt, xlnf, xlnTr, bufs, extra_v=extra_v,
                keep_stats=(mcol, rcol), dest_cols=dest_cols,
                pre_evac_wait=pev, state=ln_state)

            ph.wait("tensor", ln_evacs[3])
            if b == 0:
                ph.wait("tensor", wb_tok)
                ph.wait("tensor", bvb_tok)
                ph.wait("tensor", rep_done)
            pidx = 0
            for which, Wb, stg, bias in (("k", WkTb, kstage, bksb),
                                         ("q", WqTb, qstage, bqsb)):
                for mc in range(DC):
                    bank = pidx % 2
                    if pidx >= 2:
                        ph.wait("tensor", ev_all[(b, pidx - 2)])
                    elif b >= 1:
                        ph.wait("tensor", ev_all[(b - 1, nproj - 2 + pidx)])
                    tk = None
                    for c in range(DC):
                        tk = ph.op("tensor", lambda e, sl=sl, mc=mc, c=c,
                                   bank=bank, Wb=Wb: e.matmul(
                                       pp[:, bank, :],
                                       Wb[:, c, mc * 128:(mc + 1) * 128],
                                       xlnTr[:, c, sl * 512:(sl + 1) * 512],
                                       start=(c == 0), stop=(c == DC - 1)),
                                   sem=sT2 if c == DC - 1 else None)
                    ph.wait("scalar", tk)
                    if b >= 1 and mc == 0:
                        ph.wait("scalar", (kst_tok if which == "k"
                                           else qst_tok)[b - 1])
                    ev_all[(b, pidx)] = ph.op(
                        "scalar", lambda e, mc=mc, bank=bank, stg=stg,
                        bias=bias: e.activation(
                            stg[:, 0, mc, :], pp[:, bank, :], AF.Identity,
                            bias=bias[:, mc:mc + 1]))
                    pidx += 1
            for t in range(4):
                for hf in range(2):
                    bank = pidx % 2
                    if pidx >= 2:
                        ph.wait("tensor", ev_all[(b, pidx - 2)])
                    for c in range(DC):
                        ph.op("tensor", lambda e, sl=sl, t=t, c=c, hf=hf,
                              bank=bank: e.matmul(
                                  pp[:, bank, :],
                                  xlnTr[:, c, sl * 512 + t * 128:
                                        sl * 512 + (t + 1) * 128],
                                  WvTb[:, c, hf * 512:(hf + 1) * 512],
                                  start=(c == 0), stop=False))
                    tk = ph.op("tensor", lambda e, bank=bank, hf=hf: e.matmul(
                        pp[:, bank, :], g["onesb"][0:1, 0:128],
                        bvb[0:1, hf * 512:(hf + 1) * 512],
                        start=False, stop=True), sem=sT2)
                    ph.wait("scalar", tk)
                    if b >= 1 and t == 0 and hf == 0:
                        ph.wait("scalar", vst_tok[b - 1])
                    ev_all[(b, pidx)] = ph.op(
                        "scalar", lambda e, t=t, hf=hf, bank=bank:
                        e.activation(vstage[:, 0, t, hf * 512:(hf + 1) * 512],
                                     pp[:, bank, :], AF.Copy))
                    pidx += 1
            proj_tok[b] = ev_all[(b, pidx - 1)]
            for stg, st, toks, dst in (
                    (kstage, stG, kst_tok,
                     g["kTd"].rearrange("c p t -> p c t")[
                         :, :, b * 512:(b + 1) * 512]),
                    (qstage, stH, qst_tok,
                     g["qTd"].rearrange("c p t -> p c t")[
                         :, :, b * 512:(b + 1) * 512]),
                    (vstage, stVd, vst_tok,
                     g["vd"][4 * b:4 * b + 4].rearrange("t p d -> p t d"))):
                ph.wait("gpsimd", proj_tok[b])
                toks[b] = st.dma(ph, "gpsimd", b, dst, stg[:, 0])
            p0_drain(g, ph, 3)

        ph.wait("tensor", feat_toks[NT - 1])
        ph.wait("vector", wrow_last_mm)
        fev = None
        for gt in range(NT):
            if gt >= 1:
                ph.wait("tensor", fev)
            trt = ph.op("tensor", lambda e, gt=gt: e.transpose(
                pfa[0:16, (gt % 4) * 128:(gt % 4 + 1) * 128],
                fall[:, gt, :], g["identS"]), sem=sT)
            ph.wait("vector", trt)
            fev = ph.op("vector", lambda e, gt=gt: e.tensor_copy(
                fstage[0:10, gt * 128:(gt + 1) * 128],
                pfa[0:10, (gt % 4) * 128:(gt % 4 + 1) * 128]))

        NN = float(S)
        N2 = NN * NN
        rd_tok = ph.op("vector", lambda e: e.reduce_sum(
            red, fall.rearrange("p t f -> p f t"), axis=AX.X))
        ph.wait("tensor", rd_tok)
        smt = ph.op("tensor", lambda e: e.matmul(
            psS[0:1, 0:16], g["oneb1"], red, start=True, stop=True), sem=sT)
        ph.wait("vector", smt)
        ph.op("vector", lambda e: e.tensor_copy(sc[0:1, 0:16], psS[0:1, 0:16]))
        V = lambda fn: ph.op("vector", fn)
        V(lambda e: e.tensor_mul(sc[:, 16:17], sc[:, 0:1], sc[:, 0:1]))
        V(lambda e: e.tensor_mul(sc[:, 17:18], sc[:, 1:2], sc[:, 1:2]))
        V(lambda e: e.tensor_add(sc[:, 16:17], sc[:, 16:17], sc[:, 17:18]))
        V(lambda e: e.tensor_scalar(sc[:, 17:18], sc[:, 16:17], 2.0, None,
                                    ALU.mult))
        V(lambda e: e.scalar_tensor_tensor(sc[:, 18:19], sc[:, 2:3], 2.0 * NN,
                                           sc[:, 17:18], ALU.mult,
                                           ALU.subtract))
        V(lambda e: e.tensor_mul(sc[:, 19:20], sc[:, 10:11], sc[:, 10:11]))
        V(lambda e: e.tensor_mul(sc[:, 20:21], sc[:, 12:13], sc[:, 12:13]))
        V(lambda e: e.tensor_mul(sc[:, 21:22], sc[:, 11:12], sc[:, 11:12]))
        V(lambda e: e.scalar_tensor_tensor(sc[:, 20:21], sc[:, 20:21], 2.0,
                                           sc[:, 19:20], ALU.mult, ALU.add))
        V(lambda e: e.tensor_add(sc[:, 20:21], sc[:, 20:21], sc[:, 21:22]))
        V(lambda e: e.tensor_mul(sc[:, 22:23], sc[:, 13:14], sc[:, 0:1]))
        V(lambda e: e.tensor_mul(sc[:, 23:24], sc[:, 14:15], sc[:, 1:2]))
        V(lambda e: e.tensor_add(sc[:, 22:23], sc[:, 22:23], sc[:, 23:24]))
        V(lambda e: e.tensor_mul(sc[:, 24:25], sc[:, 2:3], sc[:, 2:3]))
        V(lambda e: e.tensor_scalar(sc[:, 24:25], sc[:, 24:25], 2.0, None,
                                    ALU.mult))
        V(lambda e: e.scalar_tensor_tensor(sc[:, 25:26], sc[:, 15:16],
                                           2.0 * NN, sc[:, 24:25], ALU.mult,
                                           ALU.add))
        V(lambda e: e.tensor_scalar(sc[:, 22:23], sc[:, 22:23], 8.0, None,
                                    ALU.mult))
        V(lambda e: e.tensor_sub(sc[:, 25:26], sc[:, 25:26], sc[:, 22:23]))
        V(lambda e: e.tensor_scalar(sc[:, 20:21], sc[:, 20:21], 4.0, None,
                                    ALU.mult))
        V(lambda e: e.tensor_add(sc[:, 25:26], sc[:, 25:26], sc[:, 20:21]))
        V(lambda e: e.tensor_scalar(sc[:, 26:27], sc[:, 18:19], 1.0 / N2,
                                    None, ALU.mult))
        V(lambda e: e.tensor_mul(sc[:, 27:28], sc[:, 18:19], sc[:, 18:19]))
        V(lambda e: e.tensor_scalar(sc[:, 27:28], sc[:, 27:28], 1.0 / N2,
                                    None, ALU.mult))
        V(lambda e: e.tensor_sub(sc[:, 28:29], sc[:, 25:26], sc[:, 27:28]))
        vv = V(lambda e: e.tensor_scalar(sc[:, 28:29], sc[:, 28:29],
                                         1.0 / (N2 - 1.0), None, ALU.mult))
        ph.wait("scalar", vv)
        ph.op("scalar", lambda e: e.activation(sc[:, 29:30], sc[:, 28:29],
                                               AF.Ln))
        sqt = ph.op("scalar", lambda e: e.activation(sc[:, 29:30],
                                                     sc[:, 29:30], AF.Exp,
                                                     scale=0.5))
        ph.wait("vector", sqt)
        tht = V(lambda e: e.scalar_tensor_tensor(
            sc[:, 30:31], sc[:, 29:30], 1.25, sc[:, 26:27], ALU.mult,
            ALU.add))
        ph.wait("tensor", tht)
        ph.wait("tensor", smt)
        rmt = ph.op("tensor", lambda e: e.matmul(
            psS[:, 32:33], g["ones"][0:1, 0:128], sc[0:1, 30:31],
            start=True, stop=True), sem=sT)
        ph.wait("vector", rmt)
        ph.op("vector", lambda e: e.tensor_copy(thr_rep, psS[:, 32:33]))

        ph.wait("sync", fev)
        asg = [(QF, 0, 9), (QF, 1, 3), (QF, 2, 4), (QF, 3, 2),
               (QF, 32, 9), (QF, 33, 5), (QF, 64, 9), (QF, 65, 6),
               (KBf, 0, 2), (KBf, 1, 0), (KBf, 2, 1), (KBf, 3, 9),
               (KBf, 32, 7), (KBf, 33, 9), (KBf, 64, 8), (KBf, 65, 9)]
        for dstt, drow, srow_i in asg:
            t0 = ph.dma("sync", sWs, dstt[drow:drow + 1, :],
                        fstage[srow_i:srow_i + 1, :])
            ph.wait("sync", t0)
        stG.wait_all(ph, "gpsimd")
        stH.wait_all(ph, "gpsimd")
        stVd.wait_all(ph, "gpsimd")
        ph.run()


def _attn_a_ph2(g, QF, KBf, thr_rep):
    nc = g["nc"]
    sT, sT2 = g["sT"], g["sT2"]
    NPAIR = float(S) * float(S)
    from contextlib import ExitStack
    with ExitStack() as stack:
        def sb(name, shape, dtype):
            return stack.enter_context(nc.sbuf_tensor(name, shape, dtype)).ap()

        sumP = sb("s_sum", [128, 256], F32)
        sqP = sb("s_sq", [128, 256], F32)
        sqsc = sb("s_sqsc", [128, 512], F32)
        tot = sb("s_tot", [128, 4], F32)
        sc = sb("s_sc", [1, 8], F32)
        pp = stack.enter_context(nc.psum_tensor("s_pp", [128, 4, 512], F32)).ap()
        pt = stack.enter_context(nc.psum_tensor("s_pt", [128, 512], F32)).ap()

        ph = Ph(g)
        vred = {}
        ssq = {}
        for qt in range(NT):
            p0_drain(g, ph, 1)
            for jb in range(8):
                idx = qt * 8 + jb
                bank = idx % 4
                if idx >= 4:
                    ph.wait("tensor", ssq[idx - 4])
                mmt = ph.op("tensor", lambda e, qt=qt, jb=jb, bank=bank:
                            e.matmul(pp[:, bank, :], QF[0:4,
                                     qt * 128:(qt + 1) * 128],
                                     KBf[0:4, jb * 512:(jb + 1) * 512],
                                     start=True, stop=True), sem=sT)
                ph.wait("vector", mmt)
                vred[idx] = ph.op("vector", lambda e, idx=idx, bank=bank:
                                  e.reduce_sum(sumP[:, idx:idx + 1],
                                               pp[:, bank, :], axis=AX.X))
                ph.wait("scalar", vred[idx])
                ssq[idx] = ph.op("scalar", lambda e, idx=idx, bank=bank:
                                 e.activation(sqsc, pp[:, bank, :], AF.Square,
                                              accum_out=sqP[:, idx:idx + 1]))
        ph.wait("vector", ssq[255])
        v1 = ph.op("vector", lambda e: e.reduce_sum(tot[:, 0:1], sumP,
                                                    axis=AX.X))
        v2 = ph.op("vector", lambda e: e.reduce_sum(tot[:, 1:2], sqP,
                                                    axis=AX.X))
        ph.wait("tensor", v2)
        mt = ph.op("tensor", lambda e: e.matmul(
            pt[0:1, 0:2], g["oneb1"], tot[:, 0:2], start=True, stop=True),
            sem=sT)
        ph.wait("vector", mt)
        ph.op("vector", lambda e: e.tensor_copy(sc[0:1, 4:6], pt[0:1, 0:2]))
        ph.op("vector", lambda e: e.tensor_scalar(
            sc[0:1, 0:1], sc[0:1, 4:5], 1.0 / NPAIR, None, ALU.mult))
        ph.op("vector", lambda e: e.tensor_mul(sc[0:1, 1:2], sc[0:1, 4:5],
                                               sc[0:1, 4:5]))
        ph.op("vector", lambda e: e.tensor_scalar(
            sc[0:1, 1:2], sc[0:1, 1:2], 1.0 / NPAIR, None, ALU.mult))
        ph.op("vector", lambda e: e.tensor_sub(sc[0:1, 1:2], sc[0:1, 5:6],
                                               sc[0:1, 1:2]))
        vvar = ph.op("vector", lambda e: e.tensor_scalar(
            sc[0:1, 1:2], sc[0:1, 1:2], 1.0 / (NPAIR - 1.0), None, ALU.mult))
        ph.wait("scalar", vvar)
        ph.op("scalar", lambda e: e.activation(sc[0:1, 2:3], sc[0:1, 1:2],
                                               AF.Ln))
        sq_tok = ph.op("scalar", lambda e: e.activation(
            sc[0:1, 2:3], sc[0:1, 2:3], AF.Exp, scale=0.5))
        ph.wait("vector", sq_tok)
        thr_tok = ph.op("vector", lambda e: e.scalar_tensor_tensor(
            sc[0:1, 3:4], sc[0:1, 2:3], 1.25, sc[0:1, 0:1], ALU.mult, ALU.add))
        ph.wait("tensor", thr_tok)
        mt2 = ph.op("tensor", lambda e: e.matmul(
            pt[:, 4:5], g["ones"][0:1, 0:128], sc[0:1, 3:4],
            start=True, stop=True), sem=sT)
        ph.wait("vector", mt2)
        fin = ph.op("vector", lambda e: e.tensor_copy(thr_rep, pt[:, 4:5]))
        ph.run()


def _attn_a_ph3(g, xin, xout, QF, KBf, mcol, rcol, qptok, thr_rep,
                WfqXr, WfqYr, brep):
    nc = g["nc"]
    sT, sT2, sG, sWs, sF4 = g["sT"], g["sT2"], g["sG"], g["sWs"], g["sF4"]
    sPV = g["sPV"]
    stW, stW2, stQ, stX, stO = (g["stW"], g["stW2"], g["stQ"], g["stX"],
                                g["stO"])
    stVg = g["stVg"]
    from contextlib import ExitStack
    with ExitStack() as stack:
        def sb(name, shape, dtype):
            return stack.enter_context(nc.sbuf_tensor(name, shape, dtype)).ap()

        def psb(name, shape, dtype=F32):
            return stack.enter_context(nc.psum_tensor(name, shape, dtype)).ap()

        scob = sb("t_sco", [128, 2, S], F32)
        expb = sb("t_exp", [128, S], BF16)
        wT = sb("t_wT", [128, S], BF16)
        kblk = sb("t_kblk", [128, 3, DC, 512], BF16)
        vblk = sb("t_vblk", [128, 4, 4, D], BF16)
        qtl = sb("t_qtl", [128, 2, DC, 128], BF16)
        xres = sb("t_xres", [128, 2, D], F32)
        WoTb = sb("t_WoTb", [128, DC, D], BF16)
        ctxsb = sb("t_ctxsb", [128, D], BF16)
        ctxT = sb("t_ctxT", [128, DC, 128], BF16)
        outa = sb("t_outa", [128, D], F32)
        outsb = sb("t_outsb", [128, 2, D], F32)
        qbuf = sb("t_qbuf", [128, D], F32)
        tmpa = sb("t_tmpa", [128, 512], F32)
        tmpb = sb("t_tmpb", [128, 512], F32)
        nm = sb("t_nm", [128, 1], F32)
        lsum = sb("t_lsum", [128, 1], F32)
        linv = sb("t_linv", [128, 1], F32)
        smal = sb("t_smal", [128, 8], F32)
        ps_s = [psb("t_psA", [128, 512]), psb("t_psB", [128, 512])]
        psD = psb("t_pd", [128, 512])
        psU = psb("t_pu", [128, 2, 512])
        trp = psb("t_trp", [128, 1024], BF16)
        ctxp = psb("t_ctx", [128, 2, 512])

        ph = Ph(g)
        ph.dma("gpsimd", sG, WoTb, g["WoT"].rearrange("(c p) o -> p c o",
                                                      p=128))
        wo_tok = sG.tok

        pv_sched = []
        for ii in range(NT):
            ng = (ii + 4) // 4
            for vb in range(ng):
                pv_sched.append(min(4, ii + 1 - 4 * vb))
        pv_prefix = [0]
        for njt in pv_sched:
            pv_prefix.append(pv_prefix[-1] + njt)
        pv_base = sPV.n
        gv_counter = [0]
        vld = {}
        prev_store = {}
        prev_out_mm = None
        prev_mask_tok = None
        prev_ctx_evac = None
        prev_tr_evac = {}
        prev_pv = {}
        prev_outa_tok = None
        ex_hist = {}
        prev_ot = {}
        blk_mm = {}
        slot_last = {}
        for i in range(NT):
            W = (i + 1) * 128
            nb = (W + 511) // 512
            if i >= 2 and prev_out_mm is not None:
                ph.wait("scalar", prev_out_mm)
            qtok = stQ.dma(ph, "scalar", i, qtl[:, i % 2],
                           g["qTd"].rearrange("c p t -> p c t")[
                               :, :, i * 128:(i + 1) * 128])
            if i >= 2:
                ph.wait("scalar", prev_ot[i - 2])
            xrtok = stX.dma(ph, "scalar", i, xres[:, i % 2, :],
                            xin[i * 128:(i + 1) * 128, :])
            p0_drain(g, ph, 2)
            mask_toks = []
            for jb in range(nb):
                wj = min(512, W - jb * 512)
                sl = jb % 2
                if slot_last.get(jb % 3) is not None:
                    ph.wait("sync", slot_last[jb % 3])
                ktok = stW.dma(ph, "sync", jb, kblk[:, jb % 3],
                               g["kTd"].rearrange("c p t -> p c t")[
                                   :, :, jb * 512:jb * 512 + 512])
                Gv = gv_counter[0]
                if Gv >= 4:
                    ph.wait("gpsimd", (sPV, pv_base + pv_prefix[Gv - 3]))
                vld[Gv] = stVg.dma(ph, "gpsimd", Gv, vblk[:, Gv % 4],
                                   g["vd"][4 * jb:4 * jb + 4].rearrange(
                                       "t p d -> p t d"))
                gv_counter[0] += 1
                ph.wait("tensor", ktok)
                ph.wait("tensor", qtok)
                if prev_mask_tok is not None and jb <= 1:
                    ph.wait("tensor", prev_mask_tok)
                if len(mask_toks) >= 2:
                    ph.wait("tensor", mask_toks[jb - 2])

                for c in range(DC):
                    ph.op("tensor", lambda e, sl=sl, c=c, i=i, jb=jb, wj=wj:
                          e.matmul(ps_s[sl][:, 0:wj], qtl[:, i % 2, c, :],
                                   kblk[:, jb % 3, c, 0:wj],
                                   start=(c == 0), stop=(c == DC - 1)))
                if mask_toks:
                    ph.wait("tensor", mask_toks[jb - 1])
                ph.op("tensor", lambda e, i=i, jb=jb, wj=wj: e.matmul(
                    psD[:, 0:wj], QF[0:4, i * 128:(i + 1) * 128],
                    KBf[0:4, jb * 512:jb * 512 + wj],
                    start=True, stop=True))
                ph.op("tensor", lambda e, i=i, jb=jb, wj=wj: e.matmul(
                    psU[:, 0, 0:wj], QF[32:34, i * 128:(i + 1) * 128],
                    KBf[32:34, jb * 512:jb * 512 + wj],
                    start=True, stop=True))
                mmt = ph.op("tensor", lambda e, i=i, jb=jb, wj=wj:
                            e.matmul(
                                psU[:, 1, 0:wj],
                                QF[64:66, i * 128:(i + 1) * 128],
                                KBf[64:66, jb * 512:jb * 512 + wj],
                                start=True, stop=True), sem=sT2)
                blk_mm[jb] = mmt
                slot_last[jb % 3] = mmt
                ph.wait("vector", mmt)
                if jb == 0 and (i - 2) in ex_hist:
                    ph.wait("vector", ex_hist[i - 2])
                ph.op("vector", lambda e, wj=wj: e.tensor_scalar(
                    tmpa[:, 0:wj], psD[:, 0:wj], thr_rep, NEG,
                    ALU.is_gt, ALU.mult))
                ph.op("vector", lambda e, wj=wj: e.tensor_scalar(
                    tmpb[:, 0:wj], psU[:, 1, 0:wj], 0.0, None,
                    ALU.is_gt))
                ph.op("vector", lambda e, wj=wj: e.scalar_tensor_tensor(
                    tmpb[:, 0:wj], psU[:, 0, 0:wj], 0.0, tmpb[:, 0:wj],
                    ALU.is_gt, ALU.mult))
                ph.op("vector", lambda e, sl=sl, jb=jb, wj=wj, i=i:
                      e.scalar_tensor_tensor(
                          scob[:, i % 2, jb * 512:jb * 512 + wj],
                          ps_s[sl][:, 0:wj],
                          INV_SQRT_D, tmpa[:, 0:wj], ALU.mult, ALU.add))
                mk = ph.op("vector", lambda e, sl=sl, jb=jb, wj=wj, i=i:
                           e.scalar_tensor_tensor(
                               scob[:, i % 2, jb * 512:jb * 512 + wj],
                               tmpb[:, 0:wj],
                               NEG, scob[:, i % 2, jb * 512:jb * 512 + wj],
                               ALU.mult, ALU.add))
                mask_toks.append(mk)
            dg = ph.op("vector", lambda e, i=i: e.tensor_add(
                scob[:, i % 2, i * 128:(i + 1) * 128],
                scob[:, i % 2, i * 128:(i + 1) * 128], g["triS"]))
            prev_mask_tok = dg
            nm_tok = ph.op("vector", lambda e, W=W, i=i: e.reduce_max(
                nm, scob[:, i % 2, 0:W], axis=AX.X, negate=True))
            ph.wait("scalar", nm_tok)
            ex_tok = ph.op("scalar", lambda e, W=W, i=i: e.activation(
                expb[:, 0:W], scob[:, i % 2, 0:W], AF.Exp, bias=nm,
                accum_out=lsum))
            ex_hist[i] = ex_tok
            ph.wait("vector", ex_tok)
            lv_tok = ph.op("vector", lambda e: e.reciprocal(linv, lsum))
            trt = {}
            trev = {}
            for jb0 in range(0, i + 1, 8):
                jend = min(jb0 + 8, i + 1)
                ph.wait("tensor", ex_tok)
                if jb0 >= 8:
                    ph.wait("tensor", trev[jb0 - 1])
                elif i >= 1 and prev_ctx_evac is not None:
                    ph.wait("tensor", prev_ctx_evac)
                for jt in range(jb0, jend):
                    trt[jt] = ph.op("tensor", lambda e, jt=jt, reg=jt % 8:
                                    e.transpose(
                                        trp[:, reg * 128:(reg + 1) * 128],
                                        expb[:, jt * 128:(jt + 1) * 128],
                                        g["identbS"]),
                                    sem=sT if jt == jend - 1 else None)
                ph.wait("scalar", trt[jend - 1])
                if i >= 1 and prev_pv and jb0 == 0:
                    ph.wait("scalar", prev_pv[max(prev_pv)])
                wev = ph.op("scalar", lambda e, jb0=jb0, jend=jend:
                            e.activation(
                                wT[:, jb0 * 128:jend * 128],
                                trp[:, 0:(jend - jb0) * 128],
                                AF.Copy))
                for jt in range(jb0, jend):
                    trev[jt] = wev
            pv = {}
            gv0 = gv_counter[0] - nb
            for jt in range(i + 1):
                vb = jt // 4
                gvb = gv0 + vb
                if jt % 4 == 0:
                    ph.wait("tensor", vld[gvb])
                ph.wait("tensor", trev[jt])
                if jt == 0 and prev_outa_tok is not None:
                    ph.wait("tensor", prev_outa_tok)
                for hf in range(2):
                    pv[(jt, hf)] = ph.op(
                        "tensor", lambda e, jt=jt, hf=hf, gvb=gvb, i=i: e.matmul(
                            ctxp[:, hf, :], wT[:, jt * 128:(jt + 1) * 128],
                            vblk[:, gvb % 4, jt % 4, hf * 512:(hf + 1) * 512],
                            start=(jt == 0), stop=(jt == i)),
                        sem=sPV if hf == 1 else None)
            assert sPV.n == pv_base + pv_prefix[gv_counter[0]], (
                sPV.n, pv_base, pv_prefix[gv_counter[0]])
            prev_pv = pv
            ph.wait("scalar", pv[(i, 1)])
            ce = None
            for hf in range(2):
                ce = ph.op("scalar", lambda e, hf=hf: e.activation(
                    ctxsb[:, hf * 512:(hf + 1) * 512], ctxp[:, hf, :],
                    AF.Copy))
            ph.wait("tensor", ce)
            ph.wait("tensor", trev[i])
            ctr = None
            for c in range(DC):
                ctr = ph.op("tensor", lambda e, c=c: e.transpose(
                    trp[:, c * 128:(c + 1) * 128],
                    ctxsb[:, c * 128:(c + 1) * 128], g["identbS"]),
                    sem=sT if c == DC - 1 else None)
            ph.wait("scalar", ctr)
            ce2 = None
            for h in range(2):
                ce2 = ph.op("scalar", lambda e, h=h: e.activation(
                    ctxT[:, 4 * h:4 * h + 4, :],
                    trp[:, h * 512:(h + 1) * 512].rearrange(
                        "p (c m) -> p c m", m=128),
                    AF.Copy))
            prev_ctx_evac = ce2
            for c in range(DC):
                if c == 0:
                    ph.wait("tensor", ce2)
                    if i == 0:
                        ph.wait("tensor", wo_tok)
                for hf in range(2):
                    prev_out_mm = ph.op(
                        "tensor", lambda e, c=c, hf=hf: e.matmul(
                            ctxp[:, hf, :],
                            ctxT[:, c, :],
                            WoTb[:, c, hf * 512:(hf + 1) * 512],
                            start=(c == 0), stop=(c == DC - 1)),
                        sem=sT2 if (c == DC - 1 and hf == 1) else None)
            ph.wait("vector", prev_out_mm)
            ph.wait("vector", lv_tok)
            for hf in range(2):
                prev_outa_tok = ph.op("vector", lambda e, hf=hf: e.tensor_scalar(
                    outa[:, hf * 512:(hf + 1) * 512],
                    ctxp[:, hf, :],
                    linv, None, ALU.mult))
            ph.op("vector", lambda e, i=i: e.tensor_mul(
                smal[:, 1:2], qptok[:, 2 * i:2 * i + 1],
                qptok[:, 2 * i:2 * i + 1]))
            ph.op("vector", lambda e, i=i: e.tensor_mul(
                smal[:, 2:3], qptok[:, 2 * i + 1:2 * i + 2],
                qptok[:, 2 * i + 1:2 * i + 2]))
            n2t = ph.op("vector", lambda e: e.tensor_add(
                smal[:, 3:4], smal[:, 1:2], smal[:, 2:3]))
            ph.wait("scalar", n2t)
            ph.op("scalar", lambda e: e.activation(smal[:, 4:5], smal[:, 3:4],
                                                   AF.Ln))
            rn_tok = ph.op("scalar", lambda e: e.activation(
                smal[:, 4:5], smal[:, 4:5], AF.Exp, scale=0.5))
            ph.wait("vector", rn_tok)
            ph.op("vector", lambda e: e.tensor_scalar(
                smal[:, 4:5], smal[:, 4:5], 1.0e-12, None, ALU.max))
            ph.op("vector", lambda e: e.reciprocal(smal[:, 5:6], smal[:, 4:5]))
            ph.op("vector", lambda e, i=i: e.tensor_scalar(
                smal[:, 6:7], qptok[:, 2 * i:2 * i + 1], smal[:, 5:6], 0.1,
                ALU.mult, ALU.mult))
            ph.op("vector", lambda e, i=i: e.tensor_scalar(
                smal[:, 7:8], qptok[:, 2 * i + 1:2 * i + 2], smal[:, 5:6], 0.1,
                ALU.mult, ALU.mult))
            ph.op("vector", lambda e: e.tensor_scalar(
                qbuf, WfqXr, smal[:, 6:7], None, ALU.mult))
            ph.op("vector", lambda e: e.scalar_tensor_tensor(
                qbuf, WfqYr, smal[:, 7:8], qbuf, ALU.mult, ALU.add))
            ph.wait("vector", xrtok)
            ph.op("vector", lambda e: e.tensor_add(outa, outa, brep))
            ph.op("vector", lambda e: e.tensor_add(outa, outa, qbuf))
            ph.op("vector", lambda e, i=i: e.tensor_scalar(
                qbuf, xres[:, i % 2, :], mcol[:, i:i + 1], rcol[:, i:i + 1],
                ALU.subtract, ALU.mult))
            ph.op("vector", lambda e: e.tensor_add(outa, outa, qbuf))
            if i >= 2:
                ph.wait("vector", prev_store[i - 2])
            ot = ph.op("vector", lambda e, i=i: e.tensor_add(
                outsb[:, i % 2, :], outa, xres[:, i % 2, :]))
            prev_ot[i] = ot
            ph.wait("scalar", ot)
            prev_store[i] = stO.dma(ph, "scalar", i,
                                    xout[i * 128:(i + 1) * 128, :],
                                    outsb[:, i % 2, :])
        stO.wait_all(ph, "scalar")
        ph.run()


def phase_attn_b(g, xin, xout):
    nc = g["nc"]
    sT, sT2, sG, sF4 = g["sT"], g["sT2"], g["sG"], g["sF4"]
    stW, stW2, stW3, stH, stO = (g["stW"], g["stW2"], g["stW3"], g["stH"],
                                 g["stO"])
    from contextlib import ExitStack

    with ExitStack() as stack:
        def sb(name, shape, dtype):
            return stack.enter_context(nc.sbuf_tensor(name, shape, dtype)).ap()

        def psb(name, shape, dtype=F32):
            return stack.enter_context(nc.psum_tensor(name, shape, dtype)).ap()

        HW = 2048
        xt = sb("b_xt", [128, 4, D], F32)
        xlnf = sb("b_xlnf", [128, 4, D], F32)
        xlnT = sb("b_xlnT", [128, DC, S], BF16)
        bufs = dict(bn=sb("b_bn", [128, 12], F32), mv=sb("b_mv", [128, 2], F32),
                    v1=sb("b_v1", [128, 1], F32), rst=sb("b_rst", [128, 1], F32),
                    pstr=psb("b_pstr", [128, 2, 512]))
        Hb = sb("b_Hb", [128, 2, HW], F32)
        Ftb = sb("b_Ftb", [128, 2, HW], F32)
        Itb = sb("b_Itb", [128, 2, HW], F32)
        Sgb = sb("b_Sgb", [128, 2, HW], F32)
        Hsc = sb("b_Hsc", [128, 2, HW], F32)
        hbb = sb("b_hbb", [128, 2, HW], BF16)
        w1g = sb("b_w1g", [128, 2, 3, DC * 128], BF16)
        b1sb = sb("b_b1", [128, 3 * GC], F32)
        b1ng = sb("b_b1ng", [128, 2 * GC], F32)
        inith = sb("b_inith", [128, GC], F32)
        ginit = sb("b_ginit", [128, GC], F32)
        gtmp = sb("b_gtmp", [128, GC], F32)
        pp = psb("b_pp", [128, 6, 512])

        ph = Ph(g)
        p0_wait(g, ph, "sync", [4])
        b1_tok = ph.dma("sync", g["sWs"], b1sb, g["b1T"])
        ph.wait("sync", b1_tok)
        ih_tok = ph.dma("sync", g["sWs"], inith, g["inithT"])
        ph.wait("vector", b1_tok)
        b1n_tok = ph.op("vector", lambda e: e.tensor_scalar(
            b1ng, b1sb[:, GC:3 * GC], -1.0, None, ALU.mult))
        ph.wait("scalar", ih_tok)
        tk = ph.op("scalar", lambda e: e.activation(gtmp, inith, AF.Sigmoid))
        ph.wait("vector", tk)
        ph.op("vector", lambda e: e.tensor_sub(ginit, inith, gtmp))
        ph.op("vector", lambda e: e.tensor_scalar(ginit, ginit, 0.5, 0.0,
                                                  ALU.add, ALU.max))
        gi_tok = ph.op("vector", lambda e: e.tensor_add(ginit, ginit, gtmp))

        ln_state = [None]
        ln_evacs = []

        def emit_ln_group(grp):
            st, ev = emit_ln_tiles(g, ph, xin, list(range(4 * grp, 4 * grp + 4)),
                                   xt, xlnf, xlnT, bufs,
                                   dest_cols=[(4 * grp + t) * 128
                                              for t in range(4)],
                                   state=ln_state[0])
            ln_state[0] = st
            ln_evacs.extend(ev)

        mm_tok = {}
        ev_tok = {}
        evI_tok = {}
        sig_tok = {}
        g_tok = {}
        v_tok = {}
        scan_tok = {}
        cast_tok = {}
        st_tok = {}
        tails = []

        def emit_tail(U):
            cc, hh = U // 2, U % 2
            p = hh
            ph.wait("vector", g_tok[U])
            v_tok[U] = ph.op("vector", lambda e, p=p: e.tensor_mul(
                Ftb[:, p, :], Ftb[:, p, :], Hb[:, p, :]))
            if U >= 2:
                ph.wait("vector", cast_tok[U - 2])
            if U == 0:
                ph.wait("vector", gi_tok)
            init = (ginit[:, cc:cc + 1] if hh == 0
                    else Hsc[:, 0, HW - 1:HW])
            scan_tok[U] = ph.op("vector", lambda e, p=p, init=init:
                                e.tensor_tensor_scan(
                                    Hsc[:, p, :], Itb[:, p, :], Ftb[:, p, :],
                                    init, ALU.mult, ALU.add))

        def emit_cast(U):
            p = U % 2
            ph.wait("scalar", scan_tok[U])
            if U >= 2:
                ph.wait("scalar", st_tok[U - 2])
            cast_tok[U] = ph.op("scalar", lambda e, p=p: e.activation(
                hbb[:, p, :], Hsc[:, p, :], AF.Copy))

        def emit_store(U):
            cc, hh = U // 2, U % 2
            ph.wait("gpsimd", cast_tok[U])
            st_tok[U] = stH.dma(ph, "gpsimd", U,
                                g["hd"][cc][:, hh * HW:(hh + 1) * HW],
                                hbb[:, hh, :])

        for cc in range(GC):
            wtoks = []
            for gi, st in ((0, stW), (1, stW2), (2, stW3)):
                if cc >= 2:
                    ph.wait("sync", mm_tok[(cc - 2, 7, 2)])
                wtoks.append(st.dma(ph, "sync", cc, w1g[:, cc % 2, gi, :],
                                    g["W1b"][gi * GC + cc]))
            for nb in range(8):
                hh = nb // 4
                U = cc * 2 + hh
                p = hh
                if cc == 0:
                    emit_ln_group(nb)
                if nb % 4 == 0:
                    if U >= 1:
                        emit_tail(U - 1)
                    if U >= 2:
                        ph.wait("scalar", v_tok[U - 2])
                        ph.wait("scalar", scan_tok[U - 2])
                        ph.wait("scalar", g_tok[U - 2])
                for gi in range(3):
                    bank = gi * 2 + (nb % 2)
                    if nb == 0:
                        ph.wait("tensor", wtoks[gi])
                    if cc == 0 and gi == 0:
                        ph.wait("tensor", ln_evacs[4 * nb + 3])
                    if nb >= 2:
                        ph.wait("tensor", ev_tok[(cc, nb - 2, gi)])
                    elif cc >= 1:
                        ph.wait("tensor", ev_tok[(cc - 1, nb + 6, gi)])
                    for c in range(DC):
                        mm_tok[(cc, nb, gi)] = ph.op(
                            "tensor", lambda e, cc=cc, gi=gi, nb=nb, c=c,
                            bank=bank: e.matmul(
                                pp[:, bank, :],
                                w1g[:, cc % 2, gi, c * 128:(c + 1) * 128],
                                xlnT[:, c, nb * 512:(nb + 1) * 512],
                                start=(c == 0), stop=(c == DC - 1)),
                            sem=sT2 if c == DC - 1 else None)
                    col = (nb % 4) * 512
                    gcol = gi * GC + cc
                    ph.wait("scalar", mm_tok[(cc, nb, gi)])
                    if cc == 0 and nb == 0 and gi == 0:
                        ph.wait("scalar", b1_tok)
                        ph.wait("scalar", b1n_tok)
                    if gi == 0:
                        ev_tok[(cc, nb, gi)] = ph.op(
                            "scalar", lambda e, p=p, col=col, gcol=gcol,
                            bank=bank: e.activation(
                                Hb[:, p, col:col + 512], pp[:, bank, :],
                                AF.Identity, bias=b1sb[:, gcol:gcol + 1]))
                    else:
                        dst = Ftb if gi == 1 else Itb
                        ncol = (gi - 1) * GC + cc
                        ev_tok[(cc, nb, gi)] = ph.op(
                            "scalar", lambda e, p=p, col=col, ncol=ncol,
                            bank=bank, dst=dst: e.activation(
                                dst[:, p, col:col + 512], pp[:, bank, :],
                                AF.Exp, scale=-1.0,
                                bias=b1ng[:, ncol:ncol + 1]))
                if nb % 4 == 3:
                    evI_tok[U] = ev_tok[(cc, nb, 2)]
                    ph.op("scalar", lambda e, p=p: e.activation(
                        Ftb[:, p, :], Ftb[:, p, :], AF.Ln, bias=g["oneb1"]))
                    lnI_tok = ph.op("scalar", lambda e, p=p: e.activation(
                        Itb[:, p, :], Itb[:, p, :], AF.Ln, bias=g["oneb1"]))
                    ph.wait("vector", lnI_tok)
                    d2_tok = ph.op("vector", lambda e, p=p: e.tensor_sub(
                        Itb[:, p, :], Itb[:, p, :], Ftb[:, p, :]))
                    ph.wait("scalar", d2_tok)
                    c_tok = ph.op("scalar", lambda e, p=p: e.activation(
                        Itb[:, p, :], Itb[:, p, :], AF.Sigmoid))
                    sig_tok[U] = ph.op("scalar", lambda e, p=p: e.activation(
                        Sgb[:, p, :], Hb[:, p, :], AF.Sigmoid))
                    if U >= 1:
                        emit_cast(U - 1)
                    ph.wait("vector", c_tok)
                    ph.op("vector", lambda e, p=p: e.tensor_scalar(
                        Ftb[:, p, :], Itb[:, p, :], -1.0, 1.0,
                        ALU.mult, ALU.add))
                    ph.wait("vector", sig_tok[U])
                    ph.op("vector", lambda e, p=p: e.tensor_sub(
                        Hb[:, p, :], Hb[:, p, :], Sgb[:, p, :]))
                    ph.op("vector", lambda e, p=p: e.tensor_scalar(
                        Hb[:, p, :], Hb[:, p, :], 0.5, 0.0,
                        ALU.add, ALU.max))
                    g_tok[U] = ph.op("vector", lambda e, p=p: e.tensor_add(
                        Hb[:, p, :], Hb[:, p, :], Sgb[:, p, :]))
                    if U >= 1:
                        emit_store(U - 1)
        emit_tail(31)
        emit_cast(31)
        emit_store(31)
        stH.wait_all(ph, "gpsimd")
        ph.run()

    with ExitStack() as stack:
        def sb(name, shape, dtype):
            return stack.enter_context(nc.sbuf_tensor(name, shape, dtype)).ap()

        def psb(name, shape, dtype=F32):
            return stack.enter_context(nc.psum_tensor(name, shape, dtype)).ap()

        W2b = sb("b_W2b", [128, GC, D], BF16)
        hsb = sb("b_hsb", [128, 2, 512], BF16)
        b2row = sb("b_b2row", [1, D], F32)
        b2rep = sb("b_b2rep", [128, D], F32)
        xres = sb("b_xres", [128, 4, D], F32)
        outsb = sb("b_outsb", [128, 2, D], F32)
        po = psb("b_po", [128, 8, 512])

        ph = Ph(g)
        w2_toks = []
        for q in range(4):
            w2_toks.append(g["stVg"].dma(
                ph, "gpsimd", q, W2b[:, 4 * q:4 * q + 4, :],
                g["W2"][512 * q:512 * (q + 1), :].rearrange(
                    "(c p) o -> p c o", p=128)))
        b2_tok = ph.dma("sync", g["sWs"], b2row, g["b2"])
        ph.wait("tensor", b2_tok)
        rep_toks = []
        for hf in range(2):
            rep_toks.append(ph.op("tensor", lambda e, hf=hf: e.matmul(
                po[:, hf, :], g["ones"][0:1, 0:128],
                b2row[0:1, hf * 512:(hf + 1) * 512], start=True, stop=True),
                sem=sT))
        ph.wait("vector", rep_toks[-1])
        brep_tok = None
        for hf in range(2):
            brep_tok = ph.op("vector", lambda e, hf=hf: e.tensor_copy(
                b2rep[:, hf * 512:(hf + 1) * 512], po[:, hf, :]))

        mm_tok = {}
        add_tok = {}
        sto = {}
        for tg in range(8):
            f4toks = []
            for t in range(4):
                gt = tg * 4 + t
                if tg > 0:
                    ph.wait("scalar", add_tok[(tg - 1, t)])
                f4toks.append(ph.dma("scalar", sF4, xres[:, t, :],
                                     xin[gt * 128:(gt + 1) * 128, :]))
            for cc in range(GC):
                kidx = tg * GC + cc
                if kidx >= 2:
                    ph.wait("sync", mm_tok[kidx - 2])
                htok = stW.dma(ph, "sync", kidx, hsb[:, kidx % 2, :],
                               g["hd"][cc][:, tg * 512:(tg + 1) * 512])
                ph.wait("tensor", htok)
                if cc % 4 == 0 and tg == 0:
                    ph.wait("tensor", w2_toks[cc // 4])
                if cc == 0:
                    if tg == 0:
                        ph.wait("tensor", brep_tok)
                    else:
                        ph.wait("tensor", add_tok[(tg - 1, 3)])
                for t in range(4):
                    for hf in range(2):
                        last = (t == 3 and hf == 1)
                        mm_tok[kidx] = ph.op(
                            "tensor", lambda e, cc=cc, t=t, hf=hf, kidx=kidx:
                            e.matmul(
                                po[:, 2 * t + hf, :],
                                hsb[:, kidx % 2, t * 128:(t + 1) * 128],
                                W2b[:, cc, hf * 512:(hf + 1) * 512],
                                start=(cc == 0), stop=(cc == GC - 1)),
                            sem=sT2 if last else None)
            for t in range(4):
                gt = tg * 4 + t
                ph.wait("vector", mm_tok[tg * GC + GC - 1])
                ph.wait("vector", f4toks[3])
                if t >= 2:
                    ph.wait("vector", sto[(tg, t - 2)])
                elif tg > 0:
                    ph.wait("vector", sto[(tg - 1, t + 2)])
                ph.op("vector", lambda e, t=t: e.tensor_add(
                    outsb[:, t % 2, :],
                    po.rearrange("p a b -> p (a b)")[:, t * 1024:(t + 1) * 1024],
                    b2rep))
                atok = ph.op("vector", lambda e, t=t: e.tensor_add(
                    outsb[:, t % 2, :], outsb[:, t % 2, :], xres[:, t, :]))
                add_tok[(tg, t)] = atok
                ph.wait("scalar", atok)
                sto[(tg, t)] = stO.dma(ph, "scalar", t,
                                       xout[gt * 128:(gt + 1) * 128, :],
                                       outsb[:, t % 2, :])
        stO.wait_all(ph, "scalar")
        ph.run()



def _prep_shared(inputs):
    f = np.ascontiguousarray
    ii = {}
    ii["WqT"] = f(inputs["L0_attn_a_Wq"].T)
    ii["WkT"] = f(inputs["L0_attn_a_Wk"].T)
    ii["WvT"] = f(inputs["L0_attn_a_Wv"].T)
    ii["WoT"] = f(inputs["L0_attn_a_Wo"].T)
    ii["bqT"] = f(inputs["L0_attn_a_bq"].reshape(DC, 128).T)
    ii["bkT"] = f(inputs["L0_attn_a_bk"].reshape(DC, 128).T)
    ii["bv"] = f(inputs["L0_attn_a_bv"].reshape(1, D))
    ii["bo"] = f(inputs["L0_attn_a_bo"].reshape(1, D))
    ii["bfq"] = f(inputs["L0_attn_a_bfq"].reshape(1, D))
    ii["WtqX"] = f(inputs["L0_attn_a_Wtq"][0:1, :])
    ii["WtqY"] = f(inputs["L0_attn_a_Wtq"][1:2, :])
    ii["btq"] = f(inputs["L0_attn_a_btq"].reshape(1, 2))
    ii["WfqX"] = f(inputs["L0_attn_a_Wfq"][:, 0].reshape(1, D))
    ii["WfqY"] = f(inputs["L0_attn_a_Wfq"][:, 1].reshape(1, D))
    ii["W1"] = f(inputs["L0_attn_b_W1"])
    ii["b1T"] = f(inputs["L0_attn_b_b1"].reshape(3 * GC, 128).T)
    ii["W2"] = f(inputs["L0_attn_b_W2"])
    ii["b2"] = f(inputs["L0_attn_b_b2"].reshape(1, D))
    ii["inithT"] = f(inputs["L0_attn_b_init_h"].reshape(GC, 128).T)
    ii["w1T"] = f(inputs["L0_ffn_w1_w"].T)
    ii["w2T"] = f(inputs["L0_ffn_w2_w"].T)
    ii["h1T"] = f(inputs["L0_ffn_h1_w"].T)
    ii["h2T"] = f(inputs["L0_ffn_h2_w"].T)
    ii["w3T"] = f(inputs["L0_ffn_w3_w"].T)
    ii["s1T"] = f(inputs["L0_ffn_w1_s"].reshape(HC, 128).T)
    ii["s2T"] = f(inputs["L0_ffn_w2_s"].reshape(HC, 128).T)
    ii["s3T"] = f(inputs["L0_ffn_h1_s"].reshape(HC, 128).T)
    ii["s4T"] = f(inputs["L0_ffn_h2_s"].reshape(HC, 128).T)
    ii["sw3r"] = f(inputs["L0_ffn_w3_s"].reshape(1, D))
    ii["ident"] = np.eye(128, dtype=np.float32)
    ii["identb"] = np.eye(128, dtype=ml_dtypes.bfloat16)
    jj, kk = np.meshgrid(np.arange(128), np.arange(128), indexing="ij")
    ii["tri"] = np.where(kk > jj, np.float32(NEG),
                         np.float32(0.0)).astype(np.float32)
    return ii


def run_cores(inputs, stages=ALL_STAGES, taps=(), cores=8, trace=False,
              dumps=()):
    xfull = np.ascontiguousarray(
        inputs["L0_attn_a_hidden_states"], dtype=np.float32)
    shared = _prep_shared(inputs)
    nc = build_program(stages=stages, taps=taps, dumps=dumps)
    in_maps = []
    for b in range(cores):
        m = dict(shared)
        m["x0"] = np.ascontiguousarray(xfull[b])
        in_maps.append(m)
    res = run_bass_kernel_spmd(nc, in_maps, core_ids=list(range(cores)),
                               trace=trace)
    return res


def kernel(**inputs):
    res = run_cores(inputs)
    out = np.stack([np.asarray(res.results[b]["y"]) for b in range(8)], axis=0)
    return out.astype(np.float32)
